# version 03
# baseline (speedup 1.0000x reference)
import numpy as np
import ml_dtypes
from contextlib import ExitStack
import concourse.bass as bass
import concourse.mybir as mybir
from concourse.bass_utils import run_bass_kernel_spmd

F32 = mybir.dt.float32
BF16 = mybir.dt.bfloat16
AF = mybir.ActivationFunctionType
ALU = mybir.AluOpType
AX = mybir.AxisListType

DBG = {}

NCORES = 8
BPC = 2
T = 2048
D = 1024
NTOK = BPC * T
NT = NTOK // 128
LN_EPS = 1e-5


class EngProxy:

    def __init__(self, ctx, eng, key):
        self.ctx = ctx
        self.eng = eng
        self.key = key

    def __getattr__(self, name):
        fn = getattr(self.eng, name)

        def wrapped(*a, **kw):
            ctx = self.ctx
            ctx.finalize_pending(self.key)
            instr = fn(*a, **kw)
            last = ctx.last.get(self.key)
            if last is not None:
                instr._wait_ge(ctx.sems[last[0]][0], last[1])
            ctx.pending[self.key] = instr
            return instr
        return wrapped


class Ctx:

    ENG_KEYS = ("dve", "act", "pool")

    def __init__(self, nc, es, parent=None):
        self.nc = nc
        self.es = es
        self.handles = []
        self.names = []
        if parent is None:
            self.sems = {}
            self.last = {}
            self.pending = {}
            for k in self.ENG_KEYS:
                self.sem("q_" + k)
        else:
            self.sems = parent.sems
            self.last = parent.last
            self.pending = parent.pending
        self.V = EngProxy(self, nc.vector, "dve")
        self.S = EngProxy(self, nc.scalar, "act")
        self.G = EngProxy(self, nc.gpsimd, "pool")

    def sem(self, name):
        assert name not in self.sems, name
        h = self.nc.alloc_semaphore(name=name)
        self.handles.append(h)
        self.names.append(name)
        self.sems[name] = [h, 0]
        return h

    def teardown_sems(self):
        if not self.handles:
            return
        self.nc.clear_and_free_semaphores(self.handles)
        self.nc.all_engine_barrier()
        mine = set(self.names)
        for k in list(self.last.keys()):
            v = self.last.get(k)
            if v is not None and v[0] in mine:
                self.last[k] = None
        for nm_ in self.names:
            del self.sems[nm_]
        self.handles = []
        self.names = []

    def finalize_pending(self, key):
        instr = self.pending.get(key)
        if instr is not None:
            q = "q_" + key
            s = self.sems[q]
            instr.then_inc(s[0], 1)
            s[1] += 1
            self.last[key] = (q, s[1])
            self.pending[key] = None

    def finalize_all(self):
        for k in self.ENG_KEYS:
            self.finalize_pending(k)

    def inc(self, instr, name, k=1):
        s = self.sems[name]
        instr.then_inc(s[0], k)
        s[1] += k
        for key, pend in self.pending.items():
            if pend is instr:
                self.last[key] = (name, s[1])
                self.pending[key] = None
                break
        return instr

    def val(self, name):
        return self.sems[name][1]

    def wait(self, eng, name, value=None):
        if isinstance(eng, EngProxy):
            eng = eng.eng
        s = self.sems[name]
        v = s[1] if value is None else value
        if v > 0:
            eng.wait_ge(s[0], v)

    def tap(self, name, ap, wait_sems=()):
        nc = self.nc
        shape = list(ap.shape)
        out = nc.dram_tensor("dbg_" + name, shape, ap.dtype, kind="ExternalOutput").ap()
        for s in wait_sems:
            self.wait(nc.gpsimd, s)
        if "dbg_st" not in self.sems:
            self.sem("dbg_st")
        self.wait(nc.gpsimd, "dbg_st")
        self.inc(self.nc.gpsimd.dma_start(out=out, in_=ap), "dbg_st", 16)

    def sb(self, name, shape, dtype):
        return self.es.enter_context(self.nc.sbuf_tensor(name, shape, dtype))

    def ps(self, name, shape, dtype=F32):
        return self.es.enter_context(self.nc.psum_tensor(name, shape, dtype))


def barrier(ctx, name):
    nc = ctx.nc
    ctx.finalize_all()
    if "g_bar_s" not in ctx.sems:
        ctx.sem("g_bar_s")
        ctx.sem("g_bar_pe")
    ctx.inc(nc.tensor.nop(), "g_bar_pe", 1)
    nc.tensor.wait_ge(ctx.sems["g_bar_pe"][0], ctx.val("g_bar_pe"))
    for key, eng in (("dve", nc.vector), ("act", nc.scalar), ("pool", nc.gpsimd)):
        last = ctx.last.get(key)
        if last is not None:
            eng.wait_ge(ctx.sems[last[0]][0], last[1])
    engs = (nc.sync, nc.vector, nc.scalar, nc.tensor, nc.gpsimd)
    h = ctx.sems["g_bar_s"][0]
    for eng in engs:
        eng.sem_inc(h, 1)
    ctx.sems["g_bar_s"][1] += 5
    for eng in engs:
        eng.wait_ge(h, ctx.val("g_bar_s"))


def make_consts():
    c = {}
    c["ident_b"] = np.eye(128, dtype=np.float32).astype(ml_dtypes.bfloat16)
    c["ones_row_b"] = np.ones((1, 128), dtype=np.float32).astype(ml_dtypes.bfloat16)
    c["ones_row_f"] = np.ones((1, 128), dtype=np.float32)
    c["ident_f"] = np.eye(128, dtype=np.float32)
    c["ones_col_b"] = np.ones((128, 1), dtype=np.float32).astype(ml_dtypes.bfloat16)
    i64 = np.arange(64)
    ut64 = (i64[:, None] <= i64[None, :]).astype(np.float32)
    lt64s = (i64[:, None] > i64[None, :]).astype(np.float32)
    bd = lambda m: np.block([[m, np.zeros((64, 64), np.float32)],
                             [np.zeros((64, 64), np.float32), m]])
    c["bd_ut_f"] = bd(ut64)
    c["bd_lts_f"] = bd(lt64s)
    c["bd_on_f"] = bd(np.ones((64, 64), np.float32))
    c["ut64i_b"] = ut64.astype(ml_dtypes.bfloat16)
    c["lt64i_b"] = (i64[:, None] >= i64[None, :]).astype(np.float32).astype(ml_dtypes.bfloat16)
    c["lt64s_b"] = lt64s.astype(ml_dtypes.bfloat16)
    c["negI64_b"] = (-1e9 * np.eye(64, dtype=np.float32)).astype(ml_dtypes.bfloat16)
    i128 = np.arange(128)
    c["uts128_b"] = (i128[:, None] < i128[None, :]).astype(np.float32).astype(ml_dtypes.bfloat16)
    c["lts128_b"] = (i128[:, None] > i128[None, :]).astype(np.float32).astype(ml_dtypes.bfloat16)
    c["negI128_b"] = (-1e9 * np.eye(128, dtype=np.float32)).astype(ml_dtypes.bfloat16)
    c["neg1e9row_b"] = (-1e9 * np.ones((1, 512), dtype=np.float32)).astype(ml_dtypes.bfloat16)
    c["eye64x2_b"] = np.concatenate([np.eye(64), np.eye(64)], 0).astype(np.float32).astype(ml_dtypes.bfloat16)
    c["eye64_b"] = np.eye(64, dtype=np.float32).astype(ml_dtypes.bfloat16)
    return c


FLOAT_CONSTS = [1e-5, 1e-6, 1.0, 1.1920929e-07, -2.4260151319598084]


CONST_NP = make_consts()
CONST_NP["fvals"] = np.tile(np.asarray(FLOAT_CONSTS, np.float32), (128, 1))


def declare_and_load_consts(ctx):
    nc = ctx.nc
    tiles = {}
    ctx.sem("c_ld")
    for k, arr in CONST_NP.items():
        dt = BF16 if k.endswith("_b") else F32
        ap = nc.dram_tensor("c_" + k, list(arr.shape), dt, kind="ExternalInput").ap()
        tile = ctx.sb("k_" + k, list(arr.shape), dt)
        ctx.inc(nc.sync.dma_start(out=tile[:], in_=ap), "c_ld", 16)
        tiles[k] = tile
    for j, v in enumerate(FLOAT_CONSTS):
        nc.const_aps.register(F32, v, tiles["fvals"][:, j:j + 1])
    return tiles


def const_inmap():
    return {"c_" + k: np.asarray(v) for k, v in CONST_NP.items()}


def emit_ln_stats(ctx, sh, x_view, mean_t, rstd_t, tag):
    nc = ctx.nc
    ctx.sem(tag + "ld0"); ctx.sem(tag + "ld1")
    ctx.sem(tag + "rdA")
    ctx.sem(tag + "rdD")
    ctx.sem(tag + "finD"); ctx.sem(tag + "finA")
    xa = sh["stA"], sh["stB"]
    nsum, ssq, scr = sh["nsum"], sh["ssq"], sh["scr"]
    for i in range(NT):
        p = i % 2
        if i >= 2:
            ctx.wait(nc.sync, tag + "rdA", i - 1)
            ctx.wait(nc.sync, tag + "rdD", i - 1)
        ctx.inc(nc.sync.dma_start(out=xa[p][:], in_=x_view[i * 128:(i + 1) * 128, :]),
                tag + "ld%d" % p, 16)
        ctx.wait(nc.scalar, tag + "ld%d" % p)
        dump = scr if p == 0 else sh["scr2"]
        ctx.inc(ctx.S.activation(dump[:], xa[p][:], AF.Square,
                                     accum_out=ssq[:, i:i + 1]), tag + "rdA", 1)
        ctx.wait(nc.vector, tag + "ld%d" % p)
        ctx.inc(ctx.V.reduce_sum(nsum[:, i:i + 1], xa[p][:], axis=AX.X, negate=True),
                tag + "rdD", 1)
    ctx.wait(nc.vector, tag + "rdD")
    ctx.wait(nc.vector, tag + "rdA")
    ctx.V.tensor_scalar(mean_t[:], nsum[:], -1.0 / D, None, ALU.mult)
    ctx.V.tensor_mul(scr[:, 0:NT], mean_t[:], mean_t[:])
    ctx.inc(ctx.V.scalar_tensor_tensor(scr[:, NT:2 * NT], ssq[:], 1.0 / D,
                                           scr[:, 0:NT], ALU.mult, ALU.subtract), tag + "finD", 1)
    ctx.wait(nc.scalar, tag + "finD")
    ctx.S.activation(scr[:, 0:NT], scr[:, NT:2 * NT], AF.Ln, bias=float(LN_EPS))
    ctx.inc(ctx.S.activation(rstd_t[:], scr[:, 0:NT], AF.Exp, scale=-0.5), tag + "finA", 1)


def alloc_shared(ctx):
    sh = {}
    sh["stA"] = ctx.sb("sh_stA", [128, D], F32)
    sh["stB"] = ctx.sb("sh_stB", [128, D], F32)
    sh["scr"] = ctx.sb("sh_scr", [128, D], F32)
    sh["scr2"] = ctx.sb("sh_scr2", [128, D], F32)
    sh["nsum"] = ctx.sb("sh_nsum", [128, NT], F32)
    sh["ssq"] = ctx.sb("sh_ssq", [128, NT], F32)
    return sh



def emit_stats_finish(ctx, sh, mean_t, rstd_t, tag):
    nc = ctx.nc
    ctx.sem(tag + "vD"); ctx.sem(tag + "vA")
    ctx.wait(nc.vector, tag + "fsA")
    ctx.wait(nc.vector, tag + "fsD")
    ctx.V.tensor_scalar(mean_t[:], sh["nsum"][:], -1.0 / D, None, ALU.mult)
    ctx.V.tensor_mul(sh["scr"][:, 0:NT], mean_t[:], mean_t[:])
    ctx.inc(ctx.V.scalar_tensor_tensor(sh["scr"][:, NT:2 * NT], sh["ssq"][:], 1.0 / D,
                                       sh["scr"][:, 0:NT], ALU.mult, ALU.subtract), tag + "vD", 1)
    ctx.wait(nc.scalar, tag + "vD")
    ctx.S.activation(sh["scr"][:, 0:NT], sh["scr"][:, NT:2 * NT], AF.Ln, bias=float(LN_EPS))
    ctx.inc(ctx.S.activation(rstd_t[:], sh["scr"][:, 0:NT], AF.Exp, scale=-0.5), tag + "vA", 1)


def emit_residual_pass(ctx, sh, x_in, f_raw, x_out, mean_f, rstd_f, tag, pre_tag):
    nc = ctx.nc
    ctx.sem(tag + "lx0"); ctx.sem(tag + "lx1"); ctx.sem(tag + "lf0"); ctx.sem(tag + "lf1")
    ctx.sem(tag + "res"); ctx.sem(tag + "st0"); ctx.sem(tag + "st1")
    xc = [sh["stA"], sh["stB"]]
    fr = [ctx.sb(tag + "_fr0", [128, D], F32), ctx.sb(tag + "_fr1", [128, D], F32)]
    xnew = [ctx.sb(tag + "_xn0", [128, D], F32), ctx.sb(tag + "_xn1", [128, D], F32)]
    for i in range(NT):
        p = i % 2
        if i == 0 and pre_tag is not None:
            ctx.wait(nc.sync, pre_tag + "rdA")
            ctx.wait(nc.sync, pre_tag + "rdD")
        if i >= 2:
            ctx.wait(nc.sync, tag + "res", ctx.val(tag + "res") - 1)
        ctx.inc(nc.sync.dma_start(out=xc[p][:], in_=x_in[i * 128:(i + 1) * 128, :]),
                tag + "lx%d" % p, 16)
        if i >= 2:
            ctx.wait(nc.gpsimd, tag + "res", ctx.val(tag + "res") - 1)
        ctx.inc(nc.gpsimd.dma_start(out=fr[p][:], in_=f_raw[i * 128:(i + 1) * 128, :]),
                tag + "lf%d" % p, 16)
        ctx.wait(nc.vector, tag + "lx%d" % p)
        ctx.wait(nc.vector, tag + "lf%d" % p)
        if i >= 2:
            ctx.wait(nc.vector, tag + "st%d" % p)
        ctx.V.tensor_scalar(xnew[p][:], fr[p][:], mean_f[:, i:i + 1], rstd_f[:, i:i + 1],
                            ALU.subtract, ALU.mult)
        ctx.inc(ctx.V.tensor_add(xnew[p][:], xnew[p][:], xc[p][:]), tag + "res", 1)
        ctx.wait(nc.gpsimd, tag + "res")
        ctx.inc(nc.gpsimd.dma_start(out=x_out[i * 128:(i + 1) * 128, :], in_=xnew[p][:]),
                tag + "st%d" % p, 16)
    ctx.wait(nc.gpsimd, tag + "st0")
    ctx.wait(nc.gpsimd, tag + "st1")


def emit_l1_ffn(ctx, ktiles, sh, x_in, x_out, w):
    nc = ctx.nc
    es = ctx.es
    H = 4096

    mean_a = ctx.sb("f_meanA", [128, NT], F32)
    rstd_a = ctx.sb("f_rstdA", [128, NT], F32)
    mean_f = ctx.sb("f_meanF", [128, NT], F32)
    rstd_f = ctx.sb("f_rstdF", [128, NT], F32)
    f_raw = sh["f_raw"]

    emit_ln_stats(ctx, sh, x_in, mean_a, rstd_a, "fA")

    w1T = ctx.sb("f_w1T", [128, 8 * H], BF16)
    w2T = ctx.sb("f_w2T", [128, 32 * D], BF16)
    b1r = ctx.sb("f_b1r", [1, H], BF16)
    b2r = ctx.sb("f_b2r", [1, D], BF16)
    prep_es = ExitStack()
    wbuf = prep_es.enter_context(nc.sbuf_tensor("f_wb", [128, H], F32))
    sgn = prep_es.enter_context(nc.sbuf_tensor("f_sgn", [128, H], BF16))
    alpha = prep_es.enter_context(nc.sbuf_tensor("f_alpha", [128, 1], F32))
    wbin = prep_es.enter_context(nc.sbuf_tensor("f_wbin", [128, H], BF16))
    ptr = ctx.ps("f_ptr", [128, 1024], BF16)

    ctx.sem("fp_ld")
    ctx.sem("fp_sgn")
    ctx.sem("fp_bin")
    ctx.sem("fp_trs")
    ctx.sem("fp_ev")
    ctx.inc(nc.gpsimd.dma_start(out=b1r[:], in_=w["l1f_b1"].unsqueeze(0)), "fp_ld", 16)
    ctx.inc(nc.gpsimd.dma_start(out=b2r[:], in_=w["l1f_b2"].unsqueeze(0)), "fp_ld", 16)

    def prep_weight(wap, n_tiles, fd, dst, dst_d):
        for i in range(n_tiles):
            if i >= 1 or ctx.val("fp_bin") > 0:
                ctx.wait(nc.gpsimd, "fp_bin", ctx.val("fp_bin"))
            ctx.inc(nc.gpsimd.dma_start(out=wbuf[:, 0:fd],
                                        in_=wap[i * 128:(i + 1) * 128, :]), "fp_ld", 16)
            ctx.wait(nc.scalar, "fp_ld")
            if i >= 1 or ctx.val("fp_bin") > 0:
                ctx.wait(nc.scalar, "fp_bin", ctx.val("fp_bin"))
            ctx.inc(ctx.S.activation(sgn[:, 0:fd], wbuf[:, 0:fd], AF.Sign),
                    "fp_sgn", 1)
            ctx.wait(nc.vector, "fp_ld")
            ctx.V.reduce_sum(alpha[:], wbuf[:, 0:fd], axis=AX.X,
                                 apply_absolute_value=True)
            ctx.wait(nc.vector, "fp_sgn")
            if i >= 1 or ctx.val("fp_trs") > 0:
                ctx.wait(nc.vector, "fp_trs", ctx.val("fp_trs"))
            ctx.inc(ctx.V.tensor_scalar(wbin[:, 0:fd], sgn[:, 0:fd], alpha[:],
                                            1.0 / fd, ALU.mult, ALU.mult), "fp_bin", 1)
            nchunk = fd // 128
            ctx.wait(nc.tensor, "fp_bin")
            dview = dst[:, :].rearrange("p (cc x) -> p cc x", x=dst_d)
            for c in range(nchunk):
                if c % 8 == 0 and ctx.val("fp_ev") > 0:
                    ctx.wait(nc.tensor, "fp_ev")
                tin = nc.tensor.transpose(ptr[:, (c % 8) * 128:(c % 8) * 128 + 128],
                                          wbin[:, c * 128:c * 128 + 128], ktiles["ident_b"][:])
                if c % 8 == 7 or c == nchunk - 1:
                    ctx.inc(tin, "fp_trs", 1)
                    ctx.wait(nc.vector, "fp_trs")
                    lo = (c // 8) * 8
                    cnt = c - lo + 1
                    ctx.inc(ctx.V.tensor_copy(
                        dview[:, lo:lo + cnt, i * 128:(i + 1) * 128],
                        ptr[:, 0:cnt * 128].rearrange("p (cc x) -> p cc x", x=128)),
                        "fp_ev", 1)

    prep_weight(w["l1f_w1"], 32, D, w1T, H)
    prep_weight(w["l1f_w2"], 8, H, w2T, D)
    prep_es.close()

    passb_es = ExitStack()
    _s = lambda n, sh_, dt_: passb_es.enter_context(nc.sbuf_tensor(n, sh_, dt_))
    xb = [_s("f_xb0", [128, D], F32), _s("f_xb1", [128, D], F32)]
    xln = [_s("f_xln0", [128, D], BF16), _s("f_xln1", [128, D], BF16)]
    xfm = [_s("f_xfm0", [128, D], BF16), _s("f_xfm1", [128, D], BF16)]
    hbf = _s("f_hbf", [128, H], BF16)
    hfm = _s("f_hfm", [128, H], BF16)
    fout = [_s("f_fo0", [128, D], F32), _s("f_fo1", [128, D], F32)]
    ph = ctx.ps("f_ph", [128, 2048], F32)
    po = ctx.ps("f_po", [128, 1024], F32)

    ctx.sem("fb_ld0"); ctx.sem("fb_ld1")
    bln = ctx.sem("fb_ln"); btrx = ctx.sem("fb_trx")
    bfm = ctx.sem("fb_fm"); bh = ctx.sem("fb_h"); bg = ctx.sem("fb_g")
    btrh = ctx.sem("fb_trh"); bhfm = ctx.sem("fb_hfm"); bom = ctx.sem("fb_om")
    boe = ctx.sem("fb_oe"); ctx.sem("fb_fsA"); ctx.sem("fb_fsD")
    ctx.sem("fb_st0"); ctx.sem("fb_st1")

    for i in range(NT):
        p = i % 2
        if i == 0:
            ctx.wait(nc.sync, "fp_ev")
        if i >= 2:
            ctx.wait(nc.sync, "fb_ln", ctx.val("fb_ln") - 1)
        ctx.inc(nc.sync.dma_start(out=xb[p][:], in_=x_in[i * 128:(i + 1) * 128, :]),
                "fb_ld%d" % p, 16)
        ctx.wait(nc.vector, "fb_ld%d" % p)
        if i == 0:
            ctx.wait(nc.vector, "fAfinA", 1)
        if i >= 2:
            ctx.wait(nc.vector, "fb_trx", ctx.val("fb_trx") - 1)
        ctx.inc(ctx.V.tensor_scalar(xln[p][:], xb[p][:], mean_a[:, i:i + 1],
                                        rstd_a[:, i:i + 1], ALU.subtract, ALU.mult), "fb_ln", 1)
        ctx.wait(nc.tensor, "fb_ln")
        if i == 0:
            ctx.wait(nc.tensor, "fp_ev")
        ctx.wait(nc.tensor, "fb_hfm")
        for c in range(8):
            tin = nc.tensor.transpose(ptr[:, c * 128:c * 128 + 128],
                                      xln[p][:, c * 128:c * 128 + 128], ktiles["ident_b"][:])
        ctx.inc(tin, "fb_trx", 1)
        ctx.wait(nc.vector, "fb_trx")
        if i >= 2:
            ctx.wait(nc.vector, "fb_h", ctx.val("fb_h") - 1)
        ctx.inc(ctx.V.tensor_copy(xfm[p][:], ptr[:]), "fb_fm", 1)
        ctx.wait(nc.tensor, "fb_fm")
        if i == 0:
            ctx.wait(nc.tensor, "fp_ld")
        if i >= 2:
            ctx.wait(nc.tensor, "fb_trh", ctx.val("fb_trh") - 4)
        for half in range(2):
            if not (i == 0 and half == 0):
                ctx.wait(nc.tensor, "fb_g", ctx.val("fb_g"))
            for b in range(4):
                col0 = half * 2048 + b * 512
                for kc in range(8):
                    nc.tensor.matmul(ph[:, b * 512:b * 512 + 512], xfm[p][:, kc * 128:kc * 128 + 128],
                                     w1T[:, kc * H + col0:kc * H + col0 + 512],
                                     start=(kc == 0), stop=False)
                tin = nc.tensor.matmul(ph[:, b * 512:b * 512 + 512], ktiles["ones_row_b"][:],
                                       b1r[:, col0:col0 + 512], start=False, stop=True)
            ctx.inc(tin, "fb_h", 1)
            ctx.wait(nc.scalar, "fb_h")
            if i >= 1 and half == 0:
                ctx.wait(nc.scalar, "fb_trh", ctx.val("fb_trh"))
            ctx.inc(ctx.S.activation(hbf[:, half * 2048:half * 2048 + 2048], ph[:],
                                         AF.Gelu), "fb_g", 1)
        ctx.wait(nc.tensor, "fb_g")
        for grp in range(4):
            ctx.wait(nc.tensor, "fb_fm" if (i == 0 and grp == 0) else "fb_hfm")
            for c in range(8):
                cc = grp * 8 + c
                tin = nc.tensor.transpose(ptr[:, c * 128:c * 128 + 128],
                                          hbf[:, cc * 128:cc * 128 + 128], ktiles["ident_b"][:])
            ctx.inc(tin, "fb_trh", 1)
            ctx.wait(nc.vector, "fb_trh")
            if i >= 1:
                ctx.wait(nc.vector, "fb_om", ctx.val("fb_om"))
            ctx.inc(ctx.V.tensor_copy(hfm[:, grp * 1024:grp * 1024 + 1024], ptr[:]),
                    "fb_hfm", 1)
        ctx.wait(nc.tensor, "fb_hfm")
        if i >= 1:
            ctx.wait(nc.tensor, "fb_oe", ctx.val("fb_oe"))
        for b in range(2):
            for kc in range(32):
                nc.tensor.matmul(po[:, b * 512:b * 512 + 512], hfm[:, kc * 128:kc * 128 + 128],
                                 w2T[:, kc * D + b * 512:kc * D + b * 512 + 512],
                                 start=(kc == 0), stop=False)
            tin = nc.tensor.matmul(po[:, b * 512:b * 512 + 512], ktiles["ones_row_b"][:],
                                   b2r[:, b * 512:b * 512 + 512], start=False, stop=True)
        ctx.inc(tin, "fb_om", 1)
        ctx.wait(nc.vector, "fb_om")
        if i >= 2:
            ctx.wait(nc.vector, "fb_st%d" % p)
        ctx.inc(ctx.V.tensor_copy(fout[p][:], po[:]), "fb_oe", 1)
        ctx.wait(nc.scalar, "fb_oe")
        ctx.inc(ctx.S.activation((sh["scr"] if p == 0 else sh["scr2"])[:], fout[p][:],
                                     AF.Square, accum_out=sh["ssq"][:, i:i + 1]), "fb_fsA", 1)
        ctx.wait(nc.vector, "fb_oe")
        ctx.inc(ctx.V.reduce_sum(sh["nsum"][:, i:i + 1], fout[p][:], axis=AX.X,
                                     negate=True), "fb_fsD", 1)
        ctx.wait(nc.gpsimd, "fb_fsA")
        ctx.wait(nc.gpsimd, "fb_fsD")
        ctx.inc(nc.gpsimd.dma_start(out=f_raw[i * 128:(i + 1) * 128, :], in_=fout[p][:]),
                "fb_st%d" % p, 16)

    emit_stats_finish(ctx, sh, mean_f, rstd_f, "fb_")
    passb_es.close()
    ctx.wait(nc.gpsimd, "fb_st0")
    ctx.wait(nc.gpsimd, "fb_st1")
    barrier(ctx, "f_bar")
    emit_residual_pass(ctx, sh, x_in, f_raw, x_out, mean_f, rstd_f, "fc_", "fA")



def emit_l0_ffn(ctx, ktiles, sh, x_in, x_out, w):
    nc = ctx.nc
    E, L, R = 6, 100, 16

    mean_a = ctx.sb("g_meanA", [128, NT], F32)
    rstd_a = ctx.sb("g_rstdA", [128, NT], F32)
    mean_f = ctx.sb("g_meanF", [128, NT], F32)
    rstd_f = ctx.sb("g_rstdF", [128, NT], F32)
    f_raw = sh["f_raw"]

    emit_ln_stats(ctx, sh, x_in, mean_a, rstd_a, "gA")

    tokfm = ctx.sb("g_tokfm", [128, 8 * 600], BF16)
    wtT = ctx.sb("g_wtT", [128, 8 * D], BF16)
    wdT = ctx.sb("g_wdT", [128, 8 * D], BF16)
    wtb = ctx.sb("g_wtb", [1, D], BF16)
    wdb = ctx.sb("g_wdb", [1, D], BF16)
    wg = ctx.sb("g_wg", [128, 8 * 6], BF16)
    val0 = ctx.sb("g_val0", [128, E * D], BF16)
    scale128 = ctx.sb("g_s128", [128, 1], F32)
    at = ctx.sb("g_at", [16, 600], BF16)
    bt = [ctx.sb("g_bt%d" % e, [16, D], BF16) for e in range(E)]
    ptr = ctx.ps("g_ptr", [128, 1024], BF16)

    prep_es = ExitStack()
    wld = prep_es.enter_context(nc.sbuf_tensor("g_wld", [128, D], F32))
    ssb = prep_es.enter_context(nc.sbuf_tensor("g_ssb", [1, 1], F32))

    ctx.sem("gp_ld")
    ctx.sem("gp_tok")
    ctx.sem("gp_tev")
    ctx.sem("gp_trs"); ctx.sem("gp_ev")
    ctx.sem("gp_val"); ctx.sem("gp_vev")
    ctx.sem("gp_s")
    g = nc.gpsimd

    def gld(out_ap, in_ap, **kw):
        ctx.wait(nc.gpsimd, "gp_ld")
        ctx.inc(g.dma_start(out=out_ap, in_=in_ap, **kw), "gp_ld", 16)

    gld(wtb[:], w["l0g_wt_b"].unsqueeze(0))
    gld(wdb[:], w["l0g_wd_b"].unsqueeze(0))
    gld(ssb[:], w["l0g_scale"].unsqueeze(0))
    gld(wg[:].rearrange("p (a c) -> p a c", c=6),
        w["l0g_w_gate"].rearrange("(a b) c -> b a c", b=128))
    gld(at[:], w["l0g_A"].rearrange("e l r -> r (e l)"), allow_slow_non_contiguous=True)
    for e in range(E):
        gld(bt[e][:], w["l0g_B"][e])

    prep_ps = ExitStack()
    psc = prep_ps.enter_context(nc.psum_tensor("g_psc", [128, 8], F32))
    ctx.wait(nc.tensor, "gp_ld")
    tin = nc.tensor.matmul(psc[:, 0:1], ktiles["ones_row_f"][:], ssb[:], start=True, stop=True)
    ctx.inc(tin, "gp_s", 1)
    ctx.wait(nc.vector, "gp_s")
    ctx.inc(ctx.V.tensor_copy(scale128[:], psc[:, 0:1]), "gp_s", 1)

    ptok = prep_ps.enter_context(nc.psum_tensor("g_ptok", [128, 768], F32))
    for kc in range(8):
        for e in range(E):
            tin = nc.tensor.matmul(ptok[:, e * 128:e * 128 + 100], bt[e][:, kc * 128:(kc + 1) * 128],
                                   at[:, e * 100:(e + 1) * 100], start=True, stop=True)
        ctx.inc(tin, "gp_tok", 1)
        ctx.wait(nc.vector, "gp_tok")
        ctx.inc(ctx.V.tensor_copy(
            tokfm[:, kc * 600:(kc + 1) * 600].rearrange("p (e l) -> p e l", l=100),
            ptok[:].rearrange("p (e l) -> p e l", l=128)[:, :, 0:100]), "gp_tev", 1)
        if kc < 7:
            ctx.wait(nc.tensor, "gp_tev")
    ptrf = prep_ps.enter_context(nc.psum_tensor("g_ptrf", [128, 1024], F32))
    def prep_t2(wap, dst):
        for i in range(8):
            if ctx.val("gp_trs") > 0:
                ctx.wait(nc.gpsimd, "gp_trs", ctx.val("gp_trs"))
            gld(wld[:], wap[i * 128:(i + 1) * 128, :])
            ctx.wait(nc.tensor, "gp_ld")
            if ctx.val("gp_ev") > 0:
                ctx.wait(nc.tensor, "gp_ev")
            for c in range(8):
                tin = nc.tensor.transpose(ptrf[:, c * 128:c * 128 + 128],
                                          wld[:, c * 128:c * 128 + 128], ktiles["ident_f"][:])
            ctx.inc(tin, "gp_trs", 1)
            ctx.wait(nc.vector, "gp_trs")
            dv = dst[:, :].rearrange("p (cc x) -> p cc x", x=D)
            ctx.inc(ctx.V.tensor_copy(dv[:, :, i * 128:(i + 1) * 128],
                                      ptrf[:].rearrange("p (cc x) -> p cc x", x=128)), "gp_ev", 1)
    prep_t2(w["l0g_wt_w"], wtT)

    pval = prep_ps.enter_context(nc.psum_tensor("g_pval", [128, 1024], F32))
    ctx.wait(nc.tensor, "gp_ev")
    for e in range(E):
        ctx.wait(nc.tensor, "gp_tev")
        for b in range(2):
            for kc in range(8):
                nc.tensor.matmul(pval[0:100, b * 512:b * 512 + 512],
                                 tokfm[:, kc * 600 + e * 100:kc * 600 + (e + 1) * 100],
                                 wtT[:, kc * D + b * 512:kc * D + b * 512 + 512],
                                 start=(kc == 0), stop=False)
            tin = nc.tensor.matmul(pval[0:100, b * 512:b * 512 + 512],
                                   ktiles["ones_row_b"][:, 0:100],
                                   wtb[:, b * 512:b * 512 + 512], start=False, stop=True)
        ctx.inc(tin, "gp_val", 1)
        ctx.wait(nc.scalar, "gp_val")
        ctx.inc(ctx.S.copy(val0[0:100, e * D:(e + 1) * D], pval[0:100, :]), "gp_vev", 1)
        if e < E - 1:
            ctx.wait(nc.tensor, "gp_vev")
    ctx.wait(nc.vector, "gp_vev")
    ctx.inc(ctx.V.memset(val0[0:1, :], 0.0), "gp_vev", 1)

    prep_t2(w["l0g_wd_w"], wdT)
    prep_es.close()
    prep_ps.close()

    if DBG.get("l0g"):
        ctx.tap("s128", scale128[:], ["gp_s"])

    passb_es = ExitStack()
    _s = lambda n, sh_, dt_: passb_es.enter_context(nc.sbuf_tensor(n, sh_, dt_))
    xb = [_s("g_xb0", [128, D], F32), _s("g_xb1", [128, D], F32)]
    xln = [_s("g_xln0", [128, D], BF16), _s("g_xln1", [128, D], BF16)]
    xlnf = [_s("g_xlnf0", [128, D], F32), _s("g_xlnf1", [128, D], F32)]
    xfm = [_s("g_xfm0", [128, D], BF16), _s("g_xfm1", [128, D], BF16)]
    gl = _s("g_gl", [128, 8], F32)
    gmx = _s("g_gmx", [128, 1], F32)
    gates = _s("g_gates", [128, 8], F32)
    s1 = _s("g_s1", [128, 600], F32)
    ebf = _s("g_ebf", [128, 600], BF16)
    sums = _s("g_sums", [128, 8], F32)
    grow = _s("g_grow", [128, 8], F32)
    ag = _s("g_ag", [128, 600], BF16)
    atg = _s("g_atg", [128, 768], BF16)
    dsx = _s("g_dsx", [128, D], BF16)
    dsxf = _s("g_dsxf", [128, D], BF16)
    esb = _s("g_esb", [128, D], F32)
    emax = _s("g_emax", [128, D], F32)
    fout = [_s("g_fo0", [128, D], F32), _s("g_fo1", [128, D], F32)]
    pz = ctx.ps("g_pz", [128, 600], F32)
    pl = ctx.ps("g_pl", [128, 8], F32)
    pds = ctx.ps("g_pds", [128, 1024], F32)
    pdl = ctx.ps("g_pdl", [128, 1024], F32)

    ctx.sem("gb_ld0"); ctx.sem("gb_ld1")
    ctx.sem("gb_ln"); ctx.sem("gb_trx"); ctx.sem("gb_fm")
    ctx.sem("gb_z"); ctx.sem("gb_l"); ctx.sem("gb_gev"); ctx.sem("gb_sm")
    ctx.sem("gb_ta"); ctx.sem("gb_tat"); ctx.sem("gb_taev"); ctx.sem("gb_ds"); ctx.sem("gb_dsx")
    ctx.sem("gb_trd"); ctx.sem("gb_dfm"); ctx.sem("gb_dl"); ctx.sem("gb_ee"); ctx.sem("gb_eexp")
    ctx.sem("gb_foD"); ctx.sem("gb_foA"); ctx.sem("gb_fo2"); ctx.sem("gb_fsA"); ctx.sem("gb_fsD")
    ctx.sem("gb_st0"); ctx.sem("gb_st1")

    for i in range(NT):
        p = i % 2
        if i == 0:
            ctx.wait(nc.sync, "gp_trs")
            ctx.wait(nc.sync, "gp_s")
        if i >= 2:
            ctx.wait(nc.sync, "gb_ln", ctx.val("gb_ln") - 1)
        ctx.inc(nc.sync.dma_start(out=xb[p][:], in_=x_in[i * 128:(i + 1) * 128, :]),
                "gb_ld%d" % p, 16)
        ctx.wait(nc.vector, "gb_ld%d" % p)
        if i == 0:
            ctx.wait(nc.vector, "gAfinA", 1)
        if i >= 2:
            ctx.wait(nc.vector, "gb_ds", ctx.val("gb_ds") - 1)
        ctx.V.tensor_scalar(xlnf[p][:], xb[p][:], mean_a[:, i:i + 1],
                            rstd_a[:, i:i + 1], ALU.subtract, ALU.mult)
        ctx.inc(ctx.V.tensor_copy(xln[p][:], xlnf[p][:]), "gb_ln", 1)
        ctx.wait(nc.tensor, "gb_ln")
        if i == 0:
            ctx.wait(nc.tensor, "gp_vev")
            ctx.wait(nc.tensor, "gp_ev")
            ctx.wait(nc.tensor, "gp_s")
        ctx.wait(nc.tensor, "gb_dfm")
        for c in range(8):
            tin = nc.tensor.transpose(ptr[:, c * 128:c * 128 + 128],
                                      xln[p][:, c * 128:c * 128 + 128], ktiles["ident_b"][:])
        ctx.inc(tin, "gb_trx", 1)
        ctx.wait(nc.vector, "gb_trx")
        if i >= 2:
            ctx.wait(nc.vector, "gb_dl", ctx.val("gb_dl") - 1)
        ctx.inc(ctx.V.tensor_copy(xfm[p][:], ptr[:]), "gb_fm", 1)
        ctx.wait(nc.tensor, "gb_fm")
        if i >= 1:
            ctx.wait(nc.tensor, "gb_gev", ctx.val("gb_gev"))
            ctx.wait(nc.tensor, "gb_sm", ctx.val("gb_sm"))
        for kc in range(8):
            tin = nc.tensor.matmul(pl[:, 0:6], xfm[p][:, kc * 128:kc * 128 + 128],
                                   wg[:, kc * 6:kc * 6 + 6], start=(kc == 0), stop=(kc == 7))
        ctx.inc(tin, "gb_l", 1)
        for kc in range(8):
            nc.tensor.matmul(pz[:, 0:512], xfm[p][:, kc * 128:kc * 128 + 128],
                             tokfm[:, kc * 600:kc * 600 + 512], start=(kc == 0), stop=(kc == 7))
            tin = nc.tensor.matmul(pz[:, 512:600], xfm[p][:, kc * 128:kc * 128 + 128],
                                   tokfm[:, kc * 600 + 512:kc * 600 + 600],
                                   start=(kc == 0), stop=(kc == 7))
        ctx.inc(tin, "gb_z", 1)
        ctx.wait(nc.scalar, "gb_l")
        if i >= 1:
            ctx.wait(nc.scalar, "gb_sm", ctx.val("gb_sm"))
        ctx.inc(ctx.S.copy(gl[:, 0:6], pl[:, 0:6]), "gb_gev", 1)
        ctx.wait(nc.vector, "gb_gev")
        ctx.V.reduce_max(gmx[:], gl[:, 0:6], axis=AX.X)
        ctx.V.tensor_tensor(gates[:, 0:6], gl[:, 0:6], gmx[:].broadcast_to([128, 6]), ALU.is_ge)
        ctx.wait(nc.vector, "gb_z")
        ctx.V.reduce_max(sums[:, 0:6], pz[:].rearrange("p (e l) -> p e l", e=6), axis=AX.X)
        ctx.inc(ctx.V.tensor_tensor(s1[:].rearrange("p (e l) -> p e l", e=6),
                                    pz[:].rearrange("p (e l) -> p e l", e=6),
                                    sums[:, 0:6].unsqueeze(2).broadcast_to([128, 6, 100]),
                                    ALU.subtract), "gb_sm", 1)
        ctx.wait(nc.scalar, "gb_sm")
        ctx.inc(ctx.S.activation(ebf[:], s1[:], AF.Exp, scale=1.0 / 32.0), "gb_eexp", 1)
        ctx.wait(nc.vector, "gb_eexp")
        ctx.V.reduce_sum(sums[:, 0:6], ebf[:].rearrange("p (e l) -> p e l", e=6), axis=AX.X)
        ctx.V.reciprocal(grow[:, 0:6], sums[:, 0:6])
        ctx.V.tensor_mul(grow[:, 0:6], grow[:, 0:6], gates[:, 0:6])
        if i >= 1:
            ctx.wait(nc.vector, "gb_tat", ctx.val("gb_tat"))
        ctx.inc(ctx.V.tensor_tensor(ag[:].rearrange("p (e l) -> p e l", e=6),
                                    ebf[:].rearrange("p (e l) -> p e l", e=6),
                                    grow[:, 0:6].unsqueeze(2).broadcast_to([128, 6, 100]),
                                    ALU.mult), "gb_ta", 1)
        ctx.wait(nc.tensor, "gb_ta")
        ctx.wait(nc.tensor, "gb_fm")
        for e in range(E):
            tin = nc.tensor.transpose(ptr[0:100, e * 128:e * 128 + 128],
                                      ag[:, e * 100:(e + 1) * 100], ktiles["ident_b"][:])
        ctx.inc(tin, "gb_tat", 1)
        ctx.wait(nc.vector, "gb_tat")
        if i >= 1:
            ctx.wait(nc.vector, "gb_ds", ctx.val("gb_ds"))
        ctx.inc(ctx.V.tensor_copy(atg[0:100, :], ptr[0:100, 0:768]), "gb_taev", 1)
        ctx.wait(nc.tensor, "gb_taev")
        if i >= 1:
            ctx.wait(nc.tensor, "gb_dsx", ctx.val("gb_dsx"))
        for e in range(E):
            for b in range(2):
                tin = nc.tensor.matmul(pds[:, b * 512:b * 512 + 512],
                                       atg[0:100, e * 128:e * 128 + 128],
                                       val0[0:100, e * D + b * 512:e * D + b * 512 + 512],
                                       start=(e == 0), stop=(e == E - 1))
        ctx.inc(tin, "gb_ds", 1)
        ctx.wait(nc.vector, "gb_ds")
        if i >= 1:
            ctx.wait(nc.vector, "gb_trd", ctx.val("gb_trd"))
        ctx.inc(ctx.V.tensor_tensor(dsx[:], pds[:], xln[p][:], ALU.add), "gb_dsx", 1)
        ctx.wait(nc.tensor, "gb_dsx")
        ctx.wait(nc.tensor, "gb_taev")
        for c in range(8):
            tin = nc.tensor.transpose(ptr[:, c * 128:c * 128 + 128],
                                      dsx[:, c * 128:c * 128 + 128], ktiles["ident_b"][:])
        ctx.inc(tin, "gb_trd", 1)
        ctx.wait(nc.vector, "gb_trd")
        if i >= 1:
            ctx.wait(nc.vector, "gb_dl", ctx.val("gb_dl"))
        ctx.inc(ctx.V.tensor_copy(dsxf[:], ptr[:]), "gb_dfm", 1)
        ctx.wait(nc.tensor, "gb_dfm")
        if i >= 1:
            ctx.wait(nc.tensor, "gb_ee", ctx.val("gb_ee"))
        for b in range(2):
            for kc in range(8):
                nc.tensor.matmul(pdl[:, b * 512:b * 512 + 512], dsxf[:, kc * 128:kc * 128 + 128],
                                 wdT[:, kc * D + b * 512:kc * D + b * 512 + 512],
                                 start=(kc == 0), stop=False)
            tin = nc.tensor.matmul(pdl[:, b * 512:b * 512 + 512], ktiles["ones_row_b"][:],
                                   wdb[:, b * 512:b * 512 + 512], start=False, stop=True)
        ctx.inc(tin, "gb_dl", 1)
        ctx.wait(nc.scalar, "gb_dl")
        if i >= 1:
            ctx.wait(nc.scalar, "gb_fo2", ctx.val("gb_fo2"))
        ctx.inc(ctx.S.activation(esb[:], pdl[:], AF.Exp, scale=scale128[:]), "gb_ee", 1)
        ctx.wait(nc.vector, "gb_ee")
        ctx.inc(ctx.V.tensor_scalar(emax[:], esb[:], 1.1920929e-07, None, ALU.max), "gb_foD", 1)
        ctx.wait(nc.scalar, "gb_foD")
        ctx.inc(ctx.S.activation(esb[:], emax[:], AF.Ln), "gb_foA", 1)
        ctx.wait(nc.vector, "gb_foA")
        if i >= 2:
            ctx.wait(nc.vector, "gb_st%d" % p)
        ctx.inc(ctx.V.tensor_add(fout[p][:], esb[:], xlnf[p][:]), "gb_fo2", 1)
        ctx.wait(nc.scalar, "gb_fo2")
        ctx.inc(ctx.S.activation((sh["scr"] if p == 0 else sh["scr2"])[:], fout[p][:],
                                 AF.Square, accum_out=sh["ssq"][:, i:i + 1]), "gb_fsA", 1)
        ctx.inc(ctx.V.reduce_sum(sh["nsum"][:, i:i + 1], fout[p][:], axis=AX.X,
                                 negate=True), "gb_fsD", 1)
        if i == 0 and DBG.get("l0g"):
            ctx.tap("xln0", xb[p][:], ["gb_ln"])
            ctx.tap("gl0", gl[:, 0:6], ["gb_gev"])
            ctx.tap("gates0", gates[:, 0:6], ["gb_sm"])
            ctx.tap("s10", s1[:], ["gb_sm"])
            ctx.tap("esb0", esb[:], ["gb_ee"])
            ctx.tap("fout0", fout[p][:], ["gb_foA"])
        ctx.wait(nc.gpsimd, "gb_fsA")
        ctx.wait(nc.gpsimd, "gb_fsD")
        ctx.inc(nc.gpsimd.dma_start(out=f_raw[i * 128:(i + 1) * 128, :], in_=fout[p][:]),
                "gb_st%d" % p, 16)

    emit_stats_finish(ctx, sh, mean_f, rstd_f, "gb_")
    passb_es.close()
    ctx.wait(nc.gpsimd, "gb_st0")
    ctx.wait(nc.gpsimd, "gb_st1")
    barrier(ctx, "g_bar")
    emit_residual_pass(ctx, sh, x_in, f_raw, x_out, mean_f, rstd_f, "gc_", "gA")


def emit_l1_attn(ctx, ktiles, sh, x_in, x_out, w):
    nc = ctx.nc
    H, HD = 16, 64
    NTT = 16
    NIT = 5

    mean_a = ctx.sb("a_meanA", [128, NT], F32)
    rstd_a = ctx.sb("a_rstdA", [128, NT], F32)
    mean_f = ctx.sb("a_meanF", [128, NT], F32)
    rstd_f = ctx.sb("a_rstdF", [128, NT], F32)
    f_raw = sh["f_raw"]

    emit_ln_stats(ctx, sh, x_in, mean_a, rstd_a, "aA")

    batt = ctx.sb("a_batt", [1, 3072], BF16)
    ctx.sem("ap_ld")
    g = nc.gpsimd

    def ald(out_ap, in_ap):
        ctx.wait(nc.gpsimd, "ap_ld")
        ctx.inc(g.dma_start(out=out_ap, in_=in_ap), "ap_ld", 16)

    ald(batt[:], w["l1a_b_attn"].unsqueeze(0))

    Qf = ctx.sb("a_Qf", [128, 8 * T], BF16)
    Kf = ctx.sb("a_Kf", [128, 8 * T], BF16)
    Vt = ctx.sb("a_Vt", [128, NTT * D], BF16)
    mN = ctx.sb("a_mN", [128, 16], F32)
    tau = ctx.sb("a_tau", [128, 16], F32)
    taup = ctx.sb("a_taup", [128, 16], F32)
    sprev = ctx.sb("a_sprev", [128, 16], F32)
    rs = ctx.sb("a_rs", [128, 16], F32)
    kcnt = ctx.sb("a_kcnt", [128, 16], F32)
    t16a = ctx.sb("a_t16a", [128, 16], F32)
    t16b = ctx.sb("a_t16b", [128, 16], F32)
    xbq = [ctx.sb("a_xb0", [128, D], F32)]
    xbq.append(xbq[0])
    xlnq = [ctx.sb("a_xln0", [128, D], BF16)]
    xlnq.append(xlnq[0])
    xfmq = [ctx.sb("a_xfm0", [128, D], BF16)]
    xfmq.append(xfmq[0])
    fout = [ctx.sb("a_fo0", [128, D], F32), ctx.sb("a_fo1", [128, D], F32)]

    ctx.sem("aq_ld0"); ctx.sem("aq_ld1")
    ctx.sem("aq_ln"); ctx.sem("aq_trx"); ctx.sem("aq_fm")
    ctx.sem("aq_mm"); ctx.sem("aq_ev")
    ctx.sem("ah_z"); ctx.sem("ah_zev"); ctx.sem("ah_m1")
    ctx.sem("ah_zm"); ctx.sem("ah_m2"); ctx.sem("ah_it"); ctx.sem("ah_tt")
    ctx.sem("ah_tev"); ctx.sem("ah_z2"); ctx.sem("ah_pr"); ctx.sem("ah_y")
    ctx.sem("ah_yev")
    ctx.sem("aj_trs"); ctx.sem("aj_fm"); ctx.sem("aj_om"); ctx.sem("aj_oe")
    ctx.sem("a_fsA"); ctx.sem("a_fsD"); ctx.sem("a_st0"); ctx.sem("a_st1")

    pr_mark = [0, 0]
    y_mark = [0, 0]
    for b in range(BPC):
        b2_ps = ExitStack()
        pq = b2_ps.enter_context(nc.psum_tensor("a_pq%d" % b, [128, 128], F32))
        pk = b2_ps.enter_context(nc.psum_tensor("a_pk%d" % b, [128, 128], F32))
        pv = b2_ps.enter_context(nc.psum_tensor("a_pv%d" % b, [128, 1024], F32))
        ptr = b2_ps.enter_context(nc.psum_tensor("a_ptr%d" % b, [128, 1024], BF16))
        watt = b2_ps.enter_context(nc.sbuf_tensor("a_watt%d" % b, [128, 8 * 3072], BF16))
        for kc in range(8):
            ald(watt[:, kc * 3072:(kc + 1) * 3072],
                w["l1a_w_attn"][kc * 128:(kc + 1) * 128, :])
        for tt in range(NTT):
            i = b * NTT + tt
            p = i % 2
            if i >= 1:
                ctx.wait(nc.sync, "aq_ln", ctx.val("aq_ln"))
            ctx.inc(nc.sync.dma_start(out=xbq[p][:], in_=x_in[i * 128:(i + 1) * 128, :]),
                    "aq_ld%d" % p, 16)
            ctx.wait(nc.vector, "aq_ld%d" % p)
            if i == 0:
                ctx.wait(nc.vector, "aAfinA", 1)
            if i >= 1:
                ctx.wait(nc.vector, "aq_trx", ctx.val("aq_trx"))
            ctx.inc(ctx.V.tensor_scalar(xlnq[p][:], xbq[p][:], mean_a[:, i:i + 1],
                                        rstd_a[:, i:i + 1], ALU.subtract, ALU.mult), "aq_ln", 1)
            ctx.wait(nc.tensor, "aq_ln")
            if tt == 0:
                ctx.wait(nc.tensor, "ap_ld")
            ctx.wait(nc.tensor, "aq_fm")
            for c in range(8):
                tin = nc.tensor.transpose(ptr[:, c * 128:c * 128 + 128],
                                          xlnq[p][:, c * 128:c * 128 + 128], ktiles["ident_b"][:])
            ctx.inc(tin, "aq_trx", 1)
            ctx.wait(nc.vector, "aq_trx")
            if i >= 1:
                ctx.wait(nc.vector, "aq_mm", ctx.val("aq_mm"))
            ctx.inc(ctx.V.tensor_copy(xfmq[p][:], ptr[:]), "aq_fm", 1)
            ctx.wait(nc.tensor, "aq_fm")
            for c in range(8):
                if not (b == 0 and tt == 0 and c == 0):
                    ctx.wait(nc.tensor, "aq_ev", ctx.val("aq_ev"))
                for kc in range(8):
                    nc.tensor.matmul(pq[:], watt[:, kc * 3072 + c * 128:kc * 3072 + c * 128 + 128],
                                     xfmq[p][:, kc * 128:kc * 128 + 128],
                                     start=(kc == 0), stop=False)
                nc.tensor.matmul(pq[:], batt[0:1, c * 128:c * 128 + 128],
                                 ktiles["ones_row_b"][:], start=False, stop=True)
                for kc in range(8):
                    nc.tensor.matmul(pk[:], watt[:, kc * 3072 + 1024 + c * 128:kc * 3072 + 1024 + c * 128 + 128],
                                     xfmq[p][:, kc * 128:kc * 128 + 128],
                                     start=(kc == 0), stop=False)
                tin = nc.tensor.matmul(pk[:], batt[0:1, 1024 + c * 128:1024 + c * 128 + 128],
                                       ktiles["ones_row_b"][:], start=False, stop=True)
                ctx.inc(tin, "aq_mm", 1)
                ctx.wait(nc.scalar, "aq_mm")
                ctx.S.activation(Qf[:, c * T + tt * 128:c * T + tt * 128 + 128], pq[:],
                                 AF.Identity, scale=0.125)
                ctx.inc(ctx.S.copy(Kf[:, c * T + tt * 128:c * T + tt * 128 + 128], pk[:]),
                        "aq_ev", 1)
            ctx.wait(nc.tensor, "aq_ev")
            for bb in range(2):
                for kc in range(8):
                    nc.tensor.matmul(pv[:, bb * 512:bb * 512 + 512],
                                     xfmq[p][:, kc * 128:kc * 128 + 128],
                                     watt[:, kc * 3072 + 2048 + bb * 512:kc * 3072 + 2048 + bb * 512 + 512],
                                     start=(kc == 0), stop=False)
                tin = nc.tensor.matmul(pv[:, bb * 512:bb * 512 + 512], ktiles["ones_row_b"][:],
                                       batt[0:1, 2048 + bb * 512:2048 + bb * 512 + 512],
                                       start=False, stop=True)
            ctx.inc(tin, "aq_mm", 1)
            ctx.wait(nc.scalar, "aq_mm")
            ctx.inc(ctx.S.copy(Vt[:, tt * D:(tt + 1) * D], pv[:]), "aq_ev", 1)
        ctx.wait(nc.tensor, "aq_ev")
        ctx.wait(nc.gpsimd, "ap_ld")
        b2_ps.close()
        barrier(ctx, "a_b2_%d" % b)

        ah_ps = ExitStack()
        _a = lambda n, sh_, dt_: ah_ps.enter_context(nc.sbuf_tensor(n + str(b), sh_, dt_))
        zsb = [_a("a_zb0", [128, T], BF16)]
        zsb.append(zsb[0])
        zmm = _a("a_zm", [128, T], BF16)
        Cb2 = [_a("a_Cb0", [128, 16 * 256], BF16), _a("a_Cb1", [128, 16 * 256], BF16)]
        ptil = [_a("a_pt0", [128, 512], BF16), _a("a_pt1", [128, 512], BF16)]
        nth = _a("a_nth", [1, T], BF16)
        ntl = _a("a_ntl", [1, T], BF16)
        dmp = [_a("a_dm0", [128, 256], BF16), _a("a_dm1", [128, 256], BF16)]
        pz1 = ah_ps.enter_context(nc.psum_tensor("a_pz1%d" % b, [128, 2048], F32))
        pz2 = [ah_ps.enter_context(nc.psum_tensor("a_pz2%d_%d" % (b, j), [128, 512], F32))
               for j in range(2)]
        pyt = ah_ps.enter_context(nc.psum_tensor("a_pyt%d" % b, [64, 512], F32))
        ptt = ah_ps.enter_context(nc.psum_tensor("a_ptt%d" % b, [1, 512], F32))
        zb_markD = [0, 0]
        zb_markP = [0, 0]

        def mk_p1(h):
            hp_ = (h % 2) * 64
            hc_ = h // 2
            CbH = Cb2[h % 2]
            ops = []
            for qtb in range(NTT):
                KT = (qtb + 1) * 128
                par = qtb % 2
                W = qtb + 1

                def u_pe(qtb=qtb, KT=KT, hp_=hp_, hc_=hc_, h=h):
                    if h + qtb > 0 or b > 0:
                        ctx.wait(nc.tensor, "ah_zev", ctx.val("ah_zev"))
                    nmm = (KT + 511) // 512
                    for mi in range(nmm):
                        k0 = mi * 512
                        kn = min(512, KT - k0)
                        nc.tensor.matmul(pz1[:, k0:k0 + kn],
                                         Qf[hp_:hp_ + 64, hc_ * T + qtb * 128:hc_ * T + qtb * 128 + 128],
                                         Kf[hp_:hp_ + 64, hc_ * T + k0:hc_ * T + k0 + kn],
                                         start=True, stop=(mi != nmm - 1))
                    tin = nc.tensor.matmul(pz1[:, KT - 128:KT], ktiles["negI128_b"][:],
                                           ktiles["uts128_b"][:], start=False, stop=True)
                    ctx.inc(tin, "ah_z", 1)
                ops.append(u_pe)

                def u_ev(KT=KT, par=par):
                    ctx.wait(nc.scalar, "ah_z")
                    ctx.wait(nc.scalar, "ah_m1", max(zb_markD))
                    ctx.wait(nc.scalar, "ah_zm", max(zb_markP))
                    ctx.inc(ctx.S.copy(zsb[par][:, 0:KT], pz1[:, 0:KT]), "ah_zev", 1)
                ops.append(u_ev)

                def u_dv(qtb=qtb, KT=KT, par=par, W=W, CbH=CbH):
                    ctx.wait(nc.vector, "ah_zev")
                    ctx.inc(ctx.V.reduce_max(CbH[:, qtb * 256:qtb * 256 + 128],
                                             zsb[par][:, 0:KT].rearrange("p (n w) -> p n w", w=W),
                                             axis=AX.X), "ah_m1", 1)
                    zb_markD[par] = ctx.val("ah_m1")
                    ctx.V.tensor_tensor(zmm[:, 0:KT].rearrange("p (n w) -> p n w", w=W),
                                        zsb[par][:, 0:KT].rearrange("p (n w) -> p n w", w=W),
                                        CbH[:, qtb * 256:qtb * 256 + 128].unsqueeze(2).broadcast_to([128, 128, W]),
                                        ALU.is_equal)
                    ctx.inc(ctx.V.scalar_tensor_tensor(zmm[:, 0:KT], zmm[:, 0:KT], -2e9,
                                                       zsb[par][:, 0:KT], ALU.mult, ALU.add), "ah_zm", 1)
                    zb_markP[par] = ctx.val("ah_zm")
                    ctx.inc(ctx.V.reduce_max(CbH[:, qtb * 256 + 128:qtb * 256 + 256],
                                             zmm[:, 0:KT].rearrange("p (n w) -> p n w", w=W),
                                             axis=AX.X), "ah_m2", 1)
                ops.append(u_dv)
            return ops

        def mk_itp2(h):
            hp_ = (h % 2) * 64
            hc_ = h // 2
            CbH = Cb2[h % 2]
            ops = []

            def u_init():
                if ctx.val("ah_tt"):
                    ctx.wait(nc.vector, "ah_tt", ctx.val("ah_tt"))
                ctx.V.reduce_max(mN[:], CbH[:].rearrange("p (n s) -> p n s", s=256),
                                 axis=AX.X)
                ctx.V.tensor_scalar(tau[:], mN[:], -1.0, None, ALU.add)
            ops.append(u_init)
            for it in range(NIT):
                for blkp in range(8):
                    def u_blk(it=it, blkp=blkp, CbH=CbH):
                        for blk in (2 * blkp, 2 * blkp + 1):
                            dp = blk % 2
                            ctx.V.tensor_scalar(dmp[dp][:], CbH[:, blk * 256:blk * 256 + 256],
                                                tau[:, blk:blk + 1], 0.0, ALU.max, ALU.add,
                                                accum_out=rs[:, blk:blk + 1])
                            ctx.V.tensor_scalar(dmp[dp][:], CbH[:, blk * 256:blk * 256 + 256],
                                                tau[:, blk:blk + 1], 0.0, ALU.is_gt, ALU.add,
                                                accum_out=kcnt[:, blk:blk + 1])
                    ops.append(u_blk)

                def u_up(it=it):
                    ctx.V.scalar_tensor_tensor(t16a[:], tau[:], -256.0, rs[:], ALU.mult, ALU.add)
                    ctx.V.tensor_scalar(t16a[:], t16a[:], -1.0, None, ALU.add)
                    ctx.V.tensor_scalar(t16b[:], kcnt[:], 1.0, None, ALU.max)
                    ctx.V.reciprocal(t16b[:], t16b[:])
                    ctx.V.tensor_mul(t16b[:], t16a[:], t16b[:])
                    ins = ctx.V.tensor_add(tau[:], tau[:], t16b[:])
                    if it == NIT - 1:
                        ctx.inc(ins, "ah_it", 1)
                ops.append(u_up)
            for grp in range(4):
                def u_tr(grp=grp, h=h):
                    if grp == 0:
                        ctx.wait(nc.tensor, "ah_it")
                    if h > 0 or grp > 0 or b > 0:
                        ctx.wait(nc.tensor, "ah_tev", ctx.val("ah_tev"))
                    for j in range(4):
                        blk = grp * 4 + j
                        tin = nc.tensor.transpose(ptt[0:1, j * 128:j * 128 + 128],
                                                  tau[:, blk:blk + 1], ktiles["ident_f"][:])
                    ctx.inc(tin, "ah_tt", 1)
                    ctx.wait(nc.vector, "ah_tt")
                    if (h > 0 or b > 0) and grp == 0:
                        ctx.wait(nc.vector, "ah_y", ctx.val("ah_y"))
                    ctx.V.tensor_scalar(nth[0:1, grp * 512:grp * 512 + 512], ptt[:],
                                        -1.0, None, ALU.mult)
                    ctx.inc(ctx.V.scalar_tensor_tensor(ntl[0:1, grp * 512:grp * 512 + 512],
                                                       ptt[:], -1.0,
                                                       nth[0:1, grp * 512:grp * 512 + 512],
                                                       ALU.mult, ALU.subtract), "ah_tev", 1)
                ops.append(u_tr)
            for sq in range(3, -1, -1):
                q0 = sq * 512
                nktc = 4 * sq + 4
                for ktc in range(nktc):
                    def u_z2(sq=sq, q0=q0, ktc=ktc, nktc=nktc, hp_=hp_, hc_=hc_, h=h):
                        par2 = ktc % 2
                        if sq == 3 and ktc == 0:
                            ctx.wait(nc.tensor, "ah_tev")
                        ctx.wait(nc.tensor, "ah_pr", pr_mark[par2])
                        if ktc == 0:
                            ctx.wait(nc.tensor, "ah_yev", ctx.val("ah_yev"))
                        nc.tensor.matmul(pz2[par2][:],
                                         Kf[hp_:hp_ + 64, hc_ * T + ktc * 128:hc_ * T + ktc * 128 + 128],
                                         Qf[hp_:hp_ + 64, hc_ * T + q0:hc_ * T + q0 + 512],
                                         start=True, stop=False)
                        nc.tensor.matmul(pz2[par2][:], ktiles["ones_row_b"][:],
                                         nth[0:1, q0:q0 + 512], start=False, stop=False)
                        dtile = ktc - 4 * sq
                        if dtile > 0:
                            nc.tensor.matmul(pz2[par2][:, 0:dtile * 128], ktiles["ones_row_b"][:],
                                             ktiles["neg1e9row_b"][0:1, 0:dtile * 128],
                                             start=False, stop=False)
                        if 0 <= dtile < 4:
                            nc.tensor.matmul(pz2[par2][:], ktiles["ones_row_b"][:],
                                             ntl[0:1, q0:q0 + 512], start=False, stop=False)
                            tin = nc.tensor.matmul(pz2[par2][:, dtile * 128:dtile * 128 + 128],
                                                   ktiles["negI128_b"][:], ktiles["lts128_b"][:],
                                                   start=False, stop=True)
                        else:
                            tin = nc.tensor.matmul(pz2[par2][:], ktiles["ones_row_b"][:],
                                                   ntl[0:1, q0:q0 + 512], start=False, stop=True)
                        ctx.inc(tin, "ah_z2", 1)
                        if ktc > 0:
                            pk_ = (ktc - 1) % 2
                            ctx.wait(nc.tensor, "ah_pr", pr_mark[pk_])
                            tin = nc.tensor.matmul(pyt[:],
                                                   Vt[:, (ktc - 1) * D + h * HD:(ktc - 1) * D + h * HD + HD],
                                                   ptil[pk_][:],
                                                   start=(ktc == 1), stop=False)
                            ctx.inc(tin, "ah_y", 1)
                            y_mark[pk_] = ctx.val("ah_y")
                        ctx.wait(nc.scalar, "ah_z2")
                        ctx.wait(nc.scalar, "ah_y", y_mark[par2])
                        ctx.inc(ctx.S.activation(ptil[par2][:], pz2[par2][:], AF.Relu), "ah_pr", 1)
                        pr_mark[par2] = ctx.val("ah_pr")
                    ops.append(u_z2)

                def u_ylast(sq=sq, q0=q0, nktc=nktc, hp_=hp_, hc_=hc_, h=h):
                    pk_ = (nktc - 1) % 2
                    ctx.wait(nc.tensor, "ah_pr", pr_mark[pk_])
                    tin = nc.tensor.matmul(pyt[:],
                                           Vt[:, (nktc - 1) * D + h * HD:(nktc - 1) * D + h * HD + HD],
                                           ptil[pk_][:],
                                           start=(nktc == 1), stop=True)
                    ctx.inc(tin, "ah_y", 1)
                    y_mark[pk_] = ctx.val("ah_y")
                    ctx.wait(nc.scalar, "ah_y")
                    ctx.inc(ctx.S.copy(Qf[hp_:hp_ + 64, hc_ * T + q0:hc_ * T + q0 + 512],
                                       pyt[:]), "ah_yev", 1)
                ops.append(u_ylast)
            return ops

        prevA = None
        for h in range(H):
            p1ops = mk_p1(h)
            if prevA is None or DBG.get("a_seq"):
                if prevA is not None:
                    for f in prevA:
                        f()
                    prevA = None
                for f in p1ops:
                    f()
            else:
                na, nb = len(prevA), len(p1ops)
                seq = [(i / na, 0, f) for i, f in enumerate(prevA)]
                seq += [(i / nb, 1, f) for i, f in enumerate(p1ops)]
                for _, _, f in sorted(seq, key=lambda x: (x[0], x[1])):
                    f()
            prevA = mk_itp2(h)
        for f in prevA:
            f()
        ctx.wait(nc.tensor, "ah_yev")
        ah_ps.close()
        barrier(ctx, "a_at_%d" % b)

        pj_ps = ExitStack()
        wpro = pj_ps.enter_context(nc.sbuf_tensor("a_wpro%d" % b, [128, 8 * D], BF16))
        bpro = pj_ps.enter_context(nc.sbuf_tensor("a_bpro%d" % b, [1, D], BF16))
        ctx.wait(nc.gpsimd, "ap_ld")
        ctx.inc(g.dma_start(out=bpro[:], in_=w["l1a_b_proj"].unsqueeze(0)), "ap_ld", 16)
        for kc in range(8):
            ctx.wait(nc.gpsimd, "ap_ld")
            ctx.inc(g.dma_start(out=wpro[:, kc * D:(kc + 1) * D],
                                in_=w["l1a_w_proj"][kc * 128:(kc + 1) * 128, :]), "ap_ld", 16)
        po = pj_ps.enter_context(nc.psum_tensor("a_po%d" % b, [128, 1024], F32))
        for tt in range(NTT):
            i = b * NTT + tt
            p = i % 2
            ctx.wait(nc.tensor, "aj_fm")
            if tt == 0:
                ctx.wait(nc.tensor, "ap_ld")
            if ctx.val("aj_oe") > 0:
                ctx.wait(nc.tensor, "aj_oe", ctx.val("aj_oe"))
            for bb in range(2):
                for kc in range(8):
                    nc.tensor.matmul(po[:, bb * 512:bb * 512 + 512],
                                     Qf[:, kc * T + tt * 128:kc * T + tt * 128 + 128],
                                     wpro[:, kc * D + bb * 512:kc * D + bb * 512 + 512],
                                     start=(kc == 0), stop=False)
                tin = nc.tensor.matmul(po[:, bb * 512:bb * 512 + 512], ktiles["ones_row_b"][:],
                                       bpro[0:1, bb * 512:bb * 512 + 512], start=False, stop=True)
            ctx.inc(tin, "aj_om", 1)
            ctx.wait(nc.vector, "aj_om")
            if i >= 2:
                ctx.wait(nc.vector, "a_st%d" % p)
            ctx.inc(ctx.V.tensor_copy(fout[p][:], po[:]), "aj_oe", 1)
            ctx.wait(nc.scalar, "aj_oe")
            ctx.inc(ctx.S.activation((sh["scr"] if p == 0 else sh["scr2"])[:], fout[p][:],
                                     AF.Square, accum_out=sh["ssq"][:, i:i + 1]), "a_fsA", 1)
            ctx.wait(nc.vector, "aj_oe")
            ctx.inc(ctx.V.reduce_sum(sh["nsum"][:, i:i + 1], fout[p][:], axis=AX.X,
                                     negate=True), "a_fsD", 1)
            ctx.wait(nc.gpsimd, "a_fsA")
            ctx.wait(nc.gpsimd, "a_fsD")
            ctx.inc(nc.gpsimd.dma_start(out=f_raw[i * 128:(i + 1) * 128, :], in_=fout[p][:]),
                    "a_st%d" % p, 16)
        ctx.wait(nc.tensor, "aj_oe")
        ctx.wait(nc.gpsimd, "a_st0")
        ctx.wait(nc.gpsimd, "a_st1")
        pj_ps.close()
        barrier(ctx, "a_pj_%d" % b)

    emit_stats_finish(ctx, sh, mean_f, rstd_f, "a_")
    ctx.wait(nc.gpsimd, "a_st0")
    ctx.wait(nc.gpsimd, "a_st1")
    barrier(ctx, "a_bar")
    emit_residual_pass(ctx, sh, x_in, f_raw, x_out, mean_f, rstd_f, "ac_", "aA")


def emit_l0_attn(ctx, ktiles, sh, x_in, x_out, w):
    nc = ctx.nc
    NH = 8
    NTT = 16
    LN128 = float(0.5 * np.log(128.0))

    mean_a = ctx.sb("d_meanA", [128, NT], F32)
    rstd_a = ctx.sb("d_rstdA", [128, NT], F32)
    mean_f = ctx.sb("d_meanF", [128, NT], F32)
    rstd_f = ctx.sb("d_rstdF", [128, NT], F32)
    f_raw = sh["f_raw"]

    emit_ln_stats(ctx, sh, x_in, mean_a, rstd_a, "dA")
    if DBG.get("l0a"):
        ctx.wait(nc.gpsimd, "dAfinA", 1)
        ctx.tap("meanA", mean_a[:])
        ctx.tap("rstdA", rstd_a[:])
        ctx.tap("nsumA", sh["nsum"][:])
        ctx.tap("ssqA", sh["ssq"][:])

    wqkvz = ctx.sb("d_wqkvz", [128, 8 * 4096], BF16)
    wba = ctx.sb("d_wba", [128, 8 * 16], BF16)
    woutb = ctx.sb("d_wout", [128, 8 * 1024], BF16)
    cw = ctx.sb("d_cw", [128, 24 * 4], F32)
    alog = ctx.sb("d_alog", [1, 8], F32)
    dtbf = ctx.sb("d_dtbf", [1, 8], F32)
    dtbr = ctx.sb("d_dtbr", [1, 16], BF16)
    nwc = ctx.sb("d_nwc", [128, 1], F32)
    nEal = ctx.sb("d_nEal", [128, 8], F32)
    ealr = ctx.sb("d_ealr", [1, 8], F32)
    ealb = ctx.sb("d_ealb", [1, 8], BF16)
    ctx.sem("dp_ld"); ctx.sem("dp_c")
    g = nc.gpsimd

    def dld(out_ap, in_ap, **kw):
        ctx.wait(nc.gpsimd, "dp_ld")
        ctx.inc(g.dma_start(out=out_ap, in_=in_ap, **kw), "dp_ld", 16)

    for kc in range(8):
        dld(wqkvz[:, kc * 4096:(kc + 1) * 4096], w["l0a_W_qkvz"][kc * 128:(kc + 1) * 128, :])
    for kc in range(8):
        dld(wba[:, kc * 16:(kc + 1) * 16], w["l0a_W_ba"][kc * 128:(kc + 1) * 128, :])
    dld(cw[:].rearrange("p (c k) -> p c k", k=4),
        w["l0a_conv_w"].rearrange("(c p) o k -> p c (o k)", p=128))
    dld(alog[:], w["l0a_A_log"].unsqueeze(0))
    dld(dtbf[:], w["l0a_dt_bias"].unsqueeze(0))
    dld(nwc[:], w["l0a_norm_weight"].unsqueeze(1), allow_slow_non_contiguous=True)
    ctx.wait(nc.vector, "dp_ld")
    ctx.V.memset(dtbr[:], 0.0)
    ctx.V.tensor_copy(
        dtbr[0:1, :].rearrange("o (a two) -> o a two", two=2)[:, :, 1:2],
        dtbf[:].unsqueeze(2))
    ctx.inc(ctx.V.tensor_copy(ealb[:], ealr[:]) if False else ctx.V.engine_nop(), "dp_c", 1) \
        if False else None
    ctx.wait(nc.scalar, "dp_ld")
    ctx.inc(ctx.S.activation(ealr[:], alog[:], AF.Exp), "dp_c", 1)
    ctx.wait(nc.vector, "dp_c")
    ctx.inc(ctx.V.tensor_copy(ealb[:], ealr[:]), "dp_c", 1)
    pEa_es = ExitStack()
    pEa = pEa_es.enter_context(nc.psum_tensor("d_pEa", [128, 8], F32))
    ctx.wait(nc.tensor, "dp_c")
    ctx.inc(nc.tensor.matmul(pEa[:], ktiles["ones_row_b"][:], ealb[:],
                             start=True, stop=True), "dp_c", 1)
    ctx.wait(nc.vector, "dp_c")
    ctx.inc(ctx.V.tensor_scalar(nEal[:], pEa[:], -1.0, None, ALU.mult), "dp_c", 1)
    wldd_es = ExitStack()
    wldd = wldd_es.enter_context(nc.sbuf_tensor("d_wldd", [128, 1024], F32))
    for hh in range(8):
        if hh > 0:
            ctx.wait(nc.gpsimd, "dp_c", ctx.val("dp_c"))
        dld(wldd[:], w["l0a_W_out"][hh * 128:(hh + 1) * 128, :])
        ctx.wait(nc.vector, "dp_ld")
        ctx.inc(ctx.V.tensor_scalar(woutb[:, hh * 1024:(hh + 1) * 1024], wldd[:],
                                    nwc[:], None, ALU.mult), "dp_c", 1)
    ctx.wait(nc.tensor, "dp_c")
    ctx.wait(nc.gpsimd, "dp_ld")
    ctx.wait(nc.vector, "dp_c")
    wldd_es.close()
    pEa_es.close()
    barrier(ctx, "d_prep")

    xbd = [ctx.sb("d_xb0", [128, D], F32), ctx.sb("d_xb1", [128, D], F32)]
    xlnd = ctx.sb("d_xln", [128, D], BF16)
    xfm256 = ctx.sb("d_xfm", [128, 8 * 256], BF16)
    rawb = ctx.sb("d_raw", [128, 24 * 260], BF16)
    qkvc = ctx.sb("d_qkvc", [128, 24 * 256], BF16)
    szfm = ctx.sb("d_szfm", [128, 8 * 256], BF16)
    batm = ctx.sb("d_batm", [128, 16], F32)
    t8a = ctx.sb("d_t8a", [128, 8], F32)
    t8b = ctx.sb("d_t8b", [128, 8], F32)
    t8c = ctx.sb("d_t8c", [128, 8], F32)
    t8d = ctx.sb("d_t8d", [128, 8], F32)
    gcb = ctx.sb("d_gcb", [128, 16 * 8], F32)
    rvb = ctx.sb("d_rvb", [128, 16 * 8], F32)
    glb = ctx.sb("d_glb", [128, 16 * 8], F32)
    lnbb = ctx.sb("d_lnbb", [128, 16 * 8], F32)
    betb = ctx.sb("d_betb", [128, 16 * 8], F32)
    Sst = [ctx.sb("d_S%d" % hh, [128, 128], F32) for hh in range(NH)]
    Sbf = [ctx.sb("d_Sb%d" % hh, [128, 128], BF16) for hh in range(NH)]
    dtile = [ctx.sb("d_dt%d" % t_, [128, 128], BF16) for t_ in range(4)]
    ogbuf = ctx.sb("d_og", [128, 8 * 128], BF16)
    cgB = ctx.sb("d_cgB", [64, 8], F32)
    crB = ctx.sb("d_crB", [64, 8], F32)
    clB = ctx.sb("d_clB", [64, 8], F32)
    cglB = ctx.sb("d_cglB", [64, 8], F32)
    cbtB = ctx.sb("d_cbtB", [64, 8], F32)
    SL = []
    for s in range(2):
        t = "ds%d_" % s
        SL.append(dict(
            ktm=ctx.sb(t + "ktm", [64, 128], BF16),
            vtm=ctx.sb(t + "vtm", [64, 128], BF16),
            vbeta=ctx.sb(t + "vbeta", [64, 128], BF16),
            kbeg=ctx.sb(t + "kbeg", [64, 128], BF16),
            kglb=ctx.sb(t + "kgl", [64, 128], BF16),
            vnew=ctx.sb(t + "vnew", [64, 128], BF16),
            vnb=ctx.sb(t + "vn", [64, 128], BF16),
            kcum=ctx.sb(t + "kcum", [128, 64], BF16),
            qeb=ctx.sb(t + "qe", [128, 64], BF16),
            sqk=ctx.sb(t + "sqk", [64, 128], BF16),
            sqq=ctx.sb(t + "sqq", [128, 64], BF16),
            sqo=ctx.sb(t + "sqo", [128, 64], BF16),
            og1=ctx.sb(t + "og1", [128, 64], BF16),
            tri=ctx.sb(t + "tri", [64, 64], BF16),
            triT=ctx.sb(t + "triT", [64, 64], BF16),
            attnT=ctx.sb(t + "attnT", [64, 64], BF16),
            dcy=ctx.sb(t + "dcy", [64, 64], F32),
            Pb=[ctx.sb(t + "P%d" % j, [64, 64], BF16) for j in range(2)],
            Qb=[ctx.sb(t + "Q%d" % j, [64, 64], BF16) for j in range(2)],
            Xb=[ctx.sb(t + "X%d" % j, [64, 64], BF16) for j in range(2)],
            Wb=[ctx.sb(t + "W%d" % j, [64, 64], BF16) for j in range(2)],
            rows=ctx.sb(t + "rows", [1, 192], F32),
            r3r=ctx.sb(t + "r3r", [1, 64], F32),
            r1b=ctx.sb(t + "r1b", [1, 64], BF16),
            r2b=ctx.sb(t + "r2b", [1, 64], BF16),
            r3b=ctx.sb(t + "r3b", [1, 64], BF16),
            r1l=ctx.sb(t + "r1l", [1, 64], BF16),
            r2l=ctx.sb(t + "r2l", [1, 64], BF16),
            r3l=ctx.sb(t + "r3l", [1, 64], BF16),
            r3e=ctx.sb(t + "r3e", [1, 64], BF16),
            ssk=ctx.sb(t + "ssk", [64, 1], F32),
            lnrkc=ctx.sb(t + "lnrk", [64, 1], F32),
            lnqr=ctx.sb(t + "lnqr", [1, 64], F32),
            r1c=ctx.sb(t + "r1c", [64, 1], F32),
            r2mc=ctx.sb(t + "r2mc", [64, 1], F32),
            e1c=ctx.sb(t + "e1c", [64, 1], F32),
            cta=ctx.sb(t + "cta", [64, 1], F32),
            ctb=ctx.sb(t + "ctb", [64, 1], F32),
            eglc=ctx.sb(t + "eglc", [128, 1], F32),
            rsr=ctx.sb(t + "rsr", [1, 64], F32),
            rsb=ctx.sb(t + "rsb", [1, 64], BF16),
        ))
    fout = [ctx.sb("d_fo0", [128, D], F32), ctx.sb("d_fo1", [128, D], F32)]

    ctx.sem("dq_ld0"); ctx.sem("dq_ld1"); ctx.sem("dq_ln"); ctx.sem("dq_trx")
    ctx.sem("dq_fm"); ctx.sem("dq_mm"); ctx.sem("dq_ev"); ctx.sem("dq_cv")
    ctx.sem("dq_g"); ctx.sem("dq_halo")
    ctx.sem("pm0"); ctx.sem("pm1"); ctx.sem("pv0"); ctx.sem("pv1")
    ctx.sem("pa0"); ctx.sem("pa1")
    ctx.sem("do_mm"); ctx.sem("do_ev")
    ctx.sem("d_fsA"); ctx.sem("d_fsD"); ctx.sem("d_st0"); ctx.sem("d_st1")

    def dmm(out_ap, lhsT, rhs, start=True, stop=True):
        return nc.tensor.matmul(out_ap, lhsT, rhs, start=start, stop=stop)

    def pe_sync_dve():
        ctx.finalize_pending("dve")
        last = ctx.last.get("dve")
        if last is not None:
            nc.tensor.wait_ge(ctx.sems[last[0]][0], last[1])

    def act_sync_dve():
        ctx.finalize_pending("dve")
        last = ctx.last.get("dve")
        if last is not None:
            nc.scalar.wait_ge(ctx.sems[last[0]][0], last[1])

    env = dict(nc=nc, dmm=dmm, qkvc=qkvc, szfm=szfm, gcb=gcb, rvb=rvb, glb=glb,
               lnbb=lnbb, betb=betb, cgB=cgB, crB=crB, clB=clB, cglB=cglB,
               cbtB=cbtB, SL=SL, Sst=Sst, Sbf=Sbf, ogbuf=ogbuf, LN128=LN128)

    for b in range(BPC):
        for hh in range(NH):
            ctx.V.memset(Sst[hh][:], 0.0)
            ins = ctx.V.memset(Sbf[hh][:], 0.0)
            if hh == NH - 2:
                ctx.inc(ins, "pv0", 1)
            elif hh == NH - 1:
                ctx.inc(ins, "pv1", 1)
        ctx.inc(ctx.V.memset(rawb[:].rearrange("p (c x) -> p c x", x=260)[:, :, 0:3], 0.0),
                "dq_halo", 1)
        for q in range(8):
            q_ps = ExitStack()
            pqz = q_ps.enter_context(nc.psum_tensor("d_pqz%d_%d" % (b, q), [128, 256], F32))
            pba = q_ps.enter_context(nc.psum_tensor("d_pba%d_%d" % (b, q), [128, 256], F32))
            pcv = q_ps.enter_context(nc.psum_tensor("d_pcv%d_%d" % (b, q), [128, 256], F32))
            ptrq = q_ps.enter_context(nc.psum_tensor("d_ptrq%d_%d" % (b, q), [128, 1024], BF16))
            for halftile in range(2):
                tt = 2 * q + halftile
                i = b * NTT + tt
                p = i % 2
                if ctx.val("dq_ln") > 0:
                    ctx.wait(nc.sync, "dq_ln", ctx.val("dq_ln") - 1)
                ctx.inc(nc.sync.dma_start(out=xbd[p][:], in_=x_in[i * 128:(i + 1) * 128, :]),
                        "dq_ld%d" % p, 16)
                ctx.wait(nc.vector, "dq_ld%d" % p)
                if i == 0:
                    ctx.wait(nc.vector, "dAfinA", 1)
                if ctx.val("dq_trx") > 0:
                    ctx.wait(nc.vector, "dq_trx", ctx.val("dq_trx"))
                ctx.inc(ctx.V.tensor_scalar(xlnd[:], xbd[p][:], mean_a[:, i:i + 1],
                                            rstd_a[:, i:i + 1], ALU.subtract, ALU.mult),
                        "dq_ln", 1)
                ctx.wait(nc.tensor, "dq_ln")
                if i == 0:
                    ctx.wait(nc.tensor, "dp_ld")
                    ctx.wait(nc.tensor, "dp_c")
                if ctx.val("dq_fm") > 0:
                    ctx.wait(nc.tensor, "dq_fm", ctx.val("dq_fm"))
                for c in range(8):
                    tin = nc.tensor.transpose(ptrq[:, c * 128:c * 128 + 128],
                                              xlnd[:, c * 128:c * 128 + 128],
                                              ktiles["ident_b"][:])
                ctx.inc(tin, "dq_trx", 1)
                ctx.wait(nc.vector, "dq_trx")
                if ctx.val("dq_mm") > 0:
                    ctx.wait(nc.vector, "dq_mm", ctx.val("dq_mm"))
                ctx.inc(ctx.V.tensor_copy(
                    xfm256[:].rearrange("p (kc t) -> p kc t", t=256)[:, :, halftile * 128:halftile * 128 + 128],
                    ptrq[:].rearrange("p (kc t) -> p kc t", t=128)), "dq_fm", 1)
                if DBG.get("l0a") and b == 0 and q == 0 and halftile == 0:
                    barrier(ctx, "dbg_xfm0")
                    dbgtile = ctx.sb("dbg_ptrq", [128, 1024], BF16)
                    ctx.V.tensor_copy(dbgtile[:], ptrq[:])
                    dbgx = ctx.sb("dbg_xlncp", [128, 1024], BF16)
                    ctx.inc(ctx.V.tensor_copy(dbgx[:], xlnd[:]), "dq_fm", 1)
                    barrier(ctx, "dbg_xfm0c")
                    ctx.tap("xfm_t0", xfm256[:])
                    ctx.tap("ptrq0", dbgtile[:])
                    ctx.tap("xln0", dbgx[:])
                    ctx.wait(nc.gpsimd, "dbg_st")
                    barrier(ctx, "dbg_xfm0b")
            ctx.wait(nc.tensor, "dq_fm")
            ctx.wait(nc.tensor, "dq_halo")
            for c in range(32):
                h = c // 4
                role = c % 4
                if ctx.val("dq_ev") > 0:
                    ctx.wait(nc.tensor, "dq_ev", ctx.val("dq_ev"))
                for kc in range(8):
                    tin = dmm(pqz[:], wqkvz[:, kc * 4096 + c * 128:kc * 4096 + c * 128 + 128],
                              xfm256[:, kc * 256:(kc + 1) * 256],
                              start=(kc == 0), stop=(kc == 7))
                ctx.inc(tin, "dq_mm", 1)
                ctx.wait(nc.scalar, "dq_mm")
                if c == 0:
                    ctx.wait(nc.scalar, "dq_halo")
                if role == 3:
                    ctx.inc(ctx.S.activation(szfm[:, h * 256:(h + 1) * 256], pqz[:], AF.Silu),
                            "dq_ev", 1)
                else:
                    m = role * 8 + h
                    ctx.inc(ctx.S.copy(rawb[:, m * 260 + 3:m * 260 + 259], pqz[:]), "dq_ev", 1)
            for halftile in range(2):
                tt = 2 * q + halftile
                if ctx.val("dq_g") > 0:
                    ctx.wait(nc.tensor, "dq_g", ctx.val("dq_g"))
                for kc in range(8):
                    dmm(pba[:, 0:16],
                        xfm256[:, kc * 256 + halftile * 128:kc * 256 + halftile * 128 + 128],
                        wba[:, kc * 16:(kc + 1) * 16], start=(kc == 0), stop=False)
                tin = dmm(pba[:, 0:16], ktiles["ones_row_b"][:], dtbr[:],
                          start=False, stop=True)
                ctx.inc(tin, "dq_mm", 1)
                ctx.wait(nc.vector, "dq_mm")
                ctx.V.tensor_copy(batm[:], pba[:, 0:16])
                bav = batm[:].rearrange("p (a two) -> p a two", two=2)
                bcol = t8c
                acol = t8d
                ctx.V.tensor_copy(bcol[:], bav[:, :, 0:1].squeeze(2))
                ctx.V.tensor_copy(acol[:], bav[:, :, 1:2].squeeze(2))
                bcol = bcol[:]
                acol = acol[:]
                lnbt = lnbb[:, tt * 8:(tt + 1) * 8]
                ctx.V.tensor_scalar(t8a[:], bcol, -1.0, None, ALU.mult)
                ctx.V.tensor_max(t8a[:], t8a[:], bcol)
                act_sync_dve()
                ctx.S.activation(t8a[:], t8a[:], AF.Exp, scale=-1.0)
                ctx.inc(ctx.S.activation(t8a[:], t8a[:], AF.Ln, bias=1.0), "dq_g", 1)
                ctx.wait(nc.vector, "dq_g")
                ctx.V.tensor_scalar(t8b[:], bcol, 0.0, None, ALU.min)
                ctx.V.tensor_sub(lnbt, t8b[:], t8a[:])
                ctx.V.tensor_scalar(t8a[:], acol, -1.0, None, ALU.mult)
                ctx.V.tensor_max(t8a[:], t8a[:], acol)
                act_sync_dve()
                ctx.S.activation(t8a[:], t8a[:], AF.Exp, scale=-1.0)
                ctx.inc(ctx.S.activation(t8a[:], t8a[:], AF.Ln, bias=1.0), "dq_g", 1)
                ctx.wait(nc.vector, "dq_g")
                ctx.V.tensor_scalar(t8b[:], acol, 0.0, None, ALU.max)
                ctx.V.tensor_add(t8a[:], t8a[:], t8b[:])
                ctx.V.tensor_mul(t8a[:], t8a[:], nEal[:])
                act_sync_dve()
                ctx.inc(ctx.S.activation(betb[:, tt * 8:(tt + 1) * 8], lnbt, AF.Exp), "dq_g", 1)
                pe_sync_dve()
                dmm(pba[:, 16:24], ktiles["bd_ut_f"][:], t8a[:])
                dmm(pba[:, 24:32], ktiles["bd_lts_f"][:], t8a[:])
                tin = dmm(pba[:, 32:40], ktiles["bd_on_f"][:], t8a[:])
                ctx.inc(tin, "dq_mm", 1)
                ctx.wait(nc.vector, "dq_mm")
                ctx.V.tensor_copy(gcb[:, tt * 8:(tt + 1) * 8], pba[:, 16:24])
                ctx.V.tensor_copy(rvb[:, tt * 8:(tt + 1) * 8], pba[:, 24:32])
                ctx.inc(ctx.V.tensor_copy(glb[:, tt * 8:(tt + 1) * 8], pba[:, 32:40]),
                        "dq_g", 1)
            ctx.wait(nc.vector, "dq_ev")
            for m in range(24):
                if ctx.val("dq_cv") > 0:
                    ctx.wait(nc.vector, "dq_cv", ctx.val("dq_cv"))
                for tp in range(4):
                    ctx.V.tensor_scalar(dtile[tp][:], ktiles["ident_b"][:],
                                        cw[:, m * 4 + tp:m * 4 + tp + 1], None, ALU.mult)
                pe_sync_dve()
                for tp in range(4):
                    tin = dmm(pcv[:], dtile[tp][:], rawb[:, m * 260 + tp:m * 260 + tp + 256],
                              start=(tp == 0), stop=(tp == 3))
                ctx.inc(tin, "dq_cv", 1)
                ctx.wait(nc.scalar, "dq_cv")
                ctx.inc(ctx.S.activation(qkvc[:, m * 256:(m + 1) * 256], pcv[:], AF.Silu),
                        "dq_ev", 1)
                ctx.wait(nc.tensor, "dq_ev")
            ctx.wait(nc.vector, "dq_cv")
            ctx.wait(nc.vector, "dq_ev")
            rawv = rawb[:].rearrange("p (c x) -> p c x", x=260)
            ctx.inc(ctx.V.tensor_copy(rawv[:, :, 0:3], rawv[:, :, 256:259]), "dq_halo", 1)
            q_ps.close()
            barrier(ctx, "d_q%d_%d" % (b, q))
            if DBG.get("l0a") and b == 0 and q == 0:
                ctx.tap("qkvc", qkvc[:])
                ctx.tap("szfm", szfm[:])
                ctx.tap("rawb", rawb[:])
                ctx.tap("gcb", gcb[:, 0:16])
                ctx.tap("lnbb", lnbb[:, 0:16])
                ctx.tap("rvb", rvb[:, 0:16])
                ctx.tap("glb", glb[:, 0:16])
                ctx.tap("betb", betb[:, 0:16])
                ctx.tap("xfm256", xfm256[:])
                ctx.tap("xlnd", xlnd[:])
                ctx.tap("wq0", wqkvz[:, 0:256])
                ctx.wait(nc.gpsimd, "dbg_st")
                barrier(ctx, "dbg_stage0")
                if DBG.get("l0a_stop"):
                    return

            d_ps = ExitStack()
            PS = []
            for s in range(2):
                pbig = d_ps.enter_context(
                    nc.psum_tensor("dB%d_%d_%d" % (b, q, s), [64, 512], F32))
                pTbs = d_ps.enter_context(
                    nc.psum_tensor("dT%d_%d_%d" % (b, q, s), [64, 256], BF16))
                pKSO = d_ps.enter_context(
                    nc.psum_tensor("dK%d_%d_%d" % (b, q, s), [128, 384], F32))
                PS.append(dict(
                    A=pbig[:, 0:64], B=pbig[:, 64:128], D=pbig[:, 128:192],
                    I0=pbig[:, 192:256], I1=pbig[:, 256:320], V=pbig[:, 192:320],
                    qrow=pbig[0:1, 320:384], tgc=pbig[0:1, 320:384],
                    tr1=pbig[0:1, 384:448], tr2m=pbig[0:1, 448:512],
                    rowsv=pbig[0:1, 320:512],
                    K=pKSO[:, 0:64], S=pKSO[:, 64:192], O=pKSO[:, 192:256],
                    ss=pKSO[0:1, 256:320], qB=pKSO[:, 320:384],
                    Tk=pTbs[:, 0:128], Tv=pTbs[:, 128:256]))
            env["PS"] = PS
            pout = d_ps.enter_context(
                nc.psum_tensor("dO%d_%d" % (b, q), [128, 1024], F32))
            for grp in range(2):
                ig = b * NTT + 2 * q + grp
                pg = ig % 2
                for c64 in range(2):
                    for hp in range(4):
                        emit_delta_pair(ctx, ktiles, env, q, grp, hp, c64)
                ctx.wait(nc.tensor, "pv0")
                ctx.wait(nc.tensor, "pv1")
                if ctx.val("do_ev") > 0:
                    ctx.wait(nc.tensor, "do_ev", ctx.val("do_ev"))
                for bb in range(2):
                    for hh2 in range(8):
                        tin = dmm(pout[:, bb * 512:bb * 512 + 512],
                                  ogbuf[:, hh2 * 128:(hh2 + 1) * 128],
                                  woutb[:, hh2 * 1024 + bb * 512:hh2 * 1024 + bb * 512 + 512],
                                  start=(hh2 == 0), stop=(hh2 == 7))
                ctx.inc(tin, "do_mm", 1)
                ctx.wait(nc.vector, "do_mm")
                if ctx.val("d_st%d" % pg) > 0:
                    ctx.wait(nc.vector, "d_st%d" % pg)
                ctx.V.tensor_copy(fout[pg][:, 0:512], pout[:, 0:512])
                ctx.inc(ctx.V.tensor_copy(fout[pg][:, 512:1024], pout[:, 512:1024]),
                        "do_ev", 1)
                ctx.wait(nc.scalar, "do_ev")
                ctx.inc(ctx.S.activation((sh["scr"] if pg == 0 else sh["scr2"])[:], fout[pg][:],
                                         AF.Square, accum_out=sh["ssq"][:, ig:ig + 1]), "d_fsA", 1)
                ctx.wait(nc.vector, "d_fsA")
                ctx.inc(ctx.V.reduce_sum(sh["nsum"][:, ig:ig + 1], fout[pg][:], axis=AX.X,
                                         negate=True), "d_fsD", 1)
                ctx.wait(nc.gpsimd, "d_fsA")
                ctx.wait(nc.gpsimd, "d_fsD")
                ctx.inc(nc.gpsimd.dma_start(out=f_raw[ig * 128:(ig + 1) * 128, :],
                                            in_=fout[pg][:]), "d_st%d" % pg, 16)
            d_ps.close()
            barrier(ctx, "d_e%d_%d" % (b, q))

    emit_stats_finish(ctx, sh, mean_f, rstd_f, "d_")
    ctx.wait(nc.gpsimd, "d_st0")
    ctx.wait(nc.gpsimd, "d_st1")
    barrier(ctx, "d_bar")
    emit_residual_pass(ctx, sh, x_in, f_raw, x_out, mean_f, rstd_f, "dc_", "dA")


def emit_delta_pair(ctx, ktiles, L, q, grp, hp, c64):
    nc = L["nc"]
    V, S = ctx.V, ctx.S
    dmm = L["dmm"]
    toff = (grp * 2 + c64) * 64
    tloc = 2 * q + (toff // 128)
    po = toff % 128
    Sst, Sbf, ogbuf = L["Sst"], L["Sbf"], L["ogbuf"]
    SL, PS = L["SL"], L["PS"]
    LN128 = L["LN128"]
    sm = ("pm0", "pm1")
    sv = ("pv0", "pv1")
    sa = ("pa0", "pa1")
    heads = (2 * hp, 2 * hp + 1)
    qfv, kfv, vfv = {}, {}, {}
    for s in (0, 1):
        hh = heads[s]
        qfv[s] = L["qkvc"][:, hh * 256 + toff:hh * 256 + toff + 64]
        kfv[s] = L["qkvc"][:, (8 + hh) * 256 + toff:(8 + hh) * 256 + toff + 64]
        vfv[s] = L["qkvc"][:, (16 + hh) * 256 + toff:(16 + hh) * 256 + toff + 64]

    if hp == 0:
        for dst, srcb in ((L["cgB"], L["gcb"]), (L["crB"], L["rvb"]),
                          (L["clB"], L["lnbb"]), (L["cglB"], L["glb"]),
                          (L["cbtB"], L["betb"])):
            V.tensor_copy(dst[:], srcb[po:po + 64, tloc * 8:tloc * 8 + 8])

    for s in (0, 1):
        P = PS[s]
        if ctx.val(sv[s]):
            ctx.wait(nc.tensor, sv[s])
        if ctx.val(sa[s]):
            ctx.wait(nc.tensor, sa[s])
        dmm(P["A"], kfv[s], kfv[s])
        ctx.inc(dmm(P["B"], kfv[s], qfv[s]), sm[s], 1)
        nc.tensor.transpose(P["Tk"], kfv[s], ktiles["ident_b"][:])
        ctx.inc(nc.tensor.transpose(P["Tv"], vfv[s], ktiles["ident_b"][:]), sm[s], 1)
    for s in (0, 1):
        B = SL[s]; P = PS[s]
        ctx.wait(nc.scalar, sm[s])
        S.copy(B["ktm"][:], P["Tk"])
        ctx.inc(S.copy(B["vtm"][:], P["Tv"]), sa[s], 1)
    for s in (0, 1):
        B = SL[s]
        ctx.wait(nc.vector, sa[s])
        V.scalar_tensor_tensor(B["sqk"][:], B["ktm"][:], 1.0, B["ktm"][:],
                               ALU.mult, ALU.mult, accum_out=B["ssk"][:])
        ctx.inc(V.scalar_tensor_tensor(B["sqq"][:], qfv[s], 1.0, qfv[s],
                                       ALU.mult, ALU.mult), sv[s], 1)
    for s in (0, 1):
        B = SL[s]
        ctx.wait(nc.scalar, sv[s])
        S.activation(B["lnrkc"][:], B["ssk"][:], AF.Ln, bias=1e-6)
        ctx.inc(S.activation(B["lnrkc"][:], B["lnrkc"][:], AF.Identity, scale=-0.5),
                sa[s], 1)
    for s in (0, 1):
        B = SL[s]; P = PS[s]
        ctx.wait(nc.tensor, sv[s])
        ctx.inc(dmm(P["qrow"], ktiles["ones_col_b"][:], B["sqq"][:]), sm[s], 1)
    for s in (0, 1):
        B = SL[s]; P = PS[s]
        ctx.wait(nc.scalar, sm[s])
        S.activation(B["lnqr"][:], P["qrow"], AF.Ln, bias=1e-6)
        ctx.inc(S.activation(B["lnqr"][:], B["lnqr"][:], AF.Identity, scale=-0.5,
                             bias=-LN128), sa[s], 1)
    for s in (0, 1):
        hh = heads[s]; B = SL[s]
        ctx.wait(nc.vector, sa[s])
        V.tensor_tensor(B["r1c"][:], L["cgB"][:, hh:hh + 1], L["clB"][:, hh:hh + 1],
                        ALU.add)
        V.tensor_tensor(B["r1c"][:], B["r1c"][:], B["lnrkc"][:], ALU.add)
        ctx.inc(V.tensor_tensor(B["r2mc"][:], B["lnrkc"][:], L["cgB"][:, hh:hh + 1],
                                ALU.subtract), sv[s], 1)
    for s in (0, 1):
        hh = heads[s]; B = SL[s]; P = PS[s]
        ctx.wait(nc.tensor, sa[s])
        ctx.wait(nc.tensor, sv[s])
        idp = ktiles["ident_f"][0:64, 0:64]
        nc.tensor.transpose(P["tgc"], L["cgB"][:, hh:hh + 1], idp)
        nc.tensor.transpose(P["tr1"], B["r1c"][:], idp)
        ctx.inc(nc.tensor.transpose(P["tr2m"], B["r2mc"][:], idp), sm[s], 1)
    for s in (0, 1):
        B = SL[s]; P = PS[s]
        ctx.wait(nc.vector, sm[s])
        V.tensor_copy(B["rows"][:], P["rowsv"])
        gcr = B["rows"][0:1, 0:64]
        r1r = B["rows"][0:1, 64:128]
        r2r = B["rows"][0:1, 128:192]
        V.tensor_add(B["r3r"][:], gcr, B["lnqr"][:])
        V.tensor_copy(B["r1b"][:], r1r)
        V.tensor_copy(B["r2b"][:], r2r)
        V.tensor_copy(B["r3b"][:], B["r3r"][:])
        V.scalar_tensor_tensor(B["r1l"][:], r1r, 1.0, B["r1b"][:], ALU.mult,
                               ALU.subtract)
        V.scalar_tensor_tensor(B["r2l"][:], r2r, 1.0, B["r2b"][:], ALU.mult,
                               ALU.subtract)
        ctx.inc(V.scalar_tensor_tensor(B["r3l"][:], B["r3r"][:], 1.0, B["r3b"][:],
                                       ALU.mult, ALU.subtract), sv[s], 1)
    for s in (0, 1):
        B = SL[s]
        ctx.wait(nc.scalar, sv[s])
        S.activation(B["r3e"][:], B["r3r"][:], AF.Exp)
        ctx.inc(S.activation(B["e1c"][:], B["r1c"][:], AF.Exp), sa[s], 1)
    for k in range(3):
        t2, t2l = (("r2b", "r2l"), ("r1b", "r1l"), ("r3b", "r3l"))[k]
        msk = ("ut64i_b", "lt64i_b", "lt64s_b")[k]
        for s in (0, 1):
            B = SL[s]; P = PS[s]
            ctx.wait(nc.tensor, sv[s])
            ctx.wait(nc.tensor, sa[s])
            dmm(P["D"], ktiles["ones_row_b"][0:1, 0:64], B[t2][:],
                start=True, stop=False)
            dmm(P["D"], ktiles["ones_row_b"][0:1, 0:64], B[t2l][:],
                start=False, stop=False)
            ctx.inc(dmm(P["D"], ktiles["negI64_b"][:], ktiles[msk][:],
                        start=False, stop=True), sm[s], 1)
        for s in (0, 1):
            B = SL[s]; P = PS[s]
            bias_col = B["r1c"] if k == 0 else B["r2mc"]
            ctx.wait(nc.scalar, sm[s])
            ctx.wait(nc.scalar, sv[s])
            ctx.inc(S.activation(B["dcy"][:], P["D"], AF.Exp, bias=bias_col[:]),
                    sa[s], 1)
        for s in (0, 1):
            B = SL[s]; P = PS[s]
            dst = (B["tri"], B["triT"], B["attnT"])[k]
            asl = ("A", "A", "B")[k]
            ctx.wait(nc.vector, sa[s])
            ctx.inc(V.tensor_tensor(dst[:], P[asl], B["dcy"][:], ALU.mult), sv[s], 1)
    for s in (0, 1):
        B = SL[s]
        V.tensor_scalar(B["Pb"][0][:], B["triT"][:], -1.0, None, ALU.mult)
        V.tensor_scalar(B["Qb"][0][:], B["tri"][:], -1.0, None, ALU.mult)
        V.tensor_add(B["Xb"][0][:], B["Pb"][0][:], ktiles["eye64_b"][:])
        ctx.inc(V.tensor_add(B["Wb"][0][:], B["Qb"][0][:], ktiles["eye64_b"][:]),
                sv[s], 1)
    cur, nxt = 0, 1
    for s_ in range(5):
        for s in (0, 1):
            B = SL[s]; P = PS[s]
            ctx.wait(nc.tensor, sv[s])
            ctx.wait(nc.tensor, sa[s])
            ctx.inc(dmm(P["I0"], B["Qb"][cur][:], B["Pb"][cur][:]), sm[s], 1)
        for s in (0, 1):
            B = SL[s]; P = PS[s]
            ctx.wait(nc.scalar, sm[s])
            ctx.inc(S.copy(B["Pb"][nxt][:], P["I0"]), sa[s], 1)
        for s in (0, 1):
            B = SL[s]; P = PS[s]
            ctx.wait(nc.tensor, sa[s])
            ctx.inc(dmm(P["I1"], B["Wb"][cur][:], B["Pb"][nxt][:]), sm[s], 1)
        for s in (0, 1):
            B = SL[s]; P = PS[s]
            ctx.wait(nc.vector, sm[s])
            ctx.inc(V.tensor_tensor(B["Xb"][nxt][:], P["I1"], B["Xb"][cur][:],
                                    ALU.add), sv[s], 1)
        if s_ < 4:
            for s in (0, 1):
                B = SL[s]; P = PS[s]
                ctx.wait(nc.tensor, sv[s])
                ctx.inc(dmm(P["I0"], B["Pb"][cur][:], B["Qb"][cur][:]), sm[s], 1)
            for s in (0, 1):
                B = SL[s]; P = PS[s]
                ctx.wait(nc.scalar, sm[s])
                ctx.inc(S.copy(B["Qb"][nxt][:], P["I0"]), sa[s], 1)
            for s in (0, 1):
                B = SL[s]; P = PS[s]
                ctx.wait(nc.tensor, sa[s])
                ctx.inc(dmm(P["I1"], B["Pb"][nxt][:], B["Wb"][cur][:]), sm[s], 1)
            for s in (0, 1):
                B = SL[s]; P = PS[s]
                ctx.wait(nc.vector, sm[s])
                ctx.inc(V.tensor_tensor(B["Wb"][nxt][:], P["I1"], B["Wb"][cur][:],
                                        ALU.add), sv[s], 1)
        cur, nxt = nxt, cur
    for s in (0, 1):
        hh = heads[s]; B = SL[s]
        ctx.wait(nc.vector, sa[s])
        V.tensor_scalar(B["vbeta"][:], B["vtm"][:], L["cbtB"][:, hh:hh + 1],
                        None, ALU.mult)
        ctx.inc(V.tensor_scalar(B["kbeg"][:], B["ktm"][:], B["e1c"][:],
                                None, ALU.mult), sv[s], 1)
    for s in (0, 1):
        B = SL[s]; P = PS[s]
        ctx.wait(nc.tensor, sv[s])
        ctx.inc(dmm(P["V"], B["Xb"][cur][:], B["vbeta"][:]), sm[s], 1)
    for s in (0, 1):
        B = SL[s]; P = PS[s]
        ctx.wait(nc.scalar, sm[s])
        ctx.inc(S.copy(B["vnew"][:], P["V"]), sa[s], 1)
    for s in (0, 1):
        B = SL[s]; P = PS[s]
        ctx.wait(nc.tensor, sa[s])
        ctx.inc(dmm(P["K"], B["kbeg"][:], B["Xb"][cur][:]), sm[s], 1)
    for s in (0, 1):
        B = SL[s]; P = PS[s]
        ctx.wait(nc.scalar, sm[s])
        ctx.inc(S.copy(B["kcum"][:], P["K"]), sa[s], 1)
    for s in (0, 1):
        B = SL[s]; P = PS[s]
        ctx.inc(dmm(P["qB"], ktiles["ones_row_b"][0:1, :], B["r3e"][:]), sm[s], 1)
    for s in (0, 1):
        B = SL[s]; P = PS[s]
        ctx.wait(nc.vector, sm[s])
        ctx.inc(V.tensor_tensor(B["qeb"][:], qfv[s], P["qB"], ALU.mult), sv[s], 1)
    for s in (0, 1):
        hh = heads[s]; B = SL[s]; P = PS[s]
        ctx.wait(nc.tensor, sa[s])
        ctx.inc(dmm(P["V"], B["kcum"][:], Sbf[hh][:]), sm[s], 1)
    for s in (0, 1):
        B = SL[s]; P = PS[s]
        ctx.wait(nc.vector, sm[s])
        ctx.inc(V.tensor_tensor(B["vnb"][:], B["vnew"][:], P["V"], ALU.subtract),
                sv[s], 1)
    for s in (0, 1):
        hh = heads[s]; B = SL[s]; P = PS[s]
        ctx.wait(nc.tensor, sv[s])
        dmm(P["O"], Sbf[hh][:], B["qeb"][:], start=True, stop=False)
        ctx.inc(dmm(P["O"], B["vnb"][:], B["attnT"][:], start=False, stop=True),
                sm[s], 1)
    for s in (0, 1):
        hh = heads[s]; B = SL[s]
        V.tensor_tensor(B["cta"][:], L["crB"][:, hh:hh + 1], B["lnrkc"][:], ALU.add)
        V.tensor_copy(B["eglc"][0:64, :], L["cglB"][:, hh:hh + 1])
        ctx.inc(V.tensor_copy(B["eglc"][64:128, :], L["cglB"][:, hh:hh + 1]),
                sv[s], 1)
    for s in (0, 1):
        B = SL[s]
        ctx.wait(nc.scalar, sv[s])
        S.activation(B["ctb"][:], B["cta"][:], AF.Exp)
        ctx.inc(S.activation(B["eglc"][:], B["eglc"][:], AF.Exp), sa[s], 1)
    for s in (0, 1):
        B = SL[s]
        ctx.wait(nc.vector, sa[s])
        ctx.inc(V.tensor_scalar(B["kglb"][:], B["ktm"][:], B["ctb"][:],
                                None, ALU.mult), sv[s], 1)
    for s in (0, 1):
        B = SL[s]; P = PS[s]
        ctx.wait(nc.tensor, sv[s])
        ctx.inc(dmm(P["S"], B["kglb"][:], B["vnb"][:]), sm[s], 1)
    for s in (0, 1):
        hh = heads[s]; B = SL[s]; P = PS[s]
        ctx.wait(nc.vector, sm[s])
        ctx.inc(V.scalar_tensor_tensor(Sst[hh][:], Sst[hh][:], B["eglc"][:],
                                       P["S"], ALU.mult, ALU.add), sv[s], 1)
    for s in (0, 1):
        hh = heads[s]
        ctx.wait(nc.scalar, sv[s])
        ctx.inc(S.copy(Sbf[hh][:], Sst[hh][:]), sa[s], 1)
    for s in (0, 1):
        B = SL[s]; P = PS[s]
        ctx.wait(nc.scalar, sm[s])
        ctx.inc(S.activation(B["sqo"][:], P["O"], AF.Square), sa[s], 1)
    for s in (0, 1):
        B = SL[s]; P = PS[s]
        ctx.wait(nc.tensor, sa[s])
        ctx.inc(dmm(P["ss"], ktiles["ones_col_b"][:], B["sqo"][:]), sm[s], 1)
    for s in (0, 1):
        B = SL[s]; P = PS[s]
        ctx.wait(nc.scalar, sm[s])
        S.activation(B["rsr"][:], P["ss"], AF.Ln, bias=1e-6, scale=1.0 / 128.0)
        ctx.inc(S.activation(B["rsr"][:], B["rsr"][:], AF.Exp, scale=-0.5), sa[s], 1)
    for s in (0, 1):
        B = SL[s]
        ctx.wait(nc.vector, sa[s])
        ctx.inc(V.tensor_copy(B["rsb"][:], B["rsr"][:]), sv[s], 1)
    for s in (0, 1):
        B = SL[s]; P = PS[s]
        ctx.wait(nc.tensor, sv[s])
        ctx.inc(dmm(P["K"], ktiles["ones_row_b"][0:1, :], B["rsb"][:]), sm[s], 1)
    for s in (0, 1):
        hh = heads[s]; B = SL[s]; P = PS[s]
        szs = L["szfm"][:, hh * 256 + toff:hh * 256 + toff + 64]
        ctx.wait(nc.vector, sm[s])
        if ctx.val("do_mm"):
            ctx.wait(nc.vector, "do_mm")
        V.tensor_tensor(B["og1"][:], P["O"], szs, ALU.mult)
        ctx.inc(V.tensor_tensor(ogbuf[:, hh * 128 + c64 * 64:hh * 128 + c64 * 64 + 64],
                                B["og1"][:], P["K"], ALU.mult), sv[s], 1)


def build_program(phases):
    nc = bass.Bass("TRN2", target_bir_lowering=False, debug=False)
    x_in = nc.dram_tensor("x_in", [NTOK, D], F32, kind="ExternalInput").ap()
    x_out = nc.dram_tensor("x_out", [NTOK, D], F32, kind="ExternalOutput").ap()
    w = {}
    wdefs = {
        "l1f_w1": [4096, D], "l1f_b1": [4096], "l1f_w2": [D, 4096], "l1f_b2": [D],
        "l0g_A": [6, 100, 16], "l0g_B": [6, 16, D], "l0g_w_gate": [D, 6],
        "l0g_wt_w": [D, D], "l0g_wt_b": [D], "l0g_wd_w": [D, D], "l0g_wd_b": [D],
        "l0g_scale": [1],
        "l1a_w_attn": [D, 3072], "l1a_b_attn": [3072],
        "l1a_w_proj": [D, D], "l1a_b_proj": [D],
        "l0a_W_qkvz": [D, 4096], "l0a_W_ba": [D, 16], "l0a_conv_w": [3072, 1, 4],
        "l0a_dt_bias": [8], "l0a_A_log": [8], "l0a_norm_weight": [128],
        "l0a_W_out": [D, D],
    }
    need = set()
    for phname in phases:
        if phname == "l1_ffn":
            need |= {"l1f_w1", "l1f_b1", "l1f_w2", "l1f_b2"}
        if phname == "l0_ffn":
            need |= {"l0g_A", "l0g_B", "l0g_w_gate", "l0g_wt_w", "l0g_wt_b",
                     "l0g_wd_w", "l0g_wd_b", "l0g_scale"}
        if phname == "l1_attn":
            need |= {"l1a_w_attn", "l1a_b_attn", "l1a_w_proj", "l1a_b_proj"}
        if phname == "l0_attn":
            need |= {"l0a_W_qkvz", "l0a_W_ba", "l0a_conv_w", "l0a_dt_bias",
                     "l0a_A_log", "l0a_norm_weight", "l0a_W_out"}
    for k in sorted(need):
        w[k] = nc.dram_tensor(k, wdefs[k], F32, kind="ExternalInput").ap()

    es = ExitStack()
    with es:
        ctx = Ctx(nc, es)
        ctx.sem("g_bar_s")
        ctx.sem("g_bar_pe")
        with nc.body("main", valid_engines=mybir.ALL_ENGINES):
            ktiles = declare_and_load_consts(ctx)
            sh = alloc_shared(ctx)
            sh["f_raw"] = nc.dram_tensor("f_raw", [NTOK, D], F32, kind="Internal").ap()
            x_mid = [None, None]
            if len(phases) > 1:
                x_mid = [nc.dram_tensor(f"x_mid{i}", [NTOK, D], F32,
                                        kind="Internal").ap() for i in range(2)]
            for eng in (nc.sync, nc.vector, nc.scalar, nc.tensor, nc.gpsimd):
                ctx.wait(eng, "c_ld")
            cur_in = x_in
            for j, phname in enumerate(phases):
                is_last = (j == len(phases) - 1)
                dst = x_out if is_last else x_mid[j % 2]
                with ExitStack() as phase_es:
                    pctx = Ctx(nc, phase_es, parent=ctx)
                    if phname == "l1_ffn":
                        emit_l1_ffn(pctx, ktiles, sh, cur_in, dst, w)
                    elif phname == "l0_ffn":
                        emit_l0_ffn(pctx, ktiles, sh, cur_in, dst, w)
                    elif phname == "l1_attn":
                        emit_l1_attn(pctx, ktiles, sh, cur_in, dst, w)
                    elif phname == "l0_attn":
                        emit_l0_attn(pctx, ktiles, sh, cur_in, dst, w)
                    else:
                        raise ValueError(phname)
                barrier(ctx, f"ph{j}")
                if not is_last:
                    pctx.teardown_sems()
                cur_in = dst
    return nc, sorted(need)


def kernel_phases(phases, **inputs):
    x = np.ascontiguousarray(np.asarray(inputs["L0_attn_x"], np.float32))
    nc, wnames = build_program(phases)
    wmap = {
        "l1f_w1": inputs["L1_ffn_w1"], "l1f_b1": inputs["L1_ffn_b1"],
        "l1f_w2": inputs["L1_ffn_w2"], "l1f_b2": inputs["L1_ffn_b2"],
        "l0g_A": inputs["L0_ffn_A"], "l0g_B": inputs["L0_ffn_B"],
        "l0g_w_gate": inputs["L0_ffn_w_gate"], "l0g_wt_w": inputs["L0_ffn_wt_w"],
        "l0g_wt_b": inputs["L0_ffn_wt_b"], "l0g_wd_w": inputs["L0_ffn_wd_w"],
        "l0g_wd_b": inputs["L0_ffn_wd_b"], "l0g_scale": inputs["L0_ffn_scale"],
        "l1a_w_attn": inputs["L1_attn_w_attn"], "l1a_b_attn": inputs["L1_attn_b_attn"],
        "l1a_w_proj": inputs["L1_attn_w_proj"], "l1a_b_proj": inputs["L1_attn_b_proj"],
        "l0a_W_qkvz": inputs["L0_attn_W_qkvz"], "l0a_W_ba": inputs["L0_attn_W_ba"],
        "l0a_conv_w": inputs["L0_attn_conv_w"], "l0a_dt_bias": inputs["L0_attn_dt_bias"],
        "l0a_A_log": inputs["L0_attn_A_log"], "l0a_norm_weight": inputs["L0_attn_norm_weight"],
        "l0a_W_out": inputs["L0_attn_W_out"],
    }
    cmap = const_inmap()
    in_maps = []
    for c in range(NCORES):
        m = dict(cmap)
        for k in wnames:
            m[k] = np.ascontiguousarray(np.asarray(wmap[k], np.float32))
        m["x_in"] = np.ascontiguousarray(x[c * BPC:(c + 1) * BPC].reshape(NTOK, D))
        in_maps.append(m)
    res = run_bass_kernel_spmd(nc, in_maps, core_ids=list(range(NCORES)))
    out = np.concatenate([r["x_out"].reshape(BPC, T, D) for r in res.results], axis=0)
    return out


def kernel(**inputs):
    return kernel_phases(["l0_attn", "l0_ffn", "l1_attn", "l1_ffn"], **inputs)


# version 04
# speedup vs baseline: 1.1992x; 1.1992x over previous
import numpy as np
import ml_dtypes
from contextlib import ExitStack
import concourse.bass as bass
import concourse.mybir as mybir
from concourse.bass_utils import run_bass_kernel_spmd

F32 = mybir.dt.float32
BF16 = mybir.dt.bfloat16
AF = mybir.ActivationFunctionType
ALU = mybir.AluOpType
AX = mybir.AxisListType

DBG = {}

NCORES = 8
BPC = 2
T = 2048
D = 1024
NTOK = BPC * T
NT = NTOK // 128
LN_EPS = 1e-5


class EngProxy:

    def __init__(self, ctx, eng, key):
        self.ctx = ctx
        self.eng = eng
        self.key = key

    def __getattr__(self, name):
        fn = getattr(self.eng, name)

        def wrapped(*a, **kw):
            ctx = self.ctx
            ctx.finalize_pending(self.key)
            instr = fn(*a, **kw)
            last = ctx.last.get(self.key)
            if last is not None:
                instr._wait_ge(ctx.sems[last[0]][0], last[1])
            ctx.pending[self.key] = instr
            return instr
        return wrapped


class Ctx:

    ENG_KEYS = ("dve", "act", "pool")

    def __init__(self, nc, es, parent=None):
        self.nc = nc
        self.es = es
        self.handles = []
        self.names = []
        if parent is None:
            self.sems = {}
            self.last = {}
            self.pending = {}
            for k in self.ENG_KEYS:
                self.sem("q_" + k)
        else:
            self.sems = parent.sems
            self.last = parent.last
            self.pending = parent.pending
        self.V = EngProxy(self, nc.vector, "dve")
        self.S = EngProxy(self, nc.scalar, "act")
        self.G = EngProxy(self, nc.gpsimd, "pool")

    def sem(self, name):
        assert name not in self.sems, name
        h = self.nc.alloc_semaphore(name=name)
        self.handles.append(h)
        self.names.append(name)
        self.sems[name] = [h, 0]
        return h

    def teardown_sems(self):
        if not self.handles:
            return
        self.nc.clear_and_free_semaphores(self.handles)
        self.nc.all_engine_barrier()
        mine = set(self.names)
        for k in list(self.last.keys()):
            v = self.last.get(k)
            if v is not None and v[0] in mine:
                self.last[k] = None
        for nm_ in self.names:
            del self.sems[nm_]
        self.handles = []
        self.names = []

    def finalize_pending(self, key):
        instr = self.pending.get(key)
        if instr is not None:
            q = "q_" + key
            s = self.sems[q]
            instr.then_inc(s[0], 1)
            s[1] += 1
            self.last[key] = (q, s[1])
            self.pending[key] = None

    def finalize_all(self):
        for k in self.ENG_KEYS:
            self.finalize_pending(k)

    def inc(self, instr, name, k=1):
        s = self.sems[name]
        instr.then_inc(s[0], k)
        s[1] += k
        for key, pend in self.pending.items():
            if pend is instr:
                self.last[key] = (name, s[1])
                self.pending[key] = None
                break
        return instr

    def val(self, name):
        return self.sems[name][1]

    def wait(self, eng, name, value=None):
        if isinstance(eng, EngProxy):
            eng = eng.eng
        s = self.sems[name]
        v = s[1] if value is None else value
        if v > 0:
            eng.wait_ge(s[0], v)

    def tap(self, name, ap, wait_sems=()):
        nc = self.nc
        shape = list(ap.shape)
        out = nc.dram_tensor("dbg_" + name, shape, ap.dtype, kind="ExternalOutput").ap()
        for s in wait_sems:
            self.wait(nc.gpsimd, s)
        if "dbg_st" not in self.sems:
            self.sem("dbg_st")
        self.wait(nc.gpsimd, "dbg_st")
        self.inc(self.nc.gpsimd.dma_start(out=out, in_=ap), "dbg_st", 16)

    def sb(self, name, shape, dtype):
        return self.es.enter_context(self.nc.sbuf_tensor(name, shape, dtype))

    def ps(self, name, shape, dtype=F32):
        return self.es.enter_context(self.nc.psum_tensor(name, shape, dtype))


def barrier(ctx, name):
    nc = ctx.nc
    ctx.finalize_all()
    if "g_bar_s" not in ctx.sems:
        ctx.sem("g_bar_s")
        ctx.sem("g_bar_pe")
    ctx.inc(nc.tensor.nop(), "g_bar_pe", 1)
    nc.tensor.wait_ge(ctx.sems["g_bar_pe"][0], ctx.val("g_bar_pe"))
    for key, eng in (("dve", nc.vector), ("act", nc.scalar), ("pool", nc.gpsimd)):
        last = ctx.last.get(key)
        if last is not None:
            eng.wait_ge(ctx.sems[last[0]][0], last[1])
    engs = (nc.sync, nc.vector, nc.scalar, nc.tensor, nc.gpsimd)
    h = ctx.sems["g_bar_s"][0]
    for eng in engs:
        eng.sem_inc(h, 1)
    ctx.sems["g_bar_s"][1] += 5
    for eng in engs:
        eng.wait_ge(h, ctx.val("g_bar_s"))


def make_consts():
    c = {}
    c["ident_b"] = np.eye(128, dtype=np.float32).astype(ml_dtypes.bfloat16)
    c["ones_row_b"] = np.ones((1, 128), dtype=np.float32).astype(ml_dtypes.bfloat16)
    c["ones_row_f"] = np.ones((1, 128), dtype=np.float32)
    c["ident_f"] = np.eye(128, dtype=np.float32)
    c["ones_col_b"] = np.ones((128, 1), dtype=np.float32).astype(ml_dtypes.bfloat16)
    i64 = np.arange(64)
    ut64 = (i64[:, None] <= i64[None, :]).astype(np.float32)
    lt64s = (i64[:, None] > i64[None, :]).astype(np.float32)
    bd = lambda m: np.block([[m, np.zeros((64, 64), np.float32)],
                             [np.zeros((64, 64), np.float32), m]])
    c["bd_ut_f"] = bd(ut64)
    c["bd_lts_f"] = bd(lt64s)
    c["bd_on_f"] = bd(np.ones((64, 64), np.float32))
    c["ut64i_b"] = ut64.astype(ml_dtypes.bfloat16)
    c["lt64i_b"] = (i64[:, None] >= i64[None, :]).astype(np.float32).astype(ml_dtypes.bfloat16)
    c["lt64s_b"] = lt64s.astype(ml_dtypes.bfloat16)
    c["negI64_b"] = (-1e9 * np.eye(64, dtype=np.float32)).astype(ml_dtypes.bfloat16)
    i128 = np.arange(128)
    c["uts128_b"] = (i128[:, None] < i128[None, :]).astype(np.float32).astype(ml_dtypes.bfloat16)
    c["lts128_b"] = (i128[:, None] > i128[None, :]).astype(np.float32).astype(ml_dtypes.bfloat16)
    c["negI128_b"] = (-1e9 * np.eye(128, dtype=np.float32)).astype(ml_dtypes.bfloat16)
    c["neg1e9row_b"] = (-1e9 * np.ones((1, 512), dtype=np.float32)).astype(ml_dtypes.bfloat16)
    c["eye64x2_b"] = np.concatenate([np.eye(64), np.eye(64)], 0).astype(np.float32).astype(ml_dtypes.bfloat16)
    c["eye64_b"] = np.eye(64, dtype=np.float32).astype(ml_dtypes.bfloat16)
    return c


FLOAT_CONSTS = [1e-5, 1e-6, 1.0, 1.1920929e-07, -2.4260151319598084]


CONST_NP = make_consts()
CONST_NP["fvals"] = np.tile(np.asarray(FLOAT_CONSTS, np.float32), (128, 1))


def declare_and_load_consts(ctx):
    nc = ctx.nc
    tiles = {}
    ctx.sem("c_ld")
    for k, arr in CONST_NP.items():
        dt = BF16 if k.endswith("_b") else F32
        ap = nc.dram_tensor("c_" + k, list(arr.shape), dt, kind="ExternalInput").ap()
        tile = ctx.sb("k_" + k, list(arr.shape), dt)
        ctx.inc(nc.sync.dma_start(out=tile[:], in_=ap), "c_ld", 16)
        tiles[k] = tile
    for j, v in enumerate(FLOAT_CONSTS):
        nc.const_aps.register(F32, v, tiles["fvals"][:, j:j + 1])
    return tiles


def const_inmap():
    return {"c_" + k: np.asarray(v) for k, v in CONST_NP.items()}


def emit_ln_stats(ctx, sh, x_view, mean_t, rstd_t, tag):
    nc = ctx.nc
    ctx.sem(tag + "ld0"); ctx.sem(tag + "ld1")
    ctx.sem(tag + "rdA")
    ctx.sem(tag + "rdD")
    ctx.sem(tag + "finD"); ctx.sem(tag + "finA")
    xa = sh["stA"], sh["stB"]
    nsum, ssq, scr = sh["nsum"], sh["ssq"], sh["scr"]
    for i in range(NT):
        p = i % 2
        if i >= 2:
            ctx.wait(nc.sync, tag + "rdA", i - 1)
            ctx.wait(nc.sync, tag + "rdD", i - 1)
        ctx.inc(nc.sync.dma_start(out=xa[p][:], in_=x_view[i * 128:(i + 1) * 128, :]),
                tag + "ld%d" % p, 16)
        ctx.wait(nc.scalar, tag + "ld%d" % p)
        dump = scr if p == 0 else sh["scr2"]
        ctx.inc(ctx.S.activation(dump[:], xa[p][:], AF.Square,
                                     accum_out=ssq[:, i:i + 1]), tag + "rdA", 1)
        ctx.wait(nc.vector, tag + "ld%d" % p)
        ctx.inc(ctx.V.reduce_sum(nsum[:, i:i + 1], xa[p][:], axis=AX.X, negate=True),
                tag + "rdD", 1)
    ctx.wait(nc.vector, tag + "rdD")
    ctx.wait(nc.vector, tag + "rdA")
    ctx.V.tensor_scalar(mean_t[:], nsum[:], -1.0 / D, None, ALU.mult)
    ctx.V.tensor_mul(scr[:, 0:NT], mean_t[:], mean_t[:])
    ctx.inc(ctx.V.scalar_tensor_tensor(scr[:, NT:2 * NT], ssq[:], 1.0 / D,
                                           scr[:, 0:NT], ALU.mult, ALU.subtract), tag + "finD", 1)
    ctx.wait(nc.scalar, tag + "finD")
    ctx.S.activation(scr[:, 0:NT], scr[:, NT:2 * NT], AF.Ln, bias=float(LN_EPS))
    ctx.inc(ctx.S.activation(rstd_t[:], scr[:, 0:NT], AF.Exp, scale=-0.5), tag + "finA", 1)


def alloc_shared(ctx):
    sh = {}
    sh["stA"] = ctx.sb("sh_stA", [128, D], F32)
    sh["stB"] = ctx.sb("sh_stB", [128, D], F32)
    sh["scr"] = ctx.sb("sh_scr", [128, D], F32)
    sh["scr2"] = ctx.sb("sh_scr2", [128, D], F32)
    sh["nsum"] = ctx.sb("sh_nsum", [128, NT], F32)
    sh["ssq"] = ctx.sb("sh_ssq", [128, NT], F32)
    return sh



def emit_stats_finish(ctx, sh, mean_t, rstd_t, tag):
    nc = ctx.nc
    ctx.sem(tag + "vD"); ctx.sem(tag + "vA")
    ctx.wait(nc.vector, tag + "fsA")
    ctx.wait(nc.vector, tag + "fsD")
    ctx.V.tensor_scalar(mean_t[:], sh["nsum"][:], -1.0 / D, None, ALU.mult)
    ctx.V.tensor_mul(sh["scr"][:, 0:NT], mean_t[:], mean_t[:])
    ctx.inc(ctx.V.scalar_tensor_tensor(sh["scr"][:, NT:2 * NT], sh["ssq"][:], 1.0 / D,
                                       sh["scr"][:, 0:NT], ALU.mult, ALU.subtract), tag + "vD", 1)
    ctx.wait(nc.scalar, tag + "vD")
    ctx.S.activation(sh["scr"][:, 0:NT], sh["scr"][:, NT:2 * NT], AF.Ln, bias=float(LN_EPS))
    ctx.inc(ctx.S.activation(rstd_t[:], sh["scr"][:, 0:NT], AF.Exp, scale=-0.5), tag + "vA", 1)


def emit_residual_pass(ctx, sh, x_in, f_raw, x_out, mean_f, rstd_f, tag, pre_tag):
    nc = ctx.nc
    ctx.sem(tag + "lx0"); ctx.sem(tag + "lx1"); ctx.sem(tag + "lf0"); ctx.sem(tag + "lf1")
    ctx.sem(tag + "res"); ctx.sem(tag + "st0"); ctx.sem(tag + "st1")
    xc = [sh["stA"], sh["stB"]]
    fr = [ctx.sb(tag + "_fr0", [128, D], F32), ctx.sb(tag + "_fr1", [128, D], F32)]
    xnew = [ctx.sb(tag + "_xn0", [128, D], F32), ctx.sb(tag + "_xn1", [128, D], F32)]
    for i in range(NT):
        p = i % 2
        if i == 0 and pre_tag is not None:
            ctx.wait(nc.sync, pre_tag + "rdA")
            ctx.wait(nc.sync, pre_tag + "rdD")
        if i >= 2:
            ctx.wait(nc.sync, tag + "res", ctx.val(tag + "res") - 1)
        ctx.inc(nc.sync.dma_start(out=xc[p][:], in_=x_in[i * 128:(i + 1) * 128, :]),
                tag + "lx%d" % p, 16)
        if i >= 2:
            ctx.wait(nc.gpsimd, tag + "res", ctx.val(tag + "res") - 1)
        ctx.inc(nc.gpsimd.dma_start(out=fr[p][:], in_=f_raw[i * 128:(i + 1) * 128, :]),
                tag + "lf%d" % p, 16)
        ctx.wait(nc.vector, tag + "lx%d" % p)
        ctx.wait(nc.vector, tag + "lf%d" % p)
        if i >= 2:
            ctx.wait(nc.vector, tag + "st%d" % p)
        ctx.V.tensor_scalar(xnew[p][:], fr[p][:], mean_f[:, i:i + 1], rstd_f[:, i:i + 1],
                            ALU.subtract, ALU.mult)
        ctx.inc(ctx.V.tensor_add(xnew[p][:], xnew[p][:], xc[p][:]), tag + "res", 1)
        ctx.wait(nc.gpsimd, tag + "res")
        ctx.inc(nc.gpsimd.dma_start(out=x_out[i * 128:(i + 1) * 128, :], in_=xnew[p][:]),
                tag + "st%d" % p, 16)
    ctx.wait(nc.gpsimd, tag + "st0")
    ctx.wait(nc.gpsimd, tag + "st1")


def emit_l1_ffn(ctx, ktiles, sh, x_in, x_out, w):
    nc = ctx.nc
    es = ctx.es
    H = 4096

    mean_a = ctx.sb("f_meanA", [128, NT], F32)
    rstd_a = ctx.sb("f_rstdA", [128, NT], F32)
    mean_f = ctx.sb("f_meanF", [128, NT], F32)
    rstd_f = ctx.sb("f_rstdF", [128, NT], F32)
    f_raw = sh["f_raw"]

    emit_ln_stats(ctx, sh, x_in, mean_a, rstd_a, "fA")

    w1T = ctx.sb("f_w1T", [128, 8 * H], BF16)
    w2T = ctx.sb("f_w2T", [128, 32 * D], BF16)
    b1r = ctx.sb("f_b1r", [1, H], BF16)
    b2r = ctx.sb("f_b2r", [1, D], BF16)
    prep_es = ExitStack()
    wbuf = prep_es.enter_context(nc.sbuf_tensor("f_wb", [128, H], F32))
    sgn = prep_es.enter_context(nc.sbuf_tensor("f_sgn", [128, H], BF16))
    alpha = prep_es.enter_context(nc.sbuf_tensor("f_alpha", [128, 1], F32))
    wbin = prep_es.enter_context(nc.sbuf_tensor("f_wbin", [128, H], BF16))
    ptr = ctx.ps("f_ptr", [128, 1024], BF16)

    ctx.sem("fp_ld")
    ctx.sem("fp_sgn")
    ctx.sem("fp_bin")
    ctx.sem("fp_trs")
    ctx.sem("fp_ev")
    ctx.inc(nc.gpsimd.dma_start(out=b1r[:], in_=w["l1f_b1"].unsqueeze(0)), "fp_ld", 16)
    ctx.inc(nc.gpsimd.dma_start(out=b2r[:], in_=w["l1f_b2"].unsqueeze(0)), "fp_ld", 16)

    def prep_weight(wap, n_tiles, fd, dst, dst_d):
        for i in range(n_tiles):
            if i >= 1 or ctx.val("fp_bin") > 0:
                ctx.wait(nc.gpsimd, "fp_bin", ctx.val("fp_bin"))
            ctx.inc(nc.gpsimd.dma_start(out=wbuf[:, 0:fd],
                                        in_=wap[i * 128:(i + 1) * 128, :]), "fp_ld", 16)
            ctx.wait(nc.scalar, "fp_ld")
            if i >= 1 or ctx.val("fp_bin") > 0:
                ctx.wait(nc.scalar, "fp_bin", ctx.val("fp_bin"))
            ctx.inc(ctx.S.activation(sgn[:, 0:fd], wbuf[:, 0:fd], AF.Sign),
                    "fp_sgn", 1)
            ctx.wait(nc.vector, "fp_ld")
            ctx.V.reduce_sum(alpha[:], wbuf[:, 0:fd], axis=AX.X,
                                 apply_absolute_value=True)
            ctx.wait(nc.vector, "fp_sgn")
            if i >= 1 or ctx.val("fp_trs") > 0:
                ctx.wait(nc.vector, "fp_trs", ctx.val("fp_trs"))
            ctx.inc(ctx.V.tensor_scalar(wbin[:, 0:fd], sgn[:, 0:fd], alpha[:],
                                            1.0 / fd, ALU.mult, ALU.mult), "fp_bin", 1)
            nchunk = fd // 128
            ctx.wait(nc.tensor, "fp_bin")
            dview = dst[:, :].rearrange("p (cc x) -> p cc x", x=dst_d)
            for c in range(nchunk):
                if c % 8 == 0 and ctx.val("fp_ev") > 0:
                    ctx.wait(nc.tensor, "fp_ev")
                tin = nc.tensor.transpose(ptr[:, (c % 8) * 128:(c % 8) * 128 + 128],
                                          wbin[:, c * 128:c * 128 + 128], ktiles["ident_b"][:])
                if c % 8 == 7 or c == nchunk - 1:
                    ctx.inc(tin, "fp_trs", 1)
                    ctx.wait(nc.vector, "fp_trs")
                    lo = (c // 8) * 8
                    cnt = c - lo + 1
                    ctx.inc(ctx.V.tensor_copy(
                        dview[:, lo:lo + cnt, i * 128:(i + 1) * 128],
                        ptr[:, 0:cnt * 128].rearrange("p (cc x) -> p cc x", x=128)),
                        "fp_ev", 1)

    prep_weight(w["l1f_w1"], 32, D, w1T, H)
    prep_weight(w["l1f_w2"], 8, H, w2T, D)
    prep_es.close()

    passb_es = ExitStack()
    _s = lambda n, sh_, dt_: passb_es.enter_context(nc.sbuf_tensor(n, sh_, dt_))
    xb = [_s("f_xb0", [128, D], F32), _s("f_xb1", [128, D], F32)]
    xln = [_s("f_xln0", [128, D], BF16), _s("f_xln1", [128, D], BF16)]
    xfm = [_s("f_xfm0", [128, D], BF16), _s("f_xfm1", [128, D], BF16)]
    hbf = _s("f_hbf", [128, H], BF16)
    hfm = _s("f_hfm", [128, H], BF16)
    fout = [_s("f_fo0", [128, D], F32), _s("f_fo1", [128, D], F32)]
    ph = ctx.ps("f_ph", [128, 2048], F32)
    po = ctx.ps("f_po", [128, 1024], F32)

    ctx.sem("fb_ld0"); ctx.sem("fb_ld1")
    bln = ctx.sem("fb_ln"); btrx = ctx.sem("fb_trx")
    bfm = ctx.sem("fb_fm"); bh = ctx.sem("fb_h"); bg = ctx.sem("fb_g")
    btrh = ctx.sem("fb_trh"); bhfm = ctx.sem("fb_hfm"); bom = ctx.sem("fb_om")
    boe = ctx.sem("fb_oe"); ctx.sem("fb_fsA"); ctx.sem("fb_fsD")
    ctx.sem("fb_st0"); ctx.sem("fb_st1")

    for i in range(NT):
        p = i % 2
        if i == 0:
            ctx.wait(nc.sync, "fp_ev")
        if i >= 2:
            ctx.wait(nc.sync, "fb_ln", ctx.val("fb_ln") - 1)
        ctx.inc(nc.sync.dma_start(out=xb[p][:], in_=x_in[i * 128:(i + 1) * 128, :]),
                "fb_ld%d" % p, 16)
        ctx.wait(nc.vector, "fb_ld%d" % p)
        if i == 0:
            ctx.wait(nc.vector, "fAfinA", 1)
        if i >= 2:
            ctx.wait(nc.vector, "fb_trx", ctx.val("fb_trx") - 1)
        ctx.inc(ctx.V.tensor_scalar(xln[p][:], xb[p][:], mean_a[:, i:i + 1],
                                        rstd_a[:, i:i + 1], ALU.subtract, ALU.mult), "fb_ln", 1)
        ctx.wait(nc.tensor, "fb_ln")
        if i == 0:
            ctx.wait(nc.tensor, "fp_ev")
        ctx.wait(nc.tensor, "fb_hfm")
        for c in range(8):
            tin = nc.tensor.transpose(ptr[:, c * 128:c * 128 + 128],
                                      xln[p][:, c * 128:c * 128 + 128], ktiles["ident_b"][:])
        ctx.inc(tin, "fb_trx", 1)
        ctx.wait(nc.vector, "fb_trx")
        if i >= 2:
            ctx.wait(nc.vector, "fb_h", ctx.val("fb_h") - 1)
        ctx.inc(ctx.V.tensor_copy(xfm[p][:], ptr[:]), "fb_fm", 1)
        ctx.wait(nc.tensor, "fb_fm")
        if i == 0:
            ctx.wait(nc.tensor, "fp_ld")
        if i >= 2:
            ctx.wait(nc.tensor, "fb_trh", ctx.val("fb_trh") - 4)
        for half in range(2):
            if not (i == 0 and half == 0):
                ctx.wait(nc.tensor, "fb_g", ctx.val("fb_g"))
            for b in range(4):
                col0 = half * 2048 + b * 512
                for kc in range(8):
                    nc.tensor.matmul(ph[:, b * 512:b * 512 + 512], xfm[p][:, kc * 128:kc * 128 + 128],
                                     w1T[:, kc * H + col0:kc * H + col0 + 512],
                                     start=(kc == 0), stop=False)
                tin = nc.tensor.matmul(ph[:, b * 512:b * 512 + 512], ktiles["ones_row_b"][:],
                                       b1r[:, col0:col0 + 512], start=False, stop=True)
            ctx.inc(tin, "fb_h", 1)
            ctx.wait(nc.scalar, "fb_h")
            if i >= 1 and half == 0:
                ctx.wait(nc.scalar, "fb_trh", ctx.val("fb_trh"))
            ctx.inc(ctx.S.activation(hbf[:, half * 2048:half * 2048 + 2048], ph[:],
                                         AF.Gelu), "fb_g", 1)
        ctx.wait(nc.tensor, "fb_g")
        for grp in range(4):
            ctx.wait(nc.tensor, "fb_fm" if (i == 0 and grp == 0) else "fb_hfm")
            for c in range(8):
                cc = grp * 8 + c
                tin = nc.tensor.transpose(ptr[:, c * 128:c * 128 + 128],
                                          hbf[:, cc * 128:cc * 128 + 128], ktiles["ident_b"][:])
            ctx.inc(tin, "fb_trh", 1)
            ctx.wait(nc.vector, "fb_trh")
            if i >= 1:
                ctx.wait(nc.vector, "fb_om", ctx.val("fb_om"))
            ctx.inc(ctx.V.tensor_copy(hfm[:, grp * 1024:grp * 1024 + 1024], ptr[:]),
                    "fb_hfm", 1)
        ctx.wait(nc.tensor, "fb_hfm")
        if i >= 1:
            ctx.wait(nc.tensor, "fb_oe", ctx.val("fb_oe"))
        for b in range(2):
            for kc in range(32):
                nc.tensor.matmul(po[:, b * 512:b * 512 + 512], hfm[:, kc * 128:kc * 128 + 128],
                                 w2T[:, kc * D + b * 512:kc * D + b * 512 + 512],
                                 start=(kc == 0), stop=False)
            tin = nc.tensor.matmul(po[:, b * 512:b * 512 + 512], ktiles["ones_row_b"][:],
                                   b2r[:, b * 512:b * 512 + 512], start=False, stop=True)
        ctx.inc(tin, "fb_om", 1)
        ctx.wait(nc.vector, "fb_om")
        if i >= 2:
            ctx.wait(nc.vector, "fb_st%d" % p)
        ctx.inc(ctx.V.tensor_copy(fout[p][:], po[:]), "fb_oe", 1)
        ctx.wait(nc.scalar, "fb_oe")
        ctx.inc(ctx.S.activation((sh["scr"] if p == 0 else sh["scr2"])[:], fout[p][:],
                                     AF.Square, accum_out=sh["ssq"][:, i:i + 1]), "fb_fsA", 1)
        ctx.wait(nc.vector, "fb_oe")
        ctx.inc(ctx.V.reduce_sum(sh["nsum"][:, i:i + 1], fout[p][:], axis=AX.X,
                                     negate=True), "fb_fsD", 1)
        ctx.wait(nc.gpsimd, "fb_fsA")
        ctx.wait(nc.gpsimd, "fb_fsD")
        ctx.inc(nc.gpsimd.dma_start(out=f_raw[i * 128:(i + 1) * 128, :], in_=fout[p][:]),
                "fb_st%d" % p, 16)

    emit_stats_finish(ctx, sh, mean_f, rstd_f, "fb_")
    passb_es.close()
    ctx.wait(nc.gpsimd, "fb_st0")
    ctx.wait(nc.gpsimd, "fb_st1")
    barrier(ctx, "f_bar")
    emit_residual_pass(ctx, sh, x_in, f_raw, x_out, mean_f, rstd_f, "fc_", "fA")



def emit_l0_ffn(ctx, ktiles, sh, x_in, x_out, w):
    nc = ctx.nc
    E, L, R = 6, 100, 16

    mean_a = ctx.sb("g_meanA", [128, NT], F32)
    rstd_a = ctx.sb("g_rstdA", [128, NT], F32)
    mean_f = ctx.sb("g_meanF", [128, NT], F32)
    rstd_f = ctx.sb("g_rstdF", [128, NT], F32)
    f_raw = sh["f_raw"]

    emit_ln_stats(ctx, sh, x_in, mean_a, rstd_a, "gA")

    tokfm = ctx.sb("g_tokfm", [128, 8 * 600], BF16)
    wtT = ctx.sb("g_wtT", [128, 8 * D], BF16)
    wdT = ctx.sb("g_wdT", [128, 8 * D], BF16)
    wtb = ctx.sb("g_wtb", [1, D], BF16)
    wdb = ctx.sb("g_wdb", [1, D], BF16)
    wg = ctx.sb("g_wg", [128, 8 * 6], BF16)
    val0 = ctx.sb("g_val0", [128, E * D], BF16)
    scale128 = ctx.sb("g_s128", [128, 1], F32)
    at = ctx.sb("g_at", [16, 600], BF16)
    bt = [ctx.sb("g_bt%d" % e, [16, D], BF16) for e in range(E)]
    ptr = ctx.ps("g_ptr", [128, 1024], BF16)

    prep_es = ExitStack()
    wld = prep_es.enter_context(nc.sbuf_tensor("g_wld", [128, D], F32))
    ssb = prep_es.enter_context(nc.sbuf_tensor("g_ssb", [1, 1], F32))

    ctx.sem("gp_ld")
    ctx.sem("gp_tok")
    ctx.sem("gp_tev")
    ctx.sem("gp_trs"); ctx.sem("gp_ev")
    ctx.sem("gp_val"); ctx.sem("gp_vev")
    ctx.sem("gp_s")
    g = nc.gpsimd

    def gld(out_ap, in_ap, **kw):
        ctx.wait(nc.gpsimd, "gp_ld")
        ctx.inc(g.dma_start(out=out_ap, in_=in_ap, **kw), "gp_ld", 16)

    gld(wtb[:], w["l0g_wt_b"].unsqueeze(0))
    gld(wdb[:], w["l0g_wd_b"].unsqueeze(0))
    gld(ssb[:], w["l0g_scale"].unsqueeze(0))
    gld(wg[:].rearrange("p (a c) -> p a c", c=6),
        w["l0g_w_gate"].rearrange("(a b) c -> b a c", b=128))
    gld(at[:], w["l0g_A"].rearrange("e l r -> r (e l)"), allow_slow_non_contiguous=True)
    for e in range(E):
        gld(bt[e][:], w["l0g_B"][e])

    prep_ps = ExitStack()
    psc = prep_ps.enter_context(nc.psum_tensor("g_psc", [128, 8], F32))
    ctx.wait(nc.tensor, "gp_ld")
    tin = nc.tensor.matmul(psc[:, 0:1], ktiles["ones_row_f"][:], ssb[:], start=True, stop=True)
    ctx.inc(tin, "gp_s", 1)
    ctx.wait(nc.vector, "gp_s")
    ctx.inc(ctx.V.tensor_copy(scale128[:], psc[:, 0:1]), "gp_s", 1)

    ptok = prep_ps.enter_context(nc.psum_tensor("g_ptok", [128, 768], F32))
    for kc in range(8):
        for e in range(E):
            tin = nc.tensor.matmul(ptok[:, e * 128:e * 128 + 100], bt[e][:, kc * 128:(kc + 1) * 128],
                                   at[:, e * 100:(e + 1) * 100], start=True, stop=True)
        ctx.inc(tin, "gp_tok", 1)
        ctx.wait(nc.vector, "gp_tok")
        ctx.inc(ctx.V.tensor_copy(
            tokfm[:, kc * 600:(kc + 1) * 600].rearrange("p (e l) -> p e l", l=100),
            ptok[:].rearrange("p (e l) -> p e l", l=128)[:, :, 0:100]), "gp_tev", 1)
        if kc < 7:
            ctx.wait(nc.tensor, "gp_tev")
    ptrf = prep_ps.enter_context(nc.psum_tensor("g_ptrf", [128, 1024], F32))
    def prep_t2(wap, dst):
        for i in range(8):
            if ctx.val("gp_trs") > 0:
                ctx.wait(nc.gpsimd, "gp_trs", ctx.val("gp_trs"))
            gld(wld[:], wap[i * 128:(i + 1) * 128, :])
            ctx.wait(nc.tensor, "gp_ld")
            if ctx.val("gp_ev") > 0:
                ctx.wait(nc.tensor, "gp_ev")
            for c in range(8):
                tin = nc.tensor.transpose(ptrf[:, c * 128:c * 128 + 128],
                                          wld[:, c * 128:c * 128 + 128], ktiles["ident_f"][:])
            ctx.inc(tin, "gp_trs", 1)
            ctx.wait(nc.vector, "gp_trs")
            dv = dst[:, :].rearrange("p (cc x) -> p cc x", x=D)
            ctx.inc(ctx.V.tensor_copy(dv[:, :, i * 128:(i + 1) * 128],
                                      ptrf[:].rearrange("p (cc x) -> p cc x", x=128)), "gp_ev", 1)
    prep_t2(w["l0g_wt_w"], wtT)

    pval = prep_ps.enter_context(nc.psum_tensor("g_pval", [128, 1024], F32))
    ctx.wait(nc.tensor, "gp_ev")
    for e in range(E):
        ctx.wait(nc.tensor, "gp_tev")
        for b in range(2):
            for kc in range(8):
                nc.tensor.matmul(pval[0:100, b * 512:b * 512 + 512],
                                 tokfm[:, kc * 600 + e * 100:kc * 600 + (e + 1) * 100],
                                 wtT[:, kc * D + b * 512:kc * D + b * 512 + 512],
                                 start=(kc == 0), stop=False)
            tin = nc.tensor.matmul(pval[0:100, b * 512:b * 512 + 512],
                                   ktiles["ones_row_b"][:, 0:100],
                                   wtb[:, b * 512:b * 512 + 512], start=False, stop=True)
        ctx.inc(tin, "gp_val", 1)
        ctx.wait(nc.scalar, "gp_val")
        ctx.inc(ctx.S.copy(val0[0:100, e * D:(e + 1) * D], pval[0:100, :]), "gp_vev", 1)
        if e < E - 1:
            ctx.wait(nc.tensor, "gp_vev")
    ctx.wait(nc.vector, "gp_vev")
    ctx.inc(ctx.V.memset(val0[0:1, :], 0.0), "gp_vev", 1)

    prep_t2(w["l0g_wd_w"], wdT)
    prep_es.close()
    prep_ps.close()

    if DBG.get("l0g"):
        ctx.tap("s128", scale128[:], ["gp_s"])

    passb_es = ExitStack()
    _s = lambda n, sh_, dt_: passb_es.enter_context(nc.sbuf_tensor(n, sh_, dt_))
    xb = [_s("g_xb0", [128, D], F32), _s("g_xb1", [128, D], F32)]
    xln = [_s("g_xln0", [128, D], BF16), _s("g_xln1", [128, D], BF16)]
    xlnf = [_s("g_xlnf0", [128, D], F32), _s("g_xlnf1", [128, D], F32)]
    xfm = [_s("g_xfm0", [128, D], BF16), _s("g_xfm1", [128, D], BF16)]
    gl = _s("g_gl", [128, 8], F32)
    gmx = _s("g_gmx", [128, 1], F32)
    gates = _s("g_gates", [128, 8], F32)
    s1 = _s("g_s1", [128, 600], F32)
    ebf = _s("g_ebf", [128, 600], BF16)
    sums = _s("g_sums", [128, 8], F32)
    grow = _s("g_grow", [128, 8], F32)
    ag = _s("g_ag", [128, 600], BF16)
    atg = _s("g_atg", [128, 768], BF16)
    dsx = _s("g_dsx", [128, D], BF16)
    dsxf = _s("g_dsxf", [128, D], BF16)
    esb = _s("g_esb", [128, D], F32)
    emax = _s("g_emax", [128, D], F32)
    fout = [_s("g_fo0", [128, D], F32), _s("g_fo1", [128, D], F32)]
    pz = ctx.ps("g_pz", [128, 600], F32)
    pl = ctx.ps("g_pl", [128, 8], F32)
    pds = ctx.ps("g_pds", [128, 1024], F32)
    pdl = ctx.ps("g_pdl", [128, 1024], F32)

    ctx.sem("gb_ld0"); ctx.sem("gb_ld1")
    ctx.sem("gb_ln"); ctx.sem("gb_trx"); ctx.sem("gb_fm")
    ctx.sem("gb_z"); ctx.sem("gb_l"); ctx.sem("gb_gev"); ctx.sem("gb_sm")
    ctx.sem("gb_ta"); ctx.sem("gb_tat"); ctx.sem("gb_taev"); ctx.sem("gb_ds"); ctx.sem("gb_dsx")
    ctx.sem("gb_trd"); ctx.sem("gb_dfm"); ctx.sem("gb_dl"); ctx.sem("gb_ee"); ctx.sem("gb_eexp")
    ctx.sem("gb_foD"); ctx.sem("gb_foA"); ctx.sem("gb_fo2"); ctx.sem("gb_fsA"); ctx.sem("gb_fsD")
    ctx.sem("gb_st0"); ctx.sem("gb_st1")

    for i in range(NT):
        p = i % 2
        if i == 0:
            ctx.wait(nc.sync, "gp_trs")
            ctx.wait(nc.sync, "gp_s")
        if i >= 2:
            ctx.wait(nc.sync, "gb_ln", ctx.val("gb_ln") - 1)
        ctx.inc(nc.sync.dma_start(out=xb[p][:], in_=x_in[i * 128:(i + 1) * 128, :]),
                "gb_ld%d" % p, 16)
        ctx.wait(nc.vector, "gb_ld%d" % p)
        if i == 0:
            ctx.wait(nc.vector, "gAfinA", 1)
        if i >= 2:
            ctx.wait(nc.vector, "gb_ds", ctx.val("gb_ds") - 1)
        ctx.V.tensor_scalar(xlnf[p][:], xb[p][:], mean_a[:, i:i + 1],
                            rstd_a[:, i:i + 1], ALU.subtract, ALU.mult)
        ctx.inc(ctx.V.tensor_copy(xln[p][:], xlnf[p][:]), "gb_ln", 1)
        ctx.wait(nc.tensor, "gb_ln")
        if i == 0:
            ctx.wait(nc.tensor, "gp_vev")
            ctx.wait(nc.tensor, "gp_ev")
            ctx.wait(nc.tensor, "gp_s")
        ctx.wait(nc.tensor, "gb_dfm")
        for c in range(8):
            tin = nc.tensor.transpose(ptr[:, c * 128:c * 128 + 128],
                                      xln[p][:, c * 128:c * 128 + 128], ktiles["ident_b"][:])
        ctx.inc(tin, "gb_trx", 1)
        ctx.wait(nc.vector, "gb_trx")
        if i >= 2:
            ctx.wait(nc.vector, "gb_dl", ctx.val("gb_dl") - 1)
        ctx.inc(ctx.V.tensor_copy(xfm[p][:], ptr[:]), "gb_fm", 1)
        ctx.wait(nc.tensor, "gb_fm")
        if i >= 1:
            ctx.wait(nc.tensor, "gb_gev", ctx.val("gb_gev"))
            ctx.wait(nc.tensor, "gb_sm", ctx.val("gb_sm"))
        for kc in range(8):
            tin = nc.tensor.matmul(pl[:, 0:6], xfm[p][:, kc * 128:kc * 128 + 128],
                                   wg[:, kc * 6:kc * 6 + 6], start=(kc == 0), stop=(kc == 7))
        ctx.inc(tin, "gb_l", 1)
        for kc in range(8):
            nc.tensor.matmul(pz[:, 0:512], xfm[p][:, kc * 128:kc * 128 + 128],
                             tokfm[:, kc * 600:kc * 600 + 512], start=(kc == 0), stop=(kc == 7))
            tin = nc.tensor.matmul(pz[:, 512:600], xfm[p][:, kc * 128:kc * 128 + 128],
                                   tokfm[:, kc * 600 + 512:kc * 600 + 600],
                                   start=(kc == 0), stop=(kc == 7))
        ctx.inc(tin, "gb_z", 1)
        ctx.wait(nc.scalar, "gb_l")
        if i >= 1:
            ctx.wait(nc.scalar, "gb_sm", ctx.val("gb_sm"))
        ctx.inc(ctx.S.copy(gl[:, 0:6], pl[:, 0:6]), "gb_gev", 1)
        ctx.wait(nc.vector, "gb_gev")
        ctx.V.reduce_max(gmx[:], gl[:, 0:6], axis=AX.X)
        ctx.V.tensor_tensor(gates[:, 0:6], gl[:, 0:6], gmx[:].broadcast_to([128, 6]), ALU.is_ge)
        ctx.wait(nc.vector, "gb_z")
        ctx.V.reduce_max(sums[:, 0:6], pz[:].rearrange("p (e l) -> p e l", e=6), axis=AX.X)
        ctx.inc(ctx.V.tensor_tensor(s1[:].rearrange("p (e l) -> p e l", e=6),
                                    pz[:].rearrange("p (e l) -> p e l", e=6),
                                    sums[:, 0:6].unsqueeze(2).broadcast_to([128, 6, 100]),
                                    ALU.subtract), "gb_sm", 1)
        ctx.wait(nc.scalar, "gb_sm")
        ctx.inc(ctx.S.activation(ebf[:], s1[:], AF.Exp, scale=1.0 / 32.0), "gb_eexp", 1)
        ctx.wait(nc.vector, "gb_eexp")
        ctx.V.reduce_sum(sums[:, 0:6], ebf[:].rearrange("p (e l) -> p e l", e=6), axis=AX.X)
        ctx.V.reciprocal(grow[:, 0:6], sums[:, 0:6])
        ctx.V.tensor_mul(grow[:, 0:6], grow[:, 0:6], gates[:, 0:6])
        if i >= 1:
            ctx.wait(nc.vector, "gb_tat", ctx.val("gb_tat"))
        ctx.inc(ctx.V.tensor_tensor(ag[:].rearrange("p (e l) -> p e l", e=6),
                                    ebf[:].rearrange("p (e l) -> p e l", e=6),
                                    grow[:, 0:6].unsqueeze(2).broadcast_to([128, 6, 100]),
                                    ALU.mult), "gb_ta", 1)
        ctx.wait(nc.tensor, "gb_ta")
        ctx.wait(nc.tensor, "gb_fm")
        for e in range(E):
            tin = nc.tensor.transpose(ptr[0:100, e * 128:e * 128 + 128],
                                      ag[:, e * 100:(e + 1) * 100], ktiles["ident_b"][:])
        ctx.inc(tin, "gb_tat", 1)
        ctx.wait(nc.vector, "gb_tat")
        if i >= 1:
            ctx.wait(nc.vector, "gb_ds", ctx.val("gb_ds"))
        ctx.inc(ctx.V.tensor_copy(atg[0:100, :], ptr[0:100, 0:768]), "gb_taev", 1)
        ctx.wait(nc.tensor, "gb_taev")
        if i >= 1:
            ctx.wait(nc.tensor, "gb_dsx", ctx.val("gb_dsx"))
        for e in range(E):
            for b in range(2):
                tin = nc.tensor.matmul(pds[:, b * 512:b * 512 + 512],
                                       atg[0:100, e * 128:e * 128 + 128],
                                       val0[0:100, e * D + b * 512:e * D + b * 512 + 512],
                                       start=(e == 0), stop=(e == E - 1))
        ctx.inc(tin, "gb_ds", 1)
        ctx.wait(nc.vector, "gb_ds")
        if i >= 1:
            ctx.wait(nc.vector, "gb_trd", ctx.val("gb_trd"))
        ctx.inc(ctx.V.tensor_tensor(dsx[:], pds[:], xln[p][:], ALU.add), "gb_dsx", 1)
        ctx.wait(nc.tensor, "gb_dsx")
        ctx.wait(nc.tensor, "gb_taev")
        for c in range(8):
            tin = nc.tensor.transpose(ptr[:, c * 128:c * 128 + 128],
                                      dsx[:, c * 128:c * 128 + 128], ktiles["ident_b"][:])
        ctx.inc(tin, "gb_trd", 1)
        ctx.wait(nc.vector, "gb_trd")
        if i >= 1:
            ctx.wait(nc.vector, "gb_dl", ctx.val("gb_dl"))
        ctx.inc(ctx.V.tensor_copy(dsxf[:], ptr[:]), "gb_dfm", 1)
        ctx.wait(nc.tensor, "gb_dfm")
        if i >= 1:
            ctx.wait(nc.tensor, "gb_ee", ctx.val("gb_ee"))
        for b in range(2):
            for kc in range(8):
                nc.tensor.matmul(pdl[:, b * 512:b * 512 + 512], dsxf[:, kc * 128:kc * 128 + 128],
                                 wdT[:, kc * D + b * 512:kc * D + b * 512 + 512],
                                 start=(kc == 0), stop=False)
            tin = nc.tensor.matmul(pdl[:, b * 512:b * 512 + 512], ktiles["ones_row_b"][:],
                                   wdb[:, b * 512:b * 512 + 512], start=False, stop=True)
        ctx.inc(tin, "gb_dl", 1)
        ctx.wait(nc.scalar, "gb_dl")
        if i >= 1:
            ctx.wait(nc.scalar, "gb_fo2", ctx.val("gb_fo2"))
        ctx.inc(ctx.S.activation(esb[:], pdl[:], AF.Exp, scale=scale128[:]), "gb_ee", 1)
        ctx.wait(nc.vector, "gb_ee")
        ctx.inc(ctx.V.tensor_scalar(emax[:], esb[:], 1.1920929e-07, None, ALU.max), "gb_foD", 1)
        ctx.wait(nc.scalar, "gb_foD")
        ctx.inc(ctx.S.activation(esb[:], emax[:], AF.Ln), "gb_foA", 1)
        ctx.wait(nc.vector, "gb_foA")
        if i >= 2:
            ctx.wait(nc.vector, "gb_st%d" % p)
        ctx.inc(ctx.V.tensor_add(fout[p][:], esb[:], xlnf[p][:]), "gb_fo2", 1)
        ctx.wait(nc.scalar, "gb_fo2")
        ctx.inc(ctx.S.activation((sh["scr"] if p == 0 else sh["scr2"])[:], fout[p][:],
                                 AF.Square, accum_out=sh["ssq"][:, i:i + 1]), "gb_fsA", 1)
        ctx.inc(ctx.V.reduce_sum(sh["nsum"][:, i:i + 1], fout[p][:], axis=AX.X,
                                 negate=True), "gb_fsD", 1)
        if i == 0 and DBG.get("l0g"):
            ctx.tap("xln0", xb[p][:], ["gb_ln"])
            ctx.tap("gl0", gl[:, 0:6], ["gb_gev"])
            ctx.tap("gates0", gates[:, 0:6], ["gb_sm"])
            ctx.tap("s10", s1[:], ["gb_sm"])
            ctx.tap("esb0", esb[:], ["gb_ee"])
            ctx.tap("fout0", fout[p][:], ["gb_foA"])
        ctx.wait(nc.gpsimd, "gb_fsA")
        ctx.wait(nc.gpsimd, "gb_fsD")
        ctx.inc(nc.gpsimd.dma_start(out=f_raw[i * 128:(i + 1) * 128, :], in_=fout[p][:]),
                "gb_st%d" % p, 16)

    emit_stats_finish(ctx, sh, mean_f, rstd_f, "gb_")
    passb_es.close()
    ctx.wait(nc.gpsimd, "gb_st0")
    ctx.wait(nc.gpsimd, "gb_st1")
    barrier(ctx, "g_bar")
    emit_residual_pass(ctx, sh, x_in, f_raw, x_out, mean_f, rstd_f, "gc_", "gA")


def emit_l1_attn(ctx, ktiles, sh, x_in, x_out, w):
    nc = ctx.nc
    H, HD = 16, 64
    NTT = 16
    NIT = 5

    mean_a = ctx.sb("a_meanA", [128, NT], F32)
    rstd_a = ctx.sb("a_rstdA", [128, NT], F32)
    mean_f = ctx.sb("a_meanF", [128, NT], F32)
    rstd_f = ctx.sb("a_rstdF", [128, NT], F32)
    f_raw = sh["f_raw"]

    emit_ln_stats(ctx, sh, x_in, mean_a, rstd_a, "aA")

    batt = ctx.sb("a_batt", [1, 3072], BF16)
    ctx.sem("ap_ld")
    g = nc.gpsimd

    def ald(out_ap, in_ap):
        ctx.wait(nc.gpsimd, "ap_ld")
        ctx.inc(g.dma_start(out=out_ap, in_=in_ap), "ap_ld", 16)

    ald(batt[:], w["l1a_b_attn"].unsqueeze(0))

    Qf = ctx.sb("a_Qf", [128, 8 * T], BF16)
    Kf = ctx.sb("a_Kf", [128, 8 * T], BF16)
    Vt = ctx.sb("a_Vt", [128, NTT * D], BF16)
    mN = ctx.sb("a_mN", [128, 16], F32)
    tau = ctx.sb("a_tau", [128, 16], F32)
    taup = ctx.sb("a_taup", [128, 16], F32)
    sprev = ctx.sb("a_sprev", [128, 16], F32)
    rs = ctx.sb("a_rs", [128, 16], F32)
    kcnt = ctx.sb("a_kcnt", [128, 16], F32)
    t16a = ctx.sb("a_t16a", [128, 16], F32)
    t16b = ctx.sb("a_t16b", [128, 16], F32)
    xbq = [ctx.sb("a_xb0", [128, D], F32)]
    xbq.append(xbq[0])
    xlnq = [ctx.sb("a_xln0", [128, D], BF16)]
    xlnq.append(xlnq[0])
    xfmq = [ctx.sb("a_xfm0", [128, D], BF16)]
    xfmq.append(xfmq[0])
    fout = [ctx.sb("a_fo0", [128, D], F32), ctx.sb("a_fo1", [128, D], F32)]

    ctx.sem("aq_ld0"); ctx.sem("aq_ld1")
    ctx.sem("aq_ln"); ctx.sem("aq_trx"); ctx.sem("aq_fm")
    ctx.sem("aq_mm"); ctx.sem("aq_ev")
    ctx.sem("ah_z"); ctx.sem("ah_zev"); ctx.sem("ah_m1")
    ctx.sem("ah_zm"); ctx.sem("ah_m2"); ctx.sem("ah_it"); ctx.sem("ah_tt")
    ctx.sem("ah_tev"); ctx.sem("ah_z2"); ctx.sem("ah_pr"); ctx.sem("ah_y")
    ctx.sem("ah_yev")
    ctx.sem("aj_trs"); ctx.sem("aj_fm"); ctx.sem("aj_om"); ctx.sem("aj_oe")
    ctx.sem("a_fsA"); ctx.sem("a_fsD"); ctx.sem("a_st0"); ctx.sem("a_st1")

    pr_mark = [0, 0]
    y_mark = [0, 0]
    for b in range(BPC):
        b2_ps = ExitStack()
        pq = b2_ps.enter_context(nc.psum_tensor("a_pq%d" % b, [128, 128], F32))
        pk = b2_ps.enter_context(nc.psum_tensor("a_pk%d" % b, [128, 128], F32))
        pv = b2_ps.enter_context(nc.psum_tensor("a_pv%d" % b, [128, 1024], F32))
        ptr = b2_ps.enter_context(nc.psum_tensor("a_ptr%d" % b, [128, 1024], BF16))
        watt = b2_ps.enter_context(nc.sbuf_tensor("a_watt%d" % b, [128, 8 * 3072], BF16))
        for kc in range(8):
            ald(watt[:, kc * 3072:(kc + 1) * 3072],
                w["l1a_w_attn"][kc * 128:(kc + 1) * 128, :])
        for tt in range(NTT):
            i = b * NTT + tt
            p = i % 2
            if i >= 1:
                ctx.wait(nc.sync, "aq_ln", ctx.val("aq_ln"))
            ctx.inc(nc.sync.dma_start(out=xbq[p][:], in_=x_in[i * 128:(i + 1) * 128, :]),
                    "aq_ld%d" % p, 16)
            ctx.wait(nc.vector, "aq_ld%d" % p)
            if i == 0:
                ctx.wait(nc.vector, "aAfinA", 1)
            if i >= 1:
                ctx.wait(nc.vector, "aq_trx", ctx.val("aq_trx"))
            ctx.inc(ctx.V.tensor_scalar(xlnq[p][:], xbq[p][:], mean_a[:, i:i + 1],
                                        rstd_a[:, i:i + 1], ALU.subtract, ALU.mult), "aq_ln", 1)
            ctx.wait(nc.tensor, "aq_ln")
            if tt == 0:
                ctx.wait(nc.tensor, "ap_ld")
            ctx.wait(nc.tensor, "aq_fm")
            for c in range(8):
                tin = nc.tensor.transpose(ptr[:, c * 128:c * 128 + 128],
                                          xlnq[p][:, c * 128:c * 128 + 128], ktiles["ident_b"][:])
            ctx.inc(tin, "aq_trx", 1)
            ctx.wait(nc.vector, "aq_trx")
            if i >= 1:
                ctx.wait(nc.vector, "aq_mm", ctx.val("aq_mm"))
            ctx.inc(ctx.V.tensor_copy(xfmq[p][:], ptr[:]), "aq_fm", 1)
            ctx.wait(nc.tensor, "aq_fm")
            for c in range(8):
                if not (b == 0 and tt == 0 and c == 0):
                    ctx.wait(nc.tensor, "aq_ev", ctx.val("aq_ev"))
                for kc in range(8):
                    nc.tensor.matmul(pq[:], watt[:, kc * 3072 + c * 128:kc * 3072 + c * 128 + 128],
                                     xfmq[p][:, kc * 128:kc * 128 + 128],
                                     start=(kc == 0), stop=False)
                nc.tensor.matmul(pq[:], batt[0:1, c * 128:c * 128 + 128],
                                 ktiles["ones_row_b"][:], start=False, stop=True)
                for kc in range(8):
                    nc.tensor.matmul(pk[:], watt[:, kc * 3072 + 1024 + c * 128:kc * 3072 + 1024 + c * 128 + 128],
                                     xfmq[p][:, kc * 128:kc * 128 + 128],
                                     start=(kc == 0), stop=False)
                tin = nc.tensor.matmul(pk[:], batt[0:1, 1024 + c * 128:1024 + c * 128 + 128],
                                       ktiles["ones_row_b"][:], start=False, stop=True)
                ctx.inc(tin, "aq_mm", 1)
                ctx.wait(nc.scalar, "aq_mm")
                ctx.S.activation(Qf[:, c * T + tt * 128:c * T + tt * 128 + 128], pq[:],
                                 AF.Identity, scale=0.125)
                ctx.inc(ctx.S.copy(Kf[:, c * T + tt * 128:c * T + tt * 128 + 128], pk[:]),
                        "aq_ev", 1)
            ctx.wait(nc.tensor, "aq_ev")
            for bb in range(2):
                for kc in range(8):
                    nc.tensor.matmul(pv[:, bb * 512:bb * 512 + 512],
                                     xfmq[p][:, kc * 128:kc * 128 + 128],
                                     watt[:, kc * 3072 + 2048 + bb * 512:kc * 3072 + 2048 + bb * 512 + 512],
                                     start=(kc == 0), stop=False)
                tin = nc.tensor.matmul(pv[:, bb * 512:bb * 512 + 512], ktiles["ones_row_b"][:],
                                       batt[0:1, 2048 + bb * 512:2048 + bb * 512 + 512],
                                       start=False, stop=True)
            ctx.inc(tin, "aq_mm", 1)
            ctx.wait(nc.scalar, "aq_mm")
            ctx.inc(ctx.S.copy(Vt[:, tt * D:(tt + 1) * D], pv[:]), "aq_ev", 1)
        ctx.wait(nc.tensor, "aq_ev")
        ctx.wait(nc.gpsimd, "ap_ld")
        b2_ps.close()
        barrier(ctx, "a_b2_%d" % b)

        ah_ps = ExitStack()
        _a = lambda n, sh_, dt_: ah_ps.enter_context(nc.sbuf_tensor(n + str(b), sh_, dt_))
        zsb = [_a("a_zb0", [128, T], BF16)]
        zsb.append(zsb[0])
        zmm = _a("a_zm", [128, T], BF16)
        Cb2 = [_a("a_Cb0", [128, 16 * 256], BF16), _a("a_Cb1", [128, 16 * 256], BF16)]
        ptil = [_a("a_pt0", [128, 512], BF16), _a("a_pt1", [128, 512], BF16)]
        nth = _a("a_nth", [1, T], BF16)
        ntl = _a("a_ntl", [1, T], BF16)
        dmp = [_a("a_dm0", [128, 256], BF16), _a("a_dm1", [128, 256], BF16)]
        pz1 = ah_ps.enter_context(nc.psum_tensor("a_pz1%d" % b, [128, 2048], F32))
        pz2 = [ah_ps.enter_context(nc.psum_tensor("a_pz2%d_%d" % (b, j), [128, 512], F32))
               for j in range(2)]
        pyt = ah_ps.enter_context(nc.psum_tensor("a_pyt%d" % b, [64, 512], F32))
        ptt = ah_ps.enter_context(nc.psum_tensor("a_ptt%d" % b, [1, 512], F32))
        zb_markD = [0, 0]
        zb_markP = [0, 0]

        def mk_p1(h):
            hp_ = (h % 2) * 64
            hc_ = h // 2
            CbH = Cb2[h % 2]
            ops = []
            for qtb in range(NTT):
                KT = (qtb + 1) * 128
                par = qtb % 2
                W = qtb + 1

                def u_pe(qtb=qtb, KT=KT, hp_=hp_, hc_=hc_, h=h):
                    if h + qtb > 0 or b > 0:
                        ctx.wait(nc.tensor, "ah_zev", ctx.val("ah_zev"))
                    nmm = (KT + 511) // 512
                    for mi in range(nmm):
                        k0 = mi * 512
                        kn = min(512, KT - k0)
                        nc.tensor.matmul(pz1[:, k0:k0 + kn],
                                         Qf[hp_:hp_ + 64, hc_ * T + qtb * 128:hc_ * T + qtb * 128 + 128],
                                         Kf[hp_:hp_ + 64, hc_ * T + k0:hc_ * T + k0 + kn],
                                         start=True, stop=(mi != nmm - 1))
                    tin = nc.tensor.matmul(pz1[:, KT - 128:KT], ktiles["negI128_b"][:],
                                           ktiles["uts128_b"][:], start=False, stop=True)
                    ctx.inc(tin, "ah_z", 1)
                ops.append(u_pe)

                def u_ev(KT=KT, par=par):
                    ctx.wait(nc.scalar, "ah_z")
                    ctx.wait(nc.scalar, "ah_m1", max(zb_markD))
                    ctx.wait(nc.scalar, "ah_zm", max(zb_markP))
                    ctx.inc(ctx.S.copy(zsb[par][:, 0:KT], pz1[:, 0:KT]), "ah_zev", 1)
                ops.append(u_ev)

                def u_dv(qtb=qtb, KT=KT, par=par, W=W, CbH=CbH):
                    ctx.wait(nc.vector, "ah_zev")
                    ctx.inc(ctx.V.reduce_max(CbH[:, qtb * 256:qtb * 256 + 128],
                                             zsb[par][:, 0:KT].rearrange("p (n w) -> p n w", w=W),
                                             axis=AX.X), "ah_m1", 1)
                    zb_markD[par] = ctx.val("ah_m1")
                    ctx.V.tensor_tensor(zmm[:, 0:KT].rearrange("p (n w) -> p n w", w=W),
                                        zsb[par][:, 0:KT].rearrange("p (n w) -> p n w", w=W),
                                        CbH[:, qtb * 256:qtb * 256 + 128].unsqueeze(2).broadcast_to([128, 128, W]),
                                        ALU.is_equal)
                    ctx.inc(ctx.V.scalar_tensor_tensor(zmm[:, 0:KT], zmm[:, 0:KT], -2e9,
                                                       zsb[par][:, 0:KT], ALU.mult, ALU.add), "ah_zm", 1)
                    zb_markP[par] = ctx.val("ah_zm")
                    ctx.inc(ctx.V.reduce_max(CbH[:, qtb * 256 + 128:qtb * 256 + 256],
                                             zmm[:, 0:KT].rearrange("p (n w) -> p n w", w=W),
                                             axis=AX.X), "ah_m2", 1)
                ops.append(u_dv)
            return ops

        def mk_itp2(h):
            hp_ = (h % 2) * 64
            hc_ = h // 2
            CbH = Cb2[h % 2]
            ops = []

            def u_init():
                if ctx.val("ah_tt"):
                    ctx.wait(nc.vector, "ah_tt", ctx.val("ah_tt"))
                ctx.V.reduce_max(mN[:], CbH[:].rearrange("p (n s) -> p n s", s=256),
                                 axis=AX.X)
                ctx.V.tensor_scalar(tau[:], mN[:], -1.0, None, ALU.add)
            ops.append(u_init)
            for it in range(NIT):
                for blkp in range(8):
                    def u_blk(it=it, blkp=blkp, CbH=CbH):
                        for blk in (2 * blkp, 2 * blkp + 1):
                            dp = blk % 2
                            ctx.V.tensor_scalar(dmp[dp][:], CbH[:, blk * 256:blk * 256 + 256],
                                                tau[:, blk:blk + 1], 0.0, ALU.max, ALU.add,
                                                accum_out=rs[:, blk:blk + 1])
                            ctx.V.tensor_scalar(dmp[dp][:], CbH[:, blk * 256:blk * 256 + 256],
                                                tau[:, blk:blk + 1], 0.0, ALU.is_gt, ALU.add,
                                                accum_out=kcnt[:, blk:blk + 1])
                    ops.append(u_blk)

                def u_up(it=it):
                    ctx.V.scalar_tensor_tensor(t16a[:], tau[:], -256.0, rs[:], ALU.mult, ALU.add)
                    ctx.V.tensor_scalar(t16a[:], t16a[:], -1.0, None, ALU.add)
                    ctx.V.tensor_scalar(t16b[:], kcnt[:], 1.0, None, ALU.max)
                    ctx.V.reciprocal(t16b[:], t16b[:])
                    ctx.V.tensor_mul(t16b[:], t16a[:], t16b[:])
                    ins = ctx.V.tensor_add(tau[:], tau[:], t16b[:])
                    if it == NIT - 1:
                        ctx.inc(ins, "ah_it", 1)
                ops.append(u_up)
            for grp in range(4):
                def u_tr(grp=grp, h=h):
                    if grp == 0:
                        ctx.wait(nc.tensor, "ah_it")
                    if h > 0 or grp > 0 or b > 0:
                        ctx.wait(nc.tensor, "ah_tev", ctx.val("ah_tev"))
                    for j in range(4):
                        blk = grp * 4 + j
                        tin = nc.tensor.transpose(ptt[0:1, j * 128:j * 128 + 128],
                                                  tau[:, blk:blk + 1], ktiles["ident_f"][:])
                    ctx.inc(tin, "ah_tt", 1)
                    ctx.wait(nc.vector, "ah_tt")
                    if (h > 0 or b > 0) and grp == 0:
                        ctx.wait(nc.vector, "ah_y", ctx.val("ah_y"))
                    ctx.V.tensor_scalar(nth[0:1, grp * 512:grp * 512 + 512], ptt[:],
                                        -1.0, None, ALU.mult)
                    ctx.inc(ctx.V.scalar_tensor_tensor(ntl[0:1, grp * 512:grp * 512 + 512],
                                                       ptt[:], -1.0,
                                                       nth[0:1, grp * 512:grp * 512 + 512],
                                                       ALU.mult, ALU.subtract), "ah_tev", 1)
                ops.append(u_tr)
            for sq in range(3, -1, -1):
                q0 = sq * 512
                nktc = 4 * sq + 4
                for ktc in range(nktc):
                    def u_z2(sq=sq, q0=q0, ktc=ktc, nktc=nktc, hp_=hp_, hc_=hc_, h=h):
                        par2 = ktc % 2
                        if sq == 3 and ktc == 0:
                            ctx.wait(nc.tensor, "ah_tev")
                        ctx.wait(nc.tensor, "ah_pr", pr_mark[par2])
                        if ktc == 0:
                            ctx.wait(nc.tensor, "ah_yev", ctx.val("ah_yev"))
                        nc.tensor.matmul(pz2[par2][:],
                                         Kf[hp_:hp_ + 64, hc_ * T + ktc * 128:hc_ * T + ktc * 128 + 128],
                                         Qf[hp_:hp_ + 64, hc_ * T + q0:hc_ * T + q0 + 512],
                                         start=True, stop=False)
                        nc.tensor.matmul(pz2[par2][:], ktiles["ones_row_b"][:],
                                         nth[0:1, q0:q0 + 512], start=False, stop=False)
                        dtile = ktc - 4 * sq
                        if dtile > 0:
                            nc.tensor.matmul(pz2[par2][:, 0:dtile * 128], ktiles["ones_row_b"][:],
                                             ktiles["neg1e9row_b"][0:1, 0:dtile * 128],
                                             start=False, stop=False)
                        if 0 <= dtile < 4:
                            nc.tensor.matmul(pz2[par2][:], ktiles["ones_row_b"][:],
                                             ntl[0:1, q0:q0 + 512], start=False, stop=False)
                            tin = nc.tensor.matmul(pz2[par2][:, dtile * 128:dtile * 128 + 128],
                                                   ktiles["negI128_b"][:], ktiles["lts128_b"][:],
                                                   start=False, stop=True)
                        else:
                            tin = nc.tensor.matmul(pz2[par2][:], ktiles["ones_row_b"][:],
                                                   ntl[0:1, q0:q0 + 512], start=False, stop=True)
                        ctx.inc(tin, "ah_z2", 1)
                        if ktc > 0:
                            pk_ = (ktc - 1) % 2
                            ctx.wait(nc.tensor, "ah_pr", pr_mark[pk_])
                            tin = nc.tensor.matmul(pyt[:],
                                                   Vt[:, (ktc - 1) * D + h * HD:(ktc - 1) * D + h * HD + HD],
                                                   ptil[pk_][:],
                                                   start=(ktc == 1), stop=False)
                            ctx.inc(tin, "ah_y", 1)
                            y_mark[pk_] = ctx.val("ah_y")
                        ctx.wait(nc.scalar, "ah_z2")
                        ctx.wait(nc.scalar, "ah_y", y_mark[par2])
                        ctx.inc(ctx.S.activation(ptil[par2][:], pz2[par2][:], AF.Relu), "ah_pr", 1)
                        pr_mark[par2] = ctx.val("ah_pr")
                    ops.append(u_z2)

                def u_ylast(sq=sq, q0=q0, nktc=nktc, hp_=hp_, hc_=hc_, h=h):
                    pk_ = (nktc - 1) % 2
                    ctx.wait(nc.tensor, "ah_pr", pr_mark[pk_])
                    tin = nc.tensor.matmul(pyt[:],
                                           Vt[:, (nktc - 1) * D + h * HD:(nktc - 1) * D + h * HD + HD],
                                           ptil[pk_][:],
                                           start=(nktc == 1), stop=True)
                    ctx.inc(tin, "ah_y", 1)
                    y_mark[pk_] = ctx.val("ah_y")
                    ctx.wait(nc.scalar, "ah_y")
                    ctx.inc(ctx.S.copy(Qf[hp_:hp_ + 64, hc_ * T + q0:hc_ * T + q0 + 512],
                                       pyt[:]), "ah_yev", 1)
                ops.append(u_ylast)
            return ops

        prevA = None
        for h in range(H):
            p1ops = mk_p1(h)
            if prevA is None or DBG.get("a_seq"):
                if prevA is not None:
                    for f in prevA:
                        f()
                    prevA = None
                for f in p1ops:
                    f()
            else:
                na, nb = len(prevA), len(p1ops)
                seq = [(i / na, 0, f) for i, f in enumerate(prevA)]
                seq += [(i / nb, 1, f) for i, f in enumerate(p1ops)]
                for _, _, f in sorted(seq, key=lambda x: (x[0], x[1])):
                    f()
            prevA = mk_itp2(h)
        for f in prevA:
            f()
        ctx.wait(nc.tensor, "ah_yev")
        ah_ps.close()
        barrier(ctx, "a_at_%d" % b)

        pj_ps = ExitStack()
        wpro = pj_ps.enter_context(nc.sbuf_tensor("a_wpro%d" % b, [128, 8 * D], BF16))
        bpro = pj_ps.enter_context(nc.sbuf_tensor("a_bpro%d" % b, [1, D], BF16))
        ctx.wait(nc.gpsimd, "ap_ld")
        ctx.inc(g.dma_start(out=bpro[:], in_=w["l1a_b_proj"].unsqueeze(0)), "ap_ld", 16)
        for kc in range(8):
            ctx.wait(nc.gpsimd, "ap_ld")
            ctx.inc(g.dma_start(out=wpro[:, kc * D:(kc + 1) * D],
                                in_=w["l1a_w_proj"][kc * 128:(kc + 1) * 128, :]), "ap_ld", 16)
        po = pj_ps.enter_context(nc.psum_tensor("a_po%d" % b, [128, 1024], F32))
        for tt in range(NTT):
            i = b * NTT + tt
            p = i % 2
            ctx.wait(nc.tensor, "aj_fm")
            if tt == 0:
                ctx.wait(nc.tensor, "ap_ld")
            if ctx.val("aj_oe") > 0:
                ctx.wait(nc.tensor, "aj_oe", ctx.val("aj_oe"))
            for bb in range(2):
                for kc in range(8):
                    nc.tensor.matmul(po[:, bb * 512:bb * 512 + 512],
                                     Qf[:, kc * T + tt * 128:kc * T + tt * 128 + 128],
                                     wpro[:, kc * D + bb * 512:kc * D + bb * 512 + 512],
                                     start=(kc == 0), stop=False)
                tin = nc.tensor.matmul(po[:, bb * 512:bb * 512 + 512], ktiles["ones_row_b"][:],
                                       bpro[0:1, bb * 512:bb * 512 + 512], start=False, stop=True)
            ctx.inc(tin, "aj_om", 1)
            ctx.wait(nc.vector, "aj_om")
            if i >= 2:
                ctx.wait(nc.vector, "a_st%d" % p)
            ctx.inc(ctx.V.tensor_copy(fout[p][:], po[:]), "aj_oe", 1)
            ctx.wait(nc.scalar, "aj_oe")
            ctx.inc(ctx.S.activation((sh["scr"] if p == 0 else sh["scr2"])[:], fout[p][:],
                                     AF.Square, accum_out=sh["ssq"][:, i:i + 1]), "a_fsA", 1)
            ctx.wait(nc.vector, "aj_oe")
            ctx.inc(ctx.V.reduce_sum(sh["nsum"][:, i:i + 1], fout[p][:], axis=AX.X,
                                     negate=True), "a_fsD", 1)
            ctx.wait(nc.gpsimd, "a_fsA")
            ctx.wait(nc.gpsimd, "a_fsD")
            ctx.inc(nc.gpsimd.dma_start(out=f_raw[i * 128:(i + 1) * 128, :], in_=fout[p][:]),
                    "a_st%d" % p, 16)
        ctx.wait(nc.tensor, "aj_oe")
        ctx.wait(nc.gpsimd, "a_st0")
        ctx.wait(nc.gpsimd, "a_st1")
        pj_ps.close()
        barrier(ctx, "a_pj_%d" % b)

    emit_stats_finish(ctx, sh, mean_f, rstd_f, "a_")
    ctx.wait(nc.gpsimd, "a_st0")
    ctx.wait(nc.gpsimd, "a_st1")
    barrier(ctx, "a_bar")
    emit_residual_pass(ctx, sh, x_in, f_raw, x_out, mean_f, rstd_f, "ac_", "aA")


def emit_l0_attn(ctx, ktiles, sh, x_in, x_out, w):
    nc = ctx.nc
    NH = 8
    NTT = 16
    LN128 = float(0.5 * np.log(128.0))

    mean_a = ctx.sb("d_meanA", [128, NT], F32)
    rstd_a = ctx.sb("d_rstdA", [128, NT], F32)
    mean_f = ctx.sb("d_meanF", [128, NT], F32)
    rstd_f = ctx.sb("d_rstdF", [128, NT], F32)
    f_raw = sh["f_raw"]

    emit_ln_stats(ctx, sh, x_in, mean_a, rstd_a, "dA")
    if DBG.get("l0a"):
        ctx.wait(nc.gpsimd, "dAfinA", 1)
        ctx.tap("meanA", mean_a[:])
        ctx.tap("rstdA", rstd_a[:])
        ctx.tap("nsumA", sh["nsum"][:])
        ctx.tap("ssqA", sh["ssq"][:])

    wqkvz = ctx.sb("d_wqkvz", [128, 8 * 4096], BF16)
    wba = ctx.sb("d_wba", [128, 8 * 16], BF16)
    woutb = ctx.sb("d_wout", [128, 8 * 1024], BF16)
    cw = ctx.sb("d_cw", [128, 24 * 4], F32)
    alog = ctx.sb("d_alog", [1, 8], F32)
    dtbf = ctx.sb("d_dtbf", [1, 8], F32)
    dtbr = ctx.sb("d_dtbr", [1, 16], BF16)
    nwc = ctx.sb("d_nwc", [128, 1], F32)
    nEal = ctx.sb("d_nEal", [128, 8], F32)
    ealr = ctx.sb("d_ealr", [1, 8], F32)
    ealb = ctx.sb("d_ealb", [1, 8], BF16)
    ctx.sem("dp_ld"); ctx.sem("dp_c")
    g = nc.gpsimd

    def dld(out_ap, in_ap, **kw):
        ctx.wait(nc.gpsimd, "dp_ld")
        ctx.inc(g.dma_start(out=out_ap, in_=in_ap, **kw), "dp_ld", 16)

    for kc in range(8):
        dld(wqkvz[:, kc * 4096:(kc + 1) * 4096], w["l0a_W_qkvz"][kc * 128:(kc + 1) * 128, :])
    for kc in range(8):
        dld(wba[:, kc * 16:(kc + 1) * 16], w["l0a_W_ba"][kc * 128:(kc + 1) * 128, :])
    dld(cw[:].rearrange("p (c k) -> p c k", k=4),
        w["l0a_conv_w"].rearrange("(c p) o k -> p c (o k)", p=128))
    dld(alog[:], w["l0a_A_log"].unsqueeze(0))
    dld(dtbf[:], w["l0a_dt_bias"].unsqueeze(0))
    dld(nwc[:], w["l0a_norm_weight"].unsqueeze(1), allow_slow_non_contiguous=True)
    ctx.wait(nc.vector, "dp_ld")
    ctx.V.memset(dtbr[:], 0.0)
    ctx.V.tensor_copy(
        dtbr[0:1, :].rearrange("o (a two) -> o a two", two=2)[:, :, 1:2],
        dtbf[:].unsqueeze(2))
    ctx.inc(ctx.V.tensor_copy(ealb[:], ealr[:]) if False else ctx.V.engine_nop(), "dp_c", 1) \
        if False else None
    ctx.wait(nc.scalar, "dp_ld")
    ctx.inc(ctx.S.activation(ealr[:], alog[:], AF.Exp), "dp_c", 1)
    ctx.wait(nc.vector, "dp_c")
    ctx.inc(ctx.V.tensor_copy(ealb[:], ealr[:]), "dp_c", 1)
    pEa_es = ExitStack()
    pEa = pEa_es.enter_context(nc.psum_tensor("d_pEa", [128, 8], F32))
    ctx.wait(nc.tensor, "dp_c")
    ctx.inc(nc.tensor.matmul(pEa[:], ktiles["ones_row_b"][:], ealb[:],
                             start=True, stop=True), "dp_c", 1)
    ctx.wait(nc.vector, "dp_c")
    ctx.inc(ctx.V.tensor_scalar(nEal[:], pEa[:], -1.0, None, ALU.mult), "dp_c", 1)
    wldd_es = ExitStack()
    wldd = wldd_es.enter_context(nc.sbuf_tensor("d_wldd", [128, 1024], F32))
    for hh in range(8):
        if hh > 0:
            ctx.wait(nc.gpsimd, "dp_c", ctx.val("dp_c"))
        dld(wldd[:], w["l0a_W_out"][hh * 128:(hh + 1) * 128, :])
        ctx.wait(nc.vector, "dp_ld")
        ctx.inc(ctx.V.tensor_scalar(woutb[:, hh * 1024:(hh + 1) * 1024], wldd[:],
                                    nwc[:], None, ALU.mult), "dp_c", 1)
    ctx.wait(nc.tensor, "dp_c")
    ctx.wait(nc.gpsimd, "dp_ld")
    ctx.wait(nc.vector, "dp_c")
    wldd_es.close()
    pEa_es.close()
    barrier(ctx, "d_prep")

    xbd = [ctx.sb("d_xb0", [128, D], F32), ctx.sb("d_xb1", [128, D], F32)]
    xlnd = ctx.sb("d_xln", [128, D], BF16)
    xfm256 = ctx.sb("d_xfm", [128, 8 * 256], BF16)
    rawb = ctx.sb("d_raw", [128, 24 * 260], BF16)
    qkvc = ctx.sb("d_qkvc", [128, 24 * 256], BF16)
    szfm = ctx.sb("d_szfm", [128, 8 * 256], BF16)
    batm = ctx.sb("d_batm", [128, 16], F32)
    t8a = ctx.sb("d_t8a", [128, 8], F32)
    t8b = ctx.sb("d_t8b", [128, 8], F32)
    t8c = ctx.sb("d_t8c", [128, 8], F32)
    t8d = ctx.sb("d_t8d", [128, 8], F32)
    gcb = ctx.sb("d_gcb", [128, 16 * 8], F32)
    rvb = ctx.sb("d_rvb", [128, 16 * 8], F32)
    glb = ctx.sb("d_glb", [128, 16 * 8], F32)
    lnbb = ctx.sb("d_lnbb", [128, 16 * 8], F32)
    betb = ctx.sb("d_betb", [128, 16 * 8], F32)
    Sst = [ctx.sb("d_S%d" % hh, [128, 128], F32) for hh in range(NH)]
    Sbf = [ctx.sb("d_Sb%d" % hh, [128, 128], BF16) for hh in range(NH)]
    dtile = [ctx.sb("d_dt%d" % t_, [128, 128], BF16) for t_ in range(4)]
    ogbuf = ctx.sb("d_og", [128, 8 * 128], BF16)
    cgB = ctx.sb("d_cgB", [64, 8], F32)
    crB = ctx.sb("d_crB", [64, 8], F32)
    clB = ctx.sb("d_clB", [64, 8], F32)
    cglB = ctx.sb("d_cglB", [64, 8], F32)
    cbtB = ctx.sb("d_cbtB", [64, 8], F32)
    SL = []
    for s in range(2):
        t = "ds%d_" % s
        SL.append(dict(
            ktm=ctx.sb(t + "ktm", [64, 128], BF16),
            vtm=ctx.sb(t + "vtm", [64, 128], BF16),
            vbeta=ctx.sb(t + "vbeta", [64, 128], BF16),
            kbeg=ctx.sb(t + "kbeg", [64, 128], BF16),
            kglb=ctx.sb(t + "kgl", [64, 128], BF16),
            vnew=ctx.sb(t + "vnew", [64, 128], BF16),
            vnb=ctx.sb(t + "vn", [64, 128], BF16),
            kcum=ctx.sb(t + "kcum", [128, 64], BF16),
            qeb=ctx.sb(t + "qe", [128, 64], BF16),
            sqk=ctx.sb(t + "sqk", [64, 128], BF16),
            sqq=ctx.sb(t + "sqq", [128, 64], BF16),
            sqo=ctx.sb(t + "sqo", [128, 64], BF16),
            og1=ctx.sb(t + "og1", [128, 64], BF16),
            tri=ctx.sb(t + "tri", [64, 64], BF16),
            triT=ctx.sb(t + "triT", [64, 64], BF16),
            attnT=ctx.sb(t + "attnT", [64, 64], BF16),
            dcy=ctx.sb(t + "dcy", [64, 64], F32),
            dcy2=ctx.sb(t + "dcy2", [64, 64], F32),
            Pb=[ctx.sb(t + "P%d" % j, [64, 64], BF16) for j in range(2)],
            Qb=[ctx.sb(t + "Q%d" % j, [64, 64], BF16) for j in range(2)],
            Xb=[ctx.sb(t + "X%d" % j, [64, 64], BF16) for j in range(2)],
            Wb=[ctx.sb(t + "W%d" % j, [64, 64], BF16) for j in range(2)],
            rows=ctx.sb(t + "rows", [1, 192], F32),
            r3r=ctx.sb(t + "r3r", [1, 64], F32),
            r1b=ctx.sb(t + "r1b", [1, 64], BF16),
            r2b=ctx.sb(t + "r2b", [1, 64], BF16),
            r3b=ctx.sb(t + "r3b", [1, 64], BF16),
            r1l=ctx.sb(t + "r1l", [1, 64], BF16),
            r2l=ctx.sb(t + "r2l", [1, 64], BF16),
            r3l=ctx.sb(t + "r3l", [1, 64], BF16),
            r3e=ctx.sb(t + "r3e", [1, 64], BF16),
            ssk=ctx.sb(t + "ssk", [64, 1], F32),
            lnrkc=ctx.sb(t + "lnrk", [64, 1], F32),
            lnqr=ctx.sb(t + "lnqr", [1, 64], F32),
            r1c=ctx.sb(t + "r1c", [64, 1], F32),
            r2mc=ctx.sb(t + "r2mc", [64, 1], F32),
            e1c=ctx.sb(t + "e1c", [64, 1], F32),
            cta=ctx.sb(t + "cta", [64, 1], F32),
            ctb=ctx.sb(t + "ctb", [64, 1], F32),
            eglc=ctx.sb(t + "eglc", [128, 1], F32),
            rsr=ctx.sb(t + "rsr", [1, 64], F32),
            rsb=ctx.sb(t + "rsb", [1, 64], BF16),
        ))
    fout = [ctx.sb("d_fo0", [128, D], F32), ctx.sb("d_fo1", [128, D], F32)]

    ctx.sem("dq_ld0"); ctx.sem("dq_ld1"); ctx.sem("dq_ln"); ctx.sem("dq_trx")
    ctx.sem("dq_fm"); ctx.sem("dq_mm"); ctx.sem("dq_ev"); ctx.sem("dq_cv")
    ctx.sem("dq_g"); ctx.sem("dq_halo")
    ctx.sem("pm0"); ctx.sem("pm1"); ctx.sem("pv0"); ctx.sem("pv1")
    ctx.sem("pa0"); ctx.sem("pa1")
    ctx.sem("do_mm"); ctx.sem("do_ev")
    ctx.sem("d_fsA"); ctx.sem("d_fsD"); ctx.sem("d_st0"); ctx.sem("d_st1")

    def dmm(out_ap, lhsT, rhs, start=True, stop=True):
        return nc.tensor.matmul(out_ap, lhsT, rhs, start=start, stop=stop)

    def pe_sync_dve():
        ctx.finalize_pending("dve")
        last = ctx.last.get("dve")
        if last is not None:
            nc.tensor.wait_ge(ctx.sems[last[0]][0], last[1])

    def act_sync_dve():
        ctx.finalize_pending("dve")
        last = ctx.last.get("dve")
        if last is not None:
            nc.scalar.wait_ge(ctx.sems[last[0]][0], last[1])

    env = dict(nc=nc, dmm=dmm, qkvc=qkvc, szfm=szfm, gcb=gcb, rvb=rvb, glb=glb,
               lnbb=lnbb, betb=betb, cgB=cgB, crB=crB, clB=clB, cglB=cglB,
               cbtB=cbtB, SL=SL, Sst=Sst, Sbf=Sbf, ogbuf=ogbuf, LN128=LN128)

    for b in range(BPC):
        for hh in range(NH):
            ctx.V.memset(Sst[hh][:], 0.0)
            ins = ctx.V.memset(Sbf[hh][:], 0.0)
            if hh == NH - 2:
                ctx.inc(ins, "pv0", 1)
            elif hh == NH - 1:
                ctx.inc(ins, "pv1", 1)
        ctx.inc(ctx.V.memset(rawb[:].rearrange("p (c x) -> p c x", x=260)[:, :, 0:3], 0.0),
                "dq_halo", 1)
        for q in range(8):
            q_ps = ExitStack()
            pqz = q_ps.enter_context(nc.psum_tensor("d_pqz%d_%d" % (b, q), [128, 256], F32))
            pba = q_ps.enter_context(nc.psum_tensor("d_pba%d_%d" % (b, q), [128, 256], F32))
            pcv = q_ps.enter_context(nc.psum_tensor("d_pcv%d_%d" % (b, q), [128, 256], F32))
            ptrq = q_ps.enter_context(nc.psum_tensor("d_ptrq%d_%d" % (b, q), [128, 1024], BF16))
            for halftile in range(2):
                tt = 2 * q + halftile
                i = b * NTT + tt
                p = i % 2
                if ctx.val("dq_ln") > 0:
                    ctx.wait(nc.sync, "dq_ln", ctx.val("dq_ln") - 1)
                ctx.inc(nc.sync.dma_start(out=xbd[p][:], in_=x_in[i * 128:(i + 1) * 128, :]),
                        "dq_ld%d" % p, 16)
                ctx.wait(nc.vector, "dq_ld%d" % p)
                if i == 0:
                    ctx.wait(nc.vector, "dAfinA", 1)
                if ctx.val("dq_trx") > 0:
                    ctx.wait(nc.vector, "dq_trx", ctx.val("dq_trx"))
                ctx.inc(ctx.V.tensor_scalar(xlnd[:], xbd[p][:], mean_a[:, i:i + 1],
                                            rstd_a[:, i:i + 1], ALU.subtract, ALU.mult),
                        "dq_ln", 1)
                ctx.wait(nc.tensor, "dq_ln")
                if i == 0:
                    ctx.wait(nc.tensor, "dp_ld")
                    ctx.wait(nc.tensor, "dp_c")
                if ctx.val("dq_fm") > 0:
                    ctx.wait(nc.tensor, "dq_fm", ctx.val("dq_fm"))
                for c in range(8):
                    tin = nc.tensor.transpose(ptrq[:, c * 128:c * 128 + 128],
                                              xlnd[:, c * 128:c * 128 + 128],
                                              ktiles["ident_b"][:])
                ctx.inc(tin, "dq_trx", 1)
                ctx.wait(nc.vector, "dq_trx")
                if ctx.val("dq_mm") > 0:
                    ctx.wait(nc.vector, "dq_mm", ctx.val("dq_mm"))
                ctx.inc(ctx.V.tensor_copy(
                    xfm256[:].rearrange("p (kc t) -> p kc t", t=256)[:, :, halftile * 128:halftile * 128 + 128],
                    ptrq[:].rearrange("p (kc t) -> p kc t", t=128)), "dq_fm", 1)
                if DBG.get("l0a") and b == 0 and q == 0 and halftile == 0:
                    barrier(ctx, "dbg_xfm0")
                    dbgtile = ctx.sb("dbg_ptrq", [128, 1024], BF16)
                    ctx.V.tensor_copy(dbgtile[:], ptrq[:])
                    dbgx = ctx.sb("dbg_xlncp", [128, 1024], BF16)
                    ctx.inc(ctx.V.tensor_copy(dbgx[:], xlnd[:]), "dq_fm", 1)
                    barrier(ctx, "dbg_xfm0c")
                    ctx.tap("xfm_t0", xfm256[:])
                    ctx.tap("ptrq0", dbgtile[:])
                    ctx.tap("xln0", dbgx[:])
                    ctx.wait(nc.gpsimd, "dbg_st")
                    barrier(ctx, "dbg_xfm0b")
            ctx.wait(nc.tensor, "dq_fm")
            ctx.wait(nc.tensor, "dq_halo")
            for c in range(32):
                h = c // 4
                role = c % 4
                if ctx.val("dq_ev") > 0:
                    ctx.wait(nc.tensor, "dq_ev", ctx.val("dq_ev"))
                for kc in range(8):
                    tin = dmm(pqz[:], wqkvz[:, kc * 4096 + c * 128:kc * 4096 + c * 128 + 128],
                              xfm256[:, kc * 256:(kc + 1) * 256],
                              start=(kc == 0), stop=(kc == 7))
                ctx.inc(tin, "dq_mm", 1)
                ctx.wait(nc.scalar, "dq_mm")
                if c == 0:
                    ctx.wait(nc.scalar, "dq_halo")
                if role == 3:
                    ctx.inc(ctx.S.activation(szfm[:, h * 256:(h + 1) * 256], pqz[:], AF.Silu),
                            "dq_ev", 1)
                else:
                    m = role * 8 + h
                    ctx.inc(ctx.S.copy(rawb[:, m * 260 + 3:m * 260 + 259], pqz[:]), "dq_ev", 1)
            for halftile in range(2):
                tt = 2 * q + halftile
                if ctx.val("dq_g") > 0:
                    ctx.wait(nc.tensor, "dq_g", ctx.val("dq_g"))
                for kc in range(8):
                    dmm(pba[:, 0:16],
                        xfm256[:, kc * 256 + halftile * 128:kc * 256 + halftile * 128 + 128],
                        wba[:, kc * 16:(kc + 1) * 16], start=(kc == 0), stop=False)
                tin = dmm(pba[:, 0:16], ktiles["ones_row_b"][:], dtbr[:],
                          start=False, stop=True)
                ctx.inc(tin, "dq_mm", 1)
                ctx.wait(nc.vector, "dq_mm")
                ctx.V.tensor_copy(batm[:], pba[:, 0:16])
                bav = batm[:].rearrange("p (a two) -> p a two", two=2)
                bcol = t8c
                acol = t8d
                ctx.V.tensor_copy(bcol[:], bav[:, :, 0:1].squeeze(2))
                ctx.V.tensor_copy(acol[:], bav[:, :, 1:2].squeeze(2))
                bcol = bcol[:]
                acol = acol[:]
                lnbt = lnbb[:, tt * 8:(tt + 1) * 8]
                ctx.V.tensor_scalar(t8a[:], bcol, -1.0, None, ALU.mult)
                ctx.V.tensor_max(t8a[:], t8a[:], bcol)
                act_sync_dve()
                ctx.S.activation(t8a[:], t8a[:], AF.Exp, scale=-1.0)
                ctx.inc(ctx.S.activation(t8a[:], t8a[:], AF.Ln, bias=1.0), "dq_g", 1)
                ctx.wait(nc.vector, "dq_g")
                ctx.V.tensor_scalar(t8b[:], bcol, 0.0, None, ALU.min)
                ctx.V.tensor_sub(lnbt, t8b[:], t8a[:])
                ctx.V.tensor_scalar(t8a[:], acol, -1.0, None, ALU.mult)
                ctx.V.tensor_max(t8a[:], t8a[:], acol)
                act_sync_dve()
                ctx.S.activation(t8a[:], t8a[:], AF.Exp, scale=-1.0)
                ctx.inc(ctx.S.activation(t8a[:], t8a[:], AF.Ln, bias=1.0), "dq_g", 1)
                ctx.wait(nc.vector, "dq_g")
                ctx.V.tensor_scalar(t8b[:], acol, 0.0, None, ALU.max)
                ctx.V.tensor_add(t8a[:], t8a[:], t8b[:])
                ctx.V.tensor_mul(t8a[:], t8a[:], nEal[:])
                act_sync_dve()
                ctx.inc(ctx.S.activation(betb[:, tt * 8:(tt + 1) * 8], lnbt, AF.Exp), "dq_g", 1)
                pe_sync_dve()
                dmm(pba[:, 16:24], ktiles["bd_ut_f"][:], t8a[:])
                dmm(pba[:, 24:32], ktiles["bd_lts_f"][:], t8a[:])
                tin = dmm(pba[:, 32:40], ktiles["bd_on_f"][:], t8a[:])
                ctx.inc(tin, "dq_mm", 1)
                ctx.wait(nc.vector, "dq_mm")
                ctx.V.tensor_copy(gcb[:, tt * 8:(tt + 1) * 8], pba[:, 16:24])
                ctx.V.tensor_copy(rvb[:, tt * 8:(tt + 1) * 8], pba[:, 24:32])
                ctx.inc(ctx.V.tensor_copy(glb[:, tt * 8:(tt + 1) * 8], pba[:, 32:40]),
                        "dq_g", 1)
            ctx.wait(nc.vector, "dq_ev")
            for m in range(24):
                if ctx.val("dq_cv") > 0:
                    ctx.wait(nc.vector, "dq_cv", ctx.val("dq_cv"))
                for tp in range(4):
                    ctx.V.tensor_scalar(dtile[tp][:], ktiles["ident_b"][:],
                                        cw[:, m * 4 + tp:m * 4 + tp + 1], None, ALU.mult)
                pe_sync_dve()
                for tp in range(4):
                    tin = dmm(pcv[:], dtile[tp][:], rawb[:, m * 260 + tp:m * 260 + tp + 256],
                              start=(tp == 0), stop=(tp == 3))
                ctx.inc(tin, "dq_cv", 1)
                ctx.wait(nc.scalar, "dq_cv")
                ctx.inc(ctx.S.activation(qkvc[:, m * 256:(m + 1) * 256], pcv[:], AF.Silu),
                        "dq_ev", 1)
                ctx.wait(nc.tensor, "dq_ev")
            ctx.wait(nc.vector, "dq_cv")
            ctx.wait(nc.vector, "dq_ev")
            rawv = rawb[:].rearrange("p (c x) -> p c x", x=260)
            ctx.inc(ctx.V.tensor_copy(rawv[:, :, 0:3], rawv[:, :, 256:259]), "dq_halo", 1)
            q_ps.close()
            barrier(ctx, "d_q%d_%d" % (b, q))
            if DBG.get("l0a") and b == 0 and q == 0:
                ctx.tap("qkvc", qkvc[:])
                ctx.tap("szfm", szfm[:])
                ctx.tap("rawb", rawb[:])
                ctx.tap("gcb", gcb[:, 0:16])
                ctx.tap("lnbb", lnbb[:, 0:16])
                ctx.tap("rvb", rvb[:, 0:16])
                ctx.tap("glb", glb[:, 0:16])
                ctx.tap("betb", betb[:, 0:16])
                ctx.tap("xfm256", xfm256[:])
                ctx.tap("xlnd", xlnd[:])
                ctx.tap("wq0", wqkvz[:, 0:256])
                ctx.wait(nc.gpsimd, "dbg_st")
                barrier(ctx, "dbg_stage0")
                if DBG.get("l0a_stop"):
                    return

            d_ps = ExitStack()
            PS = []
            for s in range(2):
                pbig = d_ps.enter_context(
                    nc.psum_tensor("dB%d_%d_%d" % (b, q, s), [64, 512], F32))
                pTbs = d_ps.enter_context(
                    nc.psum_tensor("dT%d_%d_%d" % (b, q, s), [64, 256], BF16))
                pKSO = d_ps.enter_context(
                    nc.psum_tensor("dK%d_%d_%d" % (b, q, s), [128, 384], F32))
                PS.append(dict(
                    A=pbig[:, 0:64], B=pbig[:, 64:128], D=pbig[:, 128:192],
                    I0=pbig[:, 192:256], I1=pbig[:, 256:320], V=pbig[:, 192:320],
                    qrow=pbig[0:1, 320:384], tgc=pbig[0:1, 320:384],
                    tr1=pbig[0:1, 384:448], tr2m=pbig[0:1, 448:512],
                    rowsv=pbig[0:1, 320:512],
                    K=pKSO[:, 0:64], S=pKSO[:, 64:192], O=pKSO[:, 192:256],
                    X=pKSO[0:64, 64:128],
                    ss=pKSO[0:1, 256:320], qB=pKSO[:, 320:384],
                    Tk=pTbs[:, 0:128], Tv=pTbs[:, 128:256]))
            env["PS"] = PS
            pout = d_ps.enter_context(
                nc.psum_tensor("dO%d_%d" % (b, q), [128, 1024], F32))
            for grp in range(2):
                ig = b * NTT + 2 * q + grp
                pg = ig % 2
                for c64 in range(2):
                    for hp in range(4):
                        emit_delta_pair(ctx, ktiles, env, q, grp, hp, c64)
                ctx.wait(nc.tensor, "pv0")
                ctx.wait(nc.tensor, "pv1")
                if ctx.val("do_ev") > 0:
                    ctx.wait(nc.tensor, "do_ev", ctx.val("do_ev"))
                for bb in range(2):
                    for hh2 in range(8):
                        tin = dmm(pout[:, bb * 512:bb * 512 + 512],
                                  ogbuf[:, hh2 * 128:(hh2 + 1) * 128],
                                  woutb[:, hh2 * 1024 + bb * 512:hh2 * 1024 + bb * 512 + 512],
                                  start=(hh2 == 0), stop=(hh2 == 7))
                ctx.inc(tin, "do_mm", 1)
                ctx.wait(nc.vector, "do_mm")
                if ctx.val("d_st%d" % pg) > 0:
                    ctx.wait(nc.vector, "d_st%d" % pg)
                ctx.V.tensor_copy(fout[pg][:, 0:512], pout[:, 0:512])
                ctx.inc(ctx.V.tensor_copy(fout[pg][:, 512:1024], pout[:, 512:1024]),
                        "do_ev", 1)
                ctx.wait(nc.scalar, "do_ev")
                ctx.inc(ctx.S.activation((sh["scr"] if pg == 0 else sh["scr2"])[:], fout[pg][:],
                                         AF.Square, accum_out=sh["ssq"][:, ig:ig + 1]), "d_fsA", 1)
                ctx.wait(nc.vector, "d_fsA")
                ctx.inc(ctx.V.reduce_sum(sh["nsum"][:, ig:ig + 1], fout[pg][:], axis=AX.X,
                                         negate=True), "d_fsD", 1)
                ctx.wait(nc.gpsimd, "d_fsA")
                ctx.wait(nc.gpsimd, "d_fsD")
                ctx.inc(nc.gpsimd.dma_start(out=f_raw[ig * 128:(ig + 1) * 128, :],
                                            in_=fout[pg][:]), "d_st%d" % pg, 16)
            d_ps.close()
            barrier(ctx, "d_e%d_%d" % (b, q))

    emit_stats_finish(ctx, sh, mean_f, rstd_f, "d_")
    ctx.wait(nc.gpsimd, "d_st0")
    ctx.wait(nc.gpsimd, "d_st1")
    barrier(ctx, "d_bar")
    emit_residual_pass(ctx, sh, x_in, f_raw, x_out, mean_f, rstd_f, "dc_", "dA")


def emit_delta_pair(ctx, ktiles, L, q, grp, hp, c64):
    nc = L["nc"]
    V, S = ctx.V, ctx.S
    dmm = L["dmm"]
    toff = (grp * 2 + c64) * 64
    tloc = 2 * q + (toff // 128)
    po = toff % 128
    Sst, Sbf, ogbuf = L["Sst"], L["Sbf"], L["ogbuf"]
    SL, PS = L["SL"], L["PS"]
    LN128 = L["LN128"]
    sm = ("pm0", "pm1")
    sv = ("pv0", "pv1")
    sa = ("pa0", "pa1")
    heads = (2 * hp, 2 * hp + 1)
    qfv, kfv, vfv = {}, {}, {}
    for s in (0, 1):
        hh = heads[s]
        qfv[s] = L["qkvc"][:, hh * 256 + toff:hh * 256 + toff + 64]
        kfv[s] = L["qkvc"][:, (8 + hh) * 256 + toff:(8 + hh) * 256 + toff + 64]
        vfv[s] = L["qkvc"][:, (16 + hh) * 256 + toff:(16 + hh) * 256 + toff + 64]

    if hp == 0:
        for dst, srcb in ((L["cgB"], L["gcb"]), (L["crB"], L["rvb"]),
                          (L["clB"], L["lnbb"]), (L["cglB"], L["glb"]),
                          (L["cbtB"], L["betb"])):
            V.tensor_copy(dst[:], srcb[po:po + 64, tloc * 8:tloc * 8 + 8])

    for s in (0, 1):
        P = PS[s]
        if ctx.val(sv[s]):
            ctx.wait(nc.tensor, sv[s])
        if ctx.val(sa[s]):
            ctx.wait(nc.tensor, sa[s])
        dmm(P["A"], kfv[s], kfv[s])
        ctx.inc(dmm(P["B"], kfv[s], qfv[s]), sm[s], 1)
        nc.tensor.transpose(P["Tk"], kfv[s], ktiles["ident_b"][:])
        ctx.inc(nc.tensor.transpose(P["Tv"], vfv[s], ktiles["ident_b"][:]), sm[s], 1)
    for s in (0, 1):
        B = SL[s]; P = PS[s]
        ctx.wait(nc.scalar, sm[s])
        S.copy(B["ktm"][:], P["Tk"])
        ctx.inc(S.copy(B["vtm"][:], P["Tv"]), sa[s], 1)
    for s in (0, 1):
        B = SL[s]
        ctx.wait(nc.vector, sa[s])
        V.scalar_tensor_tensor(B["sqk"][:], B["ktm"][:], 1.0, B["ktm"][:],
                               ALU.mult, ALU.mult, accum_out=B["ssk"][:])
        ctx.inc(V.scalar_tensor_tensor(B["sqq"][:], qfv[s], 1.0, qfv[s],
                                       ALU.mult, ALU.mult), sv[s], 1)
    for s in (0, 1):
        B = SL[s]
        ctx.wait(nc.scalar, sv[s])
        S.activation(B["lnrkc"][:], B["ssk"][:], AF.Ln, bias=1e-6)
        ctx.inc(S.activation(B["lnrkc"][:], B["lnrkc"][:], AF.Identity, scale=-0.5),
                sa[s], 1)
    for s in (0, 1):
        B = SL[s]; P = PS[s]
        ctx.wait(nc.tensor, sv[s])
        ctx.inc(dmm(P["qrow"], ktiles["ones_col_b"][:], B["sqq"][:]), sm[s], 1)
    for s in (0, 1):
        B = SL[s]; P = PS[s]
        ctx.wait(nc.scalar, sm[s])
        S.activation(B["lnqr"][:], P["qrow"], AF.Ln, bias=1e-6)
        ctx.inc(S.activation(B["lnqr"][:], B["lnqr"][:], AF.Identity, scale=-0.5,
                             bias=-LN128), sa[s], 1)
    for s in (0, 1):
        hh = heads[s]; B = SL[s]
        ctx.wait(nc.vector, sa[s])
        V.tensor_tensor(B["r1c"][:], L["cgB"][:, hh:hh + 1], L["clB"][:, hh:hh + 1],
                        ALU.add)
        V.tensor_tensor(B["r1c"][:], B["r1c"][:], B["lnrkc"][:], ALU.add)
        ctx.inc(V.tensor_tensor(B["r2mc"][:], B["lnrkc"][:], L["cgB"][:, hh:hh + 1],
                                ALU.subtract), sv[s], 1)
    for s in (0, 1):
        hh = heads[s]; B = SL[s]; P = PS[s]
        ctx.wait(nc.tensor, sa[s])
        ctx.wait(nc.tensor, sv[s])
        idp = ktiles["ident_f"][0:64, 0:64]
        nc.tensor.transpose(P["tgc"], L["cgB"][:, hh:hh + 1], idp)
        nc.tensor.transpose(P["tr1"], B["r1c"][:], idp)
        ctx.inc(nc.tensor.transpose(P["tr2m"], B["r2mc"][:], idp), sm[s], 1)
    for s in (0, 1):
        B = SL[s]; P = PS[s]
        ctx.wait(nc.vector, sm[s])
        V.tensor_copy(B["rows"][:], P["rowsv"])
        gcr = B["rows"][0:1, 0:64]
        r1r = B["rows"][0:1, 64:128]
        r2r = B["rows"][0:1, 128:192]
        V.tensor_add(B["r3r"][:], gcr, B["lnqr"][:])
        V.tensor_copy(B["r1b"][:], r1r)
        V.tensor_copy(B["r2b"][:], r2r)
        V.tensor_copy(B["r3b"][:], B["r3r"][:])
        V.scalar_tensor_tensor(B["r1l"][:], r1r, 1.0, B["r1b"][:], ALU.mult,
                               ALU.subtract)
        V.scalar_tensor_tensor(B["r2l"][:], r2r, 1.0, B["r2b"][:], ALU.mult,
                               ALU.subtract)
        ctx.inc(V.scalar_tensor_tensor(B["r3l"][:], B["r3r"][:], 1.0, B["r3b"][:],
                                       ALU.mult, ALU.subtract), sv[s], 1)
    for s in (0, 1):
        B = SL[s]
        ctx.wait(nc.scalar, sv[s])
        S.activation(B["r3e"][:], B["r3r"][:], AF.Exp)
        ctx.inc(S.activation(B["e1c"][:], B["r1c"][:], AF.Exp), sa[s], 1)
    for k in range(3):
        t2, t2l = (("r2b", "r2l"), ("r1b", "r1l"), ("r3b", "r3l"))[k]
        msk = ("ut64i_b", "lt64i_b", "lt64s_b")[k]
        for s in (0, 1):
            B = SL[s]; P = PS[s]
            pD = P["D"] if k % 2 == 0 else P["I1"]
            ctx.wait(nc.tensor, sv[s])
            if k != 1:
                ctx.wait(nc.tensor, sa[s])
            dmm(pD, ktiles["ones_row_b"][0:1, 0:64], B[t2][:],
                start=True, stop=False)
            dmm(pD, ktiles["ones_row_b"][0:1, 0:64], B[t2l][:],
                start=False, stop=False)
            ctx.inc(dmm(pD, ktiles["negI64_b"][:], ktiles[msk][:],
                        start=False, stop=True), sm[s], 1)
        for s in (0, 1):
            B = SL[s]; P = PS[s]
            pD = P["D"] if k % 2 == 0 else P["I1"]
            dcyk = B["dcy"] if k % 2 == 0 else B["dcy2"]
            bias_col = B["r1c"] if k == 0 else B["r2mc"]
            ctx.wait(nc.scalar, sm[s])
            ctx.wait(nc.scalar, sv[s])
            ctx.inc(S.activation(dcyk[:], pD, AF.Exp, bias=bias_col[:]),
                    sa[s], 1)
        for s in (0, 1):
            B = SL[s]; P = PS[s]
            dcyk = B["dcy"] if k % 2 == 0 else B["dcy2"]
            dst = (B["tri"], B["triT"], B["attnT"])[k]
            asl = ("A", "A", "B")[k]
            ctx.wait(nc.vector, sa[s])
            ctx.inc(V.tensor_tensor(dst[:], P[asl], dcyk[:], ALU.mult), sv[s], 1)
    for s in (0, 1):
        B = SL[s]
        V.tensor_scalar(B["Pb"][0][:], B["triT"][:], -1.0, None, ALU.mult)
        V.tensor_scalar(B["Qb"][0][:], B["tri"][:], -1.0, None, ALU.mult)
        V.tensor_add(B["Xb"][0][:], B["Pb"][0][:], ktiles["eye64_b"][:])
        ctx.inc(V.tensor_add(B["Wb"][0][:], B["Qb"][0][:], ktiles["eye64_b"][:]),
                sv[s], 1)
    for s in (0, 1):
        B = SL[s]; P = PS[s]
        ctx.wait(nc.tensor, sv[s])
        ctx.wait(nc.tensor, sa[s])
        dmm(P["X"], ktiles["eye64_b"][:], B["Xb"][0][:], start=True, stop=False)
    cur, nxt = 0, 1
    for s_ in range(5):
        for s in (0, 1):
            B = SL[s]; P = PS[s]
            ctx.wait(nc.tensor, sv[s])
            ctx.wait(nc.tensor, sa[s])
            ctx.inc(dmm(P["I0"], B["Qb"][cur][:], B["Pb"][cur][:]), sm[s], 1)
        for s in (0, 1):
            B = SL[s]; P = PS[s]
            ctx.wait(nc.scalar, sm[s])
            ctx.inc(S.copy(B["Pb"][nxt][:], P["I0"]), sa[s], 1)
        for s in (0, 1):
            B = SL[s]; P = PS[s]
            ctx.wait(nc.tensor, sa[s])
            ctx.inc(dmm(P["X"], B["Wb"][cur][:], B["Pb"][nxt][:],
                        start=False, stop=(s_ == 4)), sm[s], 1)
        if s_ < 4:
            for s in (0, 1):
                B = SL[s]; P = PS[s]
                ctx.inc(dmm(P["I0"], B["Pb"][cur][:], B["Qb"][cur][:]), sm[s], 1)
            for s in (0, 1):
                B = SL[s]; P = PS[s]
                ctx.wait(nc.scalar, sm[s])
                ctx.inc(S.copy(B["Qb"][nxt][:], P["I0"]), sa[s], 1)
            for s in (0, 1):
                B = SL[s]; P = PS[s]
                ctx.wait(nc.tensor, sa[s])
                ctx.inc(dmm(P["I1"], B["Pb"][nxt][:], B["Wb"][cur][:]), sm[s], 1)
            for s in (0, 1):
                B = SL[s]; P = PS[s]
                ctx.wait(nc.vector, sm[s])
                ctx.inc(V.tensor_tensor(B["Wb"][nxt][:], P["I1"], B["Wb"][cur][:],
                                        ALU.add), sv[s], 1)
        cur, nxt = nxt, cur
    for s in (0, 1):
        B = SL[s]; P = PS[s]
        ctx.wait(nc.scalar, sm[s])
        ctx.inc(S.copy(B["Xb"][0][:], P["X"]), sa[s], 1)
    for s in (0, 1):
        hh = heads[s]; B = SL[s]
        ctx.wait(nc.vector, sa[s])
        V.tensor_scalar(B["vbeta"][:], B["vtm"][:], L["cbtB"][:, hh:hh + 1],
                        None, ALU.mult)
        ctx.inc(V.tensor_scalar(B["kbeg"][:], B["ktm"][:], B["e1c"][:],
                                None, ALU.mult), sv[s], 1)
    for s in (0, 1):
        B = SL[s]; P = PS[s]
        ctx.wait(nc.tensor, sv[s])
        ctx.wait(nc.tensor, sa[s])
        ctx.inc(dmm(P["V"], B["Xb"][0][:], B["vbeta"][:]), sm[s], 1)
    for s in (0, 1):
        B = SL[s]; P = PS[s]
        ctx.wait(nc.scalar, sm[s])
        ctx.inc(S.copy(B["vnew"][:], P["V"]), sa[s], 1)
    for s in (0, 1):
        B = SL[s]; P = PS[s]
        ctx.wait(nc.tensor, sa[s])
        ctx.inc(dmm(P["K"], B["kbeg"][:], B["Xb"][0][:]), sm[s], 1)
    for s in (0, 1):
        B = SL[s]; P = PS[s]
        ctx.wait(nc.scalar, sm[s])
        ctx.inc(S.copy(B["kcum"][:], P["K"]), sa[s], 1)
    for s in (0, 1):
        B = SL[s]; P = PS[s]
        ctx.inc(dmm(P["qB"], ktiles["ones_row_b"][0:1, :], B["r3e"][:]), sm[s], 1)
    for s in (0, 1):
        B = SL[s]; P = PS[s]
        ctx.wait(nc.vector, sm[s])
        ctx.inc(V.tensor_tensor(B["qeb"][:], qfv[s], P["qB"], ALU.mult), sv[s], 1)
    for s in (0, 1):
        hh = heads[s]; B = SL[s]; P = PS[s]
        ctx.wait(nc.tensor, sa[s])
        ctx.inc(dmm(P["V"], B["kcum"][:], Sbf[hh][:]), sm[s], 1)
    for s in (0, 1):
        B = SL[s]; P = PS[s]
        ctx.wait(nc.vector, sm[s])
        ctx.inc(V.tensor_tensor(B["vnb"][:], B["vnew"][:], P["V"], ALU.subtract),
                sv[s], 1)
    for s in (0, 1):
        hh = heads[s]; B = SL[s]; P = PS[s]
        ctx.wait(nc.tensor, sv[s])
        dmm(P["O"], Sbf[hh][:], B["qeb"][:], start=True, stop=False)
        ctx.inc(dmm(P["O"], B["vnb"][:], B["attnT"][:], start=False, stop=True),
                sm[s], 1)
    for s in (0, 1):
        hh = heads[s]; B = SL[s]
        V.tensor_tensor(B["cta"][:], L["crB"][:, hh:hh + 1], B["lnrkc"][:], ALU.add)
        V.tensor_copy(B["eglc"][0:64, :], L["cglB"][:, hh:hh + 1])
        ctx.inc(V.tensor_copy(B["eglc"][64:128, :], L["cglB"][:, hh:hh + 1]),
                sv[s], 1)
    for s in (0, 1):
        B = SL[s]
        ctx.wait(nc.scalar, sv[s])
        S.activation(B["ctb"][:], B["cta"][:], AF.Exp)
        ctx.inc(S.activation(B["eglc"][:], B["eglc"][:], AF.Exp), sa[s], 1)
    for s in (0, 1):
        B = SL[s]
        ctx.wait(nc.vector, sa[s])
        ctx.inc(V.tensor_scalar(B["kglb"][:], B["ktm"][:], B["ctb"][:],
                                None, ALU.mult), sv[s], 1)
    for s in (0, 1):
        B = SL[s]; P = PS[s]
        ctx.wait(nc.tensor, sv[s])
        ctx.inc(dmm(P["S"], B["kglb"][:], B["vnb"][:]), sm[s], 1)
    for s in (0, 1):
        hh = heads[s]; B = SL[s]; P = PS[s]
        ctx.wait(nc.vector, sm[s])
        ctx.inc(V.scalar_tensor_tensor(Sst[hh][:], Sst[hh][:], B["eglc"][:],
                                       P["S"], ALU.mult, ALU.add), sv[s], 1)
    for s in (0, 1):
        hh = heads[s]
        ctx.wait(nc.scalar, sv[s])
        ctx.inc(S.copy(Sbf[hh][:], Sst[hh][:]), sa[s], 1)
    for s in (0, 1):
        B = SL[s]; P = PS[s]
        ctx.wait(nc.scalar, sm[s])
        ctx.inc(S.activation(B["sqo"][:], P["O"], AF.Square), sa[s], 1)
    for s in (0, 1):
        B = SL[s]; P = PS[s]
        ctx.wait(nc.tensor, sa[s])
        ctx.inc(dmm(P["ss"], ktiles["ones_col_b"][:], B["sqo"][:]), sm[s], 1)
    for s in (0, 1):
        B = SL[s]; P = PS[s]
        ctx.wait(nc.scalar, sm[s])
        S.activation(B["rsr"][:], P["ss"], AF.Ln, bias=1e-6, scale=1.0 / 128.0)
        ctx.inc(S.activation(B["rsr"][:], B["rsr"][:], AF.Exp, scale=-0.5), sa[s], 1)
    for s in (0, 1):
        B = SL[s]
        ctx.wait(nc.vector, sa[s])
        ctx.inc(V.tensor_copy(B["rsb"][:], B["rsr"][:]), sv[s], 1)
    for s in (0, 1):
        B = SL[s]; P = PS[s]
        ctx.wait(nc.tensor, sv[s])
        ctx.inc(dmm(P["K"], ktiles["ones_row_b"][0:1, :], B["rsb"][:]), sm[s], 1)
    for s in (0, 1):
        hh = heads[s]; B = SL[s]; P = PS[s]
        szs = L["szfm"][:, hh * 256 + toff:hh * 256 + toff + 64]
        ctx.wait(nc.vector, sm[s])
        if ctx.val("do_mm"):
            ctx.wait(nc.vector, "do_mm")
        V.tensor_tensor(B["og1"][:], P["O"], szs, ALU.mult)
        ctx.inc(V.tensor_tensor(ogbuf[:, hh * 128 + c64 * 64:hh * 128 + c64 * 64 + 64],
                                B["og1"][:], P["K"], ALU.mult), sv[s], 1)


def build_program(phases):
    nc = bass.Bass("TRN2", target_bir_lowering=False, debug=False)
    x_in = nc.dram_tensor("x_in", [NTOK, D], F32, kind="ExternalInput").ap()
    x_out = nc.dram_tensor("x_out", [NTOK, D], F32, kind="ExternalOutput").ap()
    w = {}
    wdefs = {
        "l1f_w1": [4096, D], "l1f_b1": [4096], "l1f_w2": [D, 4096], "l1f_b2": [D],
        "l0g_A": [6, 100, 16], "l0g_B": [6, 16, D], "l0g_w_gate": [D, 6],
        "l0g_wt_w": [D, D], "l0g_wt_b": [D], "l0g_wd_w": [D, D], "l0g_wd_b": [D],
        "l0g_scale": [1],
        "l1a_w_attn": [D, 3072], "l1a_b_attn": [3072],
        "l1a_w_proj": [D, D], "l1a_b_proj": [D],
        "l0a_W_qkvz": [D, 4096], "l0a_W_ba": [D, 16], "l0a_conv_w": [3072, 1, 4],
        "l0a_dt_bias": [8], "l0a_A_log": [8], "l0a_norm_weight": [128],
        "l0a_W_out": [D, D],
    }
    need = set()
    for phname in phases:
        if phname == "l1_ffn":
            need |= {"l1f_w1", "l1f_b1", "l1f_w2", "l1f_b2"}
        if phname == "l0_ffn":
            need |= {"l0g_A", "l0g_B", "l0g_w_gate", "l0g_wt_w", "l0g_wt_b",
                     "l0g_wd_w", "l0g_wd_b", "l0g_scale"}
        if phname == "l1_attn":
            need |= {"l1a_w_attn", "l1a_b_attn", "l1a_w_proj", "l1a_b_proj"}
        if phname == "l0_attn":
            need |= {"l0a_W_qkvz", "l0a_W_ba", "l0a_conv_w", "l0a_dt_bias",
                     "l0a_A_log", "l0a_norm_weight", "l0a_W_out"}
    for k in sorted(need):
        w[k] = nc.dram_tensor(k, wdefs[k], F32, kind="ExternalInput").ap()

    es = ExitStack()
    with es:
        ctx = Ctx(nc, es)
        ctx.sem("g_bar_s")
        ctx.sem("g_bar_pe")
        with nc.body("main", valid_engines=mybir.ALL_ENGINES):
            ktiles = declare_and_load_consts(ctx)
            sh = alloc_shared(ctx)
            sh["f_raw"] = nc.dram_tensor("f_raw", [NTOK, D], F32, kind="Internal").ap()
            x_mid = [None, None]
            if len(phases) > 1:
                x_mid = [nc.dram_tensor(f"x_mid{i}", [NTOK, D], F32,
                                        kind="Internal").ap() for i in range(2)]
            for eng in (nc.sync, nc.vector, nc.scalar, nc.tensor, nc.gpsimd):
                ctx.wait(eng, "c_ld")
            cur_in = x_in
            for j, phname in enumerate(phases):
                is_last = (j == len(phases) - 1)
                dst = x_out if is_last else x_mid[j % 2]
                with ExitStack() as phase_es:
                    pctx = Ctx(nc, phase_es, parent=ctx)
                    if phname == "l1_ffn":
                        emit_l1_ffn(pctx, ktiles, sh, cur_in, dst, w)
                    elif phname == "l0_ffn":
                        emit_l0_ffn(pctx, ktiles, sh, cur_in, dst, w)
                    elif phname == "l1_attn":
                        emit_l1_attn(pctx, ktiles, sh, cur_in, dst, w)
                    elif phname == "l0_attn":
                        emit_l0_attn(pctx, ktiles, sh, cur_in, dst, w)
                    else:
                        raise ValueError(phname)
                barrier(ctx, f"ph{j}")
                if not is_last:
                    pctx.teardown_sems()
                cur_in = dst
    return nc, sorted(need)


def kernel_phases(phases, **inputs):
    x = np.ascontiguousarray(np.asarray(inputs["L0_attn_x"], np.float32))
    nc, wnames = build_program(phases)
    wmap = {
        "l1f_w1": inputs["L1_ffn_w1"], "l1f_b1": inputs["L1_ffn_b1"],
        "l1f_w2": inputs["L1_ffn_w2"], "l1f_b2": inputs["L1_ffn_b2"],
        "l0g_A": inputs["L0_ffn_A"], "l0g_B": inputs["L0_ffn_B"],
        "l0g_w_gate": inputs["L0_ffn_w_gate"], "l0g_wt_w": inputs["L0_ffn_wt_w"],
        "l0g_wt_b": inputs["L0_ffn_wt_b"], "l0g_wd_w": inputs["L0_ffn_wd_w"],
        "l0g_wd_b": inputs["L0_ffn_wd_b"], "l0g_scale": inputs["L0_ffn_scale"],
        "l1a_w_attn": inputs["L1_attn_w_attn"], "l1a_b_attn": inputs["L1_attn_b_attn"],
        "l1a_w_proj": inputs["L1_attn_w_proj"], "l1a_b_proj": inputs["L1_attn_b_proj"],
        "l0a_W_qkvz": inputs["L0_attn_W_qkvz"], "l0a_W_ba": inputs["L0_attn_W_ba"],
        "l0a_conv_w": inputs["L0_attn_conv_w"], "l0a_dt_bias": inputs["L0_attn_dt_bias"],
        "l0a_A_log": inputs["L0_attn_A_log"], "l0a_norm_weight": inputs["L0_attn_norm_weight"],
        "l0a_W_out": inputs["L0_attn_W_out"],
    }
    cmap = const_inmap()
    in_maps = []
    for c in range(NCORES):
        m = dict(cmap)
        for k in wnames:
            m[k] = np.ascontiguousarray(np.asarray(wmap[k], np.float32))
        m["x_in"] = np.ascontiguousarray(x[c * BPC:(c + 1) * BPC].reshape(NTOK, D))
        in_maps.append(m)
    res = run_bass_kernel_spmd(nc, in_maps, core_ids=list(range(NCORES)))
    out = np.concatenate([r["x_out"].reshape(BPC, T, D) for r in res.results], axis=0)
    return out


def kernel(**inputs):
    return kernel_phases(["l0_attn", "l0_ffn", "l1_attn", "l1_ffn"], **inputs)


# version 05
# speedup vs baseline: 1.2617x; 1.0521x over previous
import numpy as np
import ml_dtypes
from contextlib import ExitStack
import concourse.bass as bass
import concourse.mybir as mybir
from concourse.bass_utils import run_bass_kernel_spmd

F32 = mybir.dt.float32
BF16 = mybir.dt.bfloat16
AF = mybir.ActivationFunctionType
ALU = mybir.AluOpType
AX = mybir.AxisListType

DBG = {}

NCORES = 8
BPC = 2
T = 2048
D = 1024
NTOK = BPC * T
NT = NTOK // 128
LN_EPS = 1e-5


class EngProxy:

    def __init__(self, ctx, eng, key):
        self.ctx = ctx
        self.eng = eng
        self.key = key

    def __getattr__(self, name):
        fn = getattr(self.eng, name)

        def wrapped(*a, **kw):
            ctx = self.ctx
            ctx.finalize_pending(self.key)
            instr = fn(*a, **kw)
            last = ctx.last.get(self.key)
            if last is not None:
                instr._wait_ge(ctx.sems[last[0]][0], last[1])
            ctx.pending[self.key] = instr
            return instr
        return wrapped


class Ctx:

    ENG_KEYS = ("dve", "act", "pool")

    def __init__(self, nc, es, parent=None):
        self.nc = nc
        self.es = es
        self.handles = []
        self.names = []
        if parent is None:
            self.sems = {}
            self.last = {}
            self.pending = {}
            for k in self.ENG_KEYS:
                self.sem("q_" + k)
        else:
            self.sems = parent.sems
            self.last = parent.last
            self.pending = parent.pending
        self.V = EngProxy(self, nc.vector, "dve")
        self.S = EngProxy(self, nc.scalar, "act")
        self.G = EngProxy(self, nc.gpsimd, "pool")

    def sem(self, name):
        assert name not in self.sems, name
        h = self.nc.alloc_semaphore(name=name)
        self.handles.append(h)
        self.names.append(name)
        self.sems[name] = [h, 0]
        return h

    def teardown_sems(self):
        if not self.handles:
            return
        self.nc.clear_and_free_semaphores(self.handles)
        self.nc.all_engine_barrier()
        mine = set(self.names)
        for k in list(self.last.keys()):
            v = self.last.get(k)
            if v is not None and v[0] in mine:
                self.last[k] = None
        for nm_ in self.names:
            del self.sems[nm_]
        self.handles = []
        self.names = []

    def finalize_pending(self, key):
        instr = self.pending.get(key)
        if instr is not None:
            q = "q_" + key
            s = self.sems[q]
            instr.then_inc(s[0], 1)
            s[1] += 1
            self.last[key] = (q, s[1])
            self.pending[key] = None

    def finalize_all(self):
        for k in self.ENG_KEYS:
            self.finalize_pending(k)

    def inc(self, instr, name, k=1):
        s = self.sems[name]
        instr.then_inc(s[0], k)
        s[1] += k
        for key, pend in self.pending.items():
            if pend is instr:
                self.last[key] = (name, s[1])
                self.pending[key] = None
                break
        return instr

    def val(self, name):
        return self.sems[name][1]

    def wait(self, eng, name, value=None):
        if isinstance(eng, EngProxy):
            eng = eng.eng
        s = self.sems[name]
        v = s[1] if value is None else value
        if v > 0:
            eng.wait_ge(s[0], v)

    def tap(self, name, ap, wait_sems=()):
        nc = self.nc
        shape = list(ap.shape)
        out = nc.dram_tensor("dbg_" + name, shape, ap.dtype, kind="ExternalOutput").ap()
        for s in wait_sems:
            self.wait(nc.gpsimd, s)
        if "dbg_st" not in self.sems:
            self.sem("dbg_st")
        self.wait(nc.gpsimd, "dbg_st")
        self.inc(self.nc.gpsimd.dma_start(out=out, in_=ap), "dbg_st", 16)

    def sb(self, name, shape, dtype):
        return self.es.enter_context(self.nc.sbuf_tensor(name, shape, dtype))

    def ps(self, name, shape, dtype=F32):
        return self.es.enter_context(self.nc.psum_tensor(name, shape, dtype))


def barrier(ctx, name):
    nc = ctx.nc
    ctx.finalize_all()
    if "g_bar_s" not in ctx.sems:
        ctx.sem("g_bar_s")
        ctx.sem("g_bar_pe")
    ctx.inc(nc.tensor.nop(), "g_bar_pe", 1)
    nc.tensor.wait_ge(ctx.sems["g_bar_pe"][0], ctx.val("g_bar_pe"))
    for key, eng in (("dve", nc.vector), ("act", nc.scalar), ("pool", nc.gpsimd)):
        last = ctx.last.get(key)
        if last is not None:
            eng.wait_ge(ctx.sems[last[0]][0], last[1])
    engs = (nc.sync, nc.vector, nc.scalar, nc.tensor, nc.gpsimd)
    h = ctx.sems["g_bar_s"][0]
    for eng in engs:
        eng.sem_inc(h, 1)
    ctx.sems["g_bar_s"][1] += 5
    for eng in engs:
        eng.wait_ge(h, ctx.val("g_bar_s"))


def make_consts():
    c = {}
    c["ident_b"] = np.eye(128, dtype=np.float32).astype(ml_dtypes.bfloat16)
    c["ones_row_b"] = np.ones((1, 128), dtype=np.float32).astype(ml_dtypes.bfloat16)
    c["ones_row_f"] = np.ones((1, 128), dtype=np.float32)
    c["ident_f"] = np.eye(128, dtype=np.float32)
    c["ones_col_b"] = np.ones((128, 1), dtype=np.float32).astype(ml_dtypes.bfloat16)
    i64 = np.arange(64)
    ut64 = (i64[:, None] <= i64[None, :]).astype(np.float32)
    lt64s = (i64[:, None] > i64[None, :]).astype(np.float32)
    bd = lambda m: np.block([[m, np.zeros((64, 64), np.float32)],
                             [np.zeros((64, 64), np.float32), m]])
    c["bd_ut_f"] = bd(ut64)
    c["bd_lts_f"] = bd(lt64s)
    c["bd_on_f"] = bd(np.ones((64, 64), np.float32))
    c["ut64i_b"] = ut64.astype(ml_dtypes.bfloat16)
    c["lt64i_b"] = (i64[:, None] >= i64[None, :]).astype(np.float32).astype(ml_dtypes.bfloat16)
    c["lt64s_b"] = lt64s.astype(ml_dtypes.bfloat16)
    c["negI64_b"] = (-1e9 * np.eye(64, dtype=np.float32)).astype(ml_dtypes.bfloat16)
    i128 = np.arange(128)
    c["uts128_b"] = (i128[:, None] < i128[None, :]).astype(np.float32).astype(ml_dtypes.bfloat16)
    c["lts128_b"] = (i128[:, None] > i128[None, :]).astype(np.float32).astype(ml_dtypes.bfloat16)
    c["negI128_b"] = (-1e9 * np.eye(128, dtype=np.float32)).astype(ml_dtypes.bfloat16)
    c["neg1e9row_b"] = (-1e9 * np.ones((1, 512), dtype=np.float32)).astype(ml_dtypes.bfloat16)
    c["eye64x2_b"] = np.concatenate([np.eye(64), np.eye(64)], 0).astype(np.float32).astype(ml_dtypes.bfloat16)
    c["eye64_b"] = np.eye(64, dtype=np.float32).astype(ml_dtypes.bfloat16)
    return c


FLOAT_CONSTS = [1e-5, 1e-6, 1.0, 1.1920929e-07, -2.4260151319598084]


CONST_NP = make_consts()
CONST_NP["fvals"] = np.tile(np.asarray(FLOAT_CONSTS, np.float32), (128, 1))


def declare_and_load_consts(ctx):
    nc = ctx.nc
    tiles = {}
    ctx.sem("c_ld")
    for k, arr in CONST_NP.items():
        dt = BF16 if k.endswith("_b") else F32
        ap = nc.dram_tensor("c_" + k, list(arr.shape), dt, kind="ExternalInput").ap()
        tile = ctx.sb("k_" + k, list(arr.shape), dt)
        ctx.inc(nc.sync.dma_start(out=tile[:], in_=ap), "c_ld", 16)
        tiles[k] = tile
    for j, v in enumerate(FLOAT_CONSTS):
        nc.const_aps.register(F32, v, tiles["fvals"][:, j:j + 1])
    return tiles


def const_inmap():
    return {"c_" + k: np.asarray(v) for k, v in CONST_NP.items()}


def emit_ln_stats(ctx, sh, x_view, mean_t, rstd_t, tag):
    nc = ctx.nc
    ctx.sem(tag + "ld0"); ctx.sem(tag + "ld1")
    ctx.sem(tag + "rdA")
    ctx.sem(tag + "rdD")
    ctx.sem(tag + "finD"); ctx.sem(tag + "finA")
    xa = sh["stA"], sh["stB"]
    nsum, ssq, scr = sh["nsum"], sh["ssq"], sh["scr"]
    for i in range(NT):
        p = i % 2
        if i >= 2:
            ctx.wait(nc.sync, tag + "rdA", i - 1)
            ctx.wait(nc.sync, tag + "rdD", i - 1)
        ctx.inc(nc.sync.dma_start(out=xa[p][:], in_=x_view[i * 128:(i + 1) * 128, :]),
                tag + "ld%d" % p, 16)
        ctx.wait(nc.scalar, tag + "ld%d" % p)
        dump = scr if p == 0 else sh["scr2"]
        ctx.inc(ctx.S.activation(dump[:], xa[p][:], AF.Square,
                                     accum_out=ssq[:, i:i + 1]), tag + "rdA", 1)
        ctx.wait(nc.vector, tag + "ld%d" % p)
        ctx.inc(ctx.V.reduce_sum(nsum[:, i:i + 1], xa[p][:], axis=AX.X, negate=True),
                tag + "rdD", 1)
    ctx.wait(nc.vector, tag + "rdD")
    ctx.wait(nc.vector, tag + "rdA")
    ctx.V.tensor_scalar(mean_t[:], nsum[:], -1.0 / D, None, ALU.mult)
    ctx.V.tensor_mul(scr[:, 0:NT], mean_t[:], mean_t[:])
    ctx.inc(ctx.V.scalar_tensor_tensor(scr[:, NT:2 * NT], ssq[:], 1.0 / D,
                                           scr[:, 0:NT], ALU.mult, ALU.subtract), tag + "finD", 1)
    ctx.wait(nc.scalar, tag + "finD")
    ctx.S.activation(scr[:, 0:NT], scr[:, NT:2 * NT], AF.Ln, bias=float(LN_EPS))
    ctx.inc(ctx.S.activation(rstd_t[:], scr[:, 0:NT], AF.Exp, scale=-0.5), tag + "finA", 1)


def alloc_shared(ctx):
    sh = {}
    sh["stA"] = ctx.sb("sh_stA", [128, D], F32)
    sh["stB"] = ctx.sb("sh_stB", [128, D], F32)
    sh["scr"] = ctx.sb("sh_scr", [128, D], F32)
    sh["scr2"] = ctx.sb("sh_scr2", [128, D], F32)
    sh["nsum"] = ctx.sb("sh_nsum", [128, NT], F32)
    sh["ssq"] = ctx.sb("sh_ssq", [128, NT], F32)
    return sh



def emit_stats_finish(ctx, sh, mean_t, rstd_t, tag):
    nc = ctx.nc
    ctx.sem(tag + "vD"); ctx.sem(tag + "vA")
    ctx.wait(nc.vector, tag + "fsA")
    ctx.wait(nc.vector, tag + "fsD")
    ctx.V.tensor_scalar(mean_t[:], sh["nsum"][:], -1.0 / D, None, ALU.mult)
    ctx.V.tensor_mul(sh["scr"][:, 0:NT], mean_t[:], mean_t[:])
    ctx.inc(ctx.V.scalar_tensor_tensor(sh["scr"][:, NT:2 * NT], sh["ssq"][:], 1.0 / D,
                                       sh["scr"][:, 0:NT], ALU.mult, ALU.subtract), tag + "vD", 1)
    ctx.wait(nc.scalar, tag + "vD")
    ctx.S.activation(sh["scr"][:, 0:NT], sh["scr"][:, NT:2 * NT], AF.Ln, bias=float(LN_EPS))
    ctx.inc(ctx.S.activation(rstd_t[:], sh["scr"][:, 0:NT], AF.Exp, scale=-0.5), tag + "vA", 1)


def emit_residual_pass(ctx, sh, x_in, f_raw, x_out, mean_f, rstd_f, tag, pre_tag):
    nc = ctx.nc
    ctx.sem(tag + "lx0"); ctx.sem(tag + "lx1"); ctx.sem(tag + "lf0"); ctx.sem(tag + "lf1")
    ctx.sem(tag + "res"); ctx.sem(tag + "st0"); ctx.sem(tag + "st1")
    xc = [sh["stA"], sh["stB"]]
    fr = [ctx.sb(tag + "_fr0", [128, D], F32), ctx.sb(tag + "_fr1", [128, D], F32)]
    xnew = [ctx.sb(tag + "_xn0", [128, D], F32), ctx.sb(tag + "_xn1", [128, D], F32)]
    for i in range(NT):
        p = i % 2
        if i == 0 and pre_tag is not None:
            ctx.wait(nc.sync, pre_tag + "rdA")
            ctx.wait(nc.sync, pre_tag + "rdD")
        if i >= 2:
            ctx.wait(nc.sync, tag + "res", ctx.val(tag + "res") - 1)
        ctx.inc(nc.sync.dma_start(out=xc[p][:], in_=x_in[i * 128:(i + 1) * 128, :]),
                tag + "lx%d" % p, 16)
        if i >= 2:
            ctx.wait(nc.gpsimd, tag + "res", ctx.val(tag + "res") - 1)
        ctx.inc(nc.gpsimd.dma_start(out=fr[p][:], in_=f_raw[i * 128:(i + 1) * 128, :]),
                tag + "lf%d" % p, 16)
        ctx.wait(nc.vector, tag + "lx%d" % p)
        ctx.wait(nc.vector, tag + "lf%d" % p)
        if i >= 2:
            ctx.wait(nc.vector, tag + "st%d" % p)
        ctx.V.tensor_scalar(xnew[p][:], fr[p][:], mean_f[:, i:i + 1], rstd_f[:, i:i + 1],
                            ALU.subtract, ALU.mult)
        ctx.inc(ctx.V.tensor_add(xnew[p][:], xnew[p][:], xc[p][:]), tag + "res", 1)
        ctx.wait(nc.gpsimd, tag + "res")
        ctx.inc(nc.gpsimd.dma_start(out=x_out[i * 128:(i + 1) * 128, :], in_=xnew[p][:]),
                tag + "st%d" % p, 16)
    ctx.wait(nc.gpsimd, tag + "st0")
    ctx.wait(nc.gpsimd, tag + "st1")


def emit_l1_ffn(ctx, ktiles, sh, x_in, x_out, w):
    nc = ctx.nc
    es = ctx.es
    H = 4096

    mean_a = ctx.sb("f_meanA", [128, NT], F32)
    rstd_a = ctx.sb("f_rstdA", [128, NT], F32)
    mean_f = ctx.sb("f_meanF", [128, NT], F32)
    rstd_f = ctx.sb("f_rstdF", [128, NT], F32)
    f_raw = sh["f_raw"]

    emit_ln_stats(ctx, sh, x_in, mean_a, rstd_a, "fA")

    w1T = ctx.sb("f_w1T", [128, 8 * H], BF16)
    w2T = ctx.sb("f_w2T", [128, 32 * D], BF16)
    b1r = ctx.sb("f_b1r", [1, H], BF16)
    b2r = ctx.sb("f_b2r", [1, D], BF16)
    prep_es = ExitStack()
    wbuf = prep_es.enter_context(nc.sbuf_tensor("f_wb", [128, H], F32))
    sgn = prep_es.enter_context(nc.sbuf_tensor("f_sgn", [128, H], BF16))
    alpha = prep_es.enter_context(nc.sbuf_tensor("f_alpha", [128, 1], F32))
    wbin = prep_es.enter_context(nc.sbuf_tensor("f_wbin", [128, H], BF16))
    ptr = ctx.ps("f_ptr", [128, 1024], BF16)

    ctx.sem("fp_ld")
    ctx.sem("fp_sgn")
    ctx.sem("fp_bin")
    ctx.sem("fp_trs")
    ctx.sem("fp_ev")
    ctx.inc(nc.gpsimd.dma_start(out=b1r[:], in_=w["l1f_b1"].unsqueeze(0)), "fp_ld", 16)
    ctx.inc(nc.gpsimd.dma_start(out=b2r[:], in_=w["l1f_b2"].unsqueeze(0)), "fp_ld", 16)

    def prep_weight(wap, n_tiles, fd, dst, dst_d):
        for i in range(n_tiles):
            if i >= 1 or ctx.val("fp_bin") > 0:
                ctx.wait(nc.gpsimd, "fp_bin", ctx.val("fp_bin"))
            ctx.inc(nc.gpsimd.dma_start(out=wbuf[:, 0:fd],
                                        in_=wap[i * 128:(i + 1) * 128, :]), "fp_ld", 16)
            ctx.wait(nc.scalar, "fp_ld")
            if i >= 1 or ctx.val("fp_bin") > 0:
                ctx.wait(nc.scalar, "fp_bin", ctx.val("fp_bin"))
            ctx.inc(ctx.S.activation(sgn[:, 0:fd], wbuf[:, 0:fd], AF.Sign),
                    "fp_sgn", 1)
            ctx.wait(nc.vector, "fp_ld")
            ctx.V.reduce_sum(alpha[:], wbuf[:, 0:fd], axis=AX.X,
                                 apply_absolute_value=True)
            ctx.wait(nc.vector, "fp_sgn")
            if i >= 1 or ctx.val("fp_trs") > 0:
                ctx.wait(nc.vector, "fp_trs", ctx.val("fp_trs"))
            ctx.inc(ctx.V.tensor_scalar(wbin[:, 0:fd], sgn[:, 0:fd], alpha[:],
                                            1.0 / fd, ALU.mult, ALU.mult), "fp_bin", 1)
            nchunk = fd // 128
            ctx.wait(nc.tensor, "fp_bin")
            dview = dst[:, :].rearrange("p (cc x) -> p cc x", x=dst_d)
            for c in range(nchunk):
                if c % 8 == 0 and ctx.val("fp_ev") > 0:
                    ctx.wait(nc.tensor, "fp_ev")
                tin = nc.tensor.transpose(ptr[:, (c % 8) * 128:(c % 8) * 128 + 128],
                                          wbin[:, c * 128:c * 128 + 128], ktiles["ident_b"][:])
                if c % 8 == 7 or c == nchunk - 1:
                    ctx.inc(tin, "fp_trs", 1)
                    ctx.wait(nc.vector, "fp_trs")
                    lo = (c // 8) * 8
                    cnt = c - lo + 1
                    ctx.inc(ctx.V.tensor_copy(
                        dview[:, lo:lo + cnt, i * 128:(i + 1) * 128],
                        ptr[:, 0:cnt * 128].rearrange("p (cc x) -> p cc x", x=128)),
                        "fp_ev", 1)

    prep_weight(w["l1f_w1"], 32, D, w1T, H)
    prep_weight(w["l1f_w2"], 8, H, w2T, D)
    prep_es.close()

    passb_es = ExitStack()
    _s = lambda n, sh_, dt_: passb_es.enter_context(nc.sbuf_tensor(n, sh_, dt_))
    xb = [_s("f_xb0", [128, D], F32), _s("f_xb1", [128, D], F32)]
    xln = [_s("f_xln0", [128, D], BF16), _s("f_xln1", [128, D], BF16)]
    xfm = [_s("f_xfm0", [128, D], BF16), _s("f_xfm1", [128, D], BF16)]
    hbf = _s("f_hbf", [128, H], BF16)
    hfm = _s("f_hfm", [128, H], BF16)
    fout = [_s("f_fo0", [128, D], F32), _s("f_fo1", [128, D], F32)]
    ph = ctx.ps("f_ph", [128, 2048], F32)
    po = ctx.ps("f_po", [128, 1024], F32)

    ctx.sem("fb_ld0"); ctx.sem("fb_ld1")
    bln = ctx.sem("fb_ln"); btrx = ctx.sem("fb_trx")
    bfm = ctx.sem("fb_fm"); bh = ctx.sem("fb_h"); bg = ctx.sem("fb_g")
    btrh = ctx.sem("fb_trh"); bhfm = ctx.sem("fb_hfm"); bom = ctx.sem("fb_om")
    boe = ctx.sem("fb_oe"); ctx.sem("fb_fsA"); ctx.sem("fb_fsD")
    ctx.sem("fb_st0"); ctx.sem("fb_st1")

    for i in range(NT):
        p = i % 2
        if i == 0:
            ctx.wait(nc.sync, "fp_ev")
        if i >= 2:
            ctx.wait(nc.sync, "fb_ln", ctx.val("fb_ln") - 1)
        ctx.inc(nc.sync.dma_start(out=xb[p][:], in_=x_in[i * 128:(i + 1) * 128, :]),
                "fb_ld%d" % p, 16)
        ctx.wait(nc.vector, "fb_ld%d" % p)
        if i == 0:
            ctx.wait(nc.vector, "fAfinA", 1)
        if i >= 2:
            ctx.wait(nc.vector, "fb_trx", ctx.val("fb_trx") - 1)
        ctx.inc(ctx.V.tensor_scalar(xln[p][:], xb[p][:], mean_a[:, i:i + 1],
                                        rstd_a[:, i:i + 1], ALU.subtract, ALU.mult), "fb_ln", 1)
        ctx.wait(nc.tensor, "fb_ln")
        if i == 0:
            ctx.wait(nc.tensor, "fp_ev")
        ctx.wait(nc.tensor, "fb_hfm")
        for c in range(8):
            tin = nc.tensor.transpose(ptr[:, c * 128:c * 128 + 128],
                                      xln[p][:, c * 128:c * 128 + 128], ktiles["ident_b"][:])
        ctx.inc(tin, "fb_trx", 1)
        ctx.wait(nc.vector, "fb_trx")
        if i >= 2:
            ctx.wait(nc.vector, "fb_h", ctx.val("fb_h") - 1)
        ctx.inc(ctx.V.tensor_copy(xfm[p][:], ptr[:]), "fb_fm", 1)
        ctx.wait(nc.tensor, "fb_fm")
        if i == 0:
            ctx.wait(nc.tensor, "fp_ld")
        if i >= 2:
            ctx.wait(nc.tensor, "fb_trh", ctx.val("fb_trh") - 4)
        for half in range(2):
            if not (i == 0 and half == 0):
                ctx.wait(nc.tensor, "fb_g", ctx.val("fb_g"))
            for b in range(4):
                col0 = half * 2048 + b * 512
                for kc in range(8):
                    nc.tensor.matmul(ph[:, b * 512:b * 512 + 512], xfm[p][:, kc * 128:kc * 128 + 128],
                                     w1T[:, kc * H + col0:kc * H + col0 + 512],
                                     start=(kc == 0), stop=False)
                tin = nc.tensor.matmul(ph[:, b * 512:b * 512 + 512], ktiles["ones_row_b"][:],
                                       b1r[:, col0:col0 + 512], start=False, stop=True)
            ctx.inc(tin, "fb_h", 1)
            ctx.wait(nc.scalar, "fb_h")
            if i >= 1 and half == 0:
                ctx.wait(nc.scalar, "fb_trh", ctx.val("fb_trh"))
            ctx.inc(ctx.S.activation(hbf[:, half * 2048:half * 2048 + 2048], ph[:],
                                         AF.Gelu), "fb_g", 1)
        ctx.wait(nc.tensor, "fb_g")
        for grp in range(4):
            ctx.wait(nc.tensor, "fb_fm" if (i == 0 and grp == 0) else "fb_hfm")
            for c in range(8):
                cc = grp * 8 + c
                tin = nc.tensor.transpose(ptr[:, c * 128:c * 128 + 128],
                                          hbf[:, cc * 128:cc * 128 + 128], ktiles["ident_b"][:])
            ctx.inc(tin, "fb_trh", 1)
            ctx.wait(nc.vector, "fb_trh")
            if i >= 1:
                ctx.wait(nc.vector, "fb_om", ctx.val("fb_om"))
            ctx.inc(ctx.V.tensor_copy(hfm[:, grp * 1024:grp * 1024 + 1024], ptr[:]),
                    "fb_hfm", 1)
        ctx.wait(nc.tensor, "fb_hfm")
        if i >= 1:
            ctx.wait(nc.tensor, "fb_oe", ctx.val("fb_oe"))
        for b in range(2):
            for kc in range(32):
                nc.tensor.matmul(po[:, b * 512:b * 512 + 512], hfm[:, kc * 128:kc * 128 + 128],
                                 w2T[:, kc * D + b * 512:kc * D + b * 512 + 512],
                                 start=(kc == 0), stop=False)
            tin = nc.tensor.matmul(po[:, b * 512:b * 512 + 512], ktiles["ones_row_b"][:],
                                   b2r[:, b * 512:b * 512 + 512], start=False, stop=True)
        ctx.inc(tin, "fb_om", 1)
        ctx.wait(nc.vector, "fb_om")
        if i >= 2:
            ctx.wait(nc.vector, "fb_st%d" % p)
        ctx.inc(ctx.V.tensor_copy(fout[p][:], po[:]), "fb_oe", 1)
        ctx.wait(nc.scalar, "fb_oe")
        ctx.inc(ctx.S.activation((sh["scr"] if p == 0 else sh["scr2"])[:], fout[p][:],
                                     AF.Square, accum_out=sh["ssq"][:, i:i + 1]), "fb_fsA", 1)
        ctx.wait(nc.vector, "fb_oe")
        ctx.inc(ctx.V.reduce_sum(sh["nsum"][:, i:i + 1], fout[p][:], axis=AX.X,
                                     negate=True), "fb_fsD", 1)
        ctx.wait(nc.gpsimd, "fb_fsA")
        ctx.wait(nc.gpsimd, "fb_fsD")
        ctx.inc(nc.gpsimd.dma_start(out=f_raw[i * 128:(i + 1) * 128, :], in_=fout[p][:]),
                "fb_st%d" % p, 16)

    emit_stats_finish(ctx, sh, mean_f, rstd_f, "fb_")
    passb_es.close()
    ctx.wait(nc.gpsimd, "fb_st0")
    ctx.wait(nc.gpsimd, "fb_st1")
    barrier(ctx, "f_bar")
    emit_residual_pass(ctx, sh, x_in, f_raw, x_out, mean_f, rstd_f, "fc_", "fA")



def emit_l0_ffn(ctx, ktiles, sh, x_in, x_out, w):
    nc = ctx.nc
    E, L, R = 6, 100, 16

    mean_a = ctx.sb("g_meanA", [128, NT], F32)
    rstd_a = ctx.sb("g_rstdA", [128, NT], F32)
    mean_f = ctx.sb("g_meanF", [128, NT], F32)
    rstd_f = ctx.sb("g_rstdF", [128, NT], F32)
    f_raw = sh["f_raw"]

    emit_ln_stats(ctx, sh, x_in, mean_a, rstd_a, "gA")

    tokfm = ctx.sb("g_tokfm", [128, 8 * 600], BF16)
    wtT = ctx.sb("g_wtT", [128, 8 * D], BF16)
    wdT = ctx.sb("g_wdT", [128, 8 * D], BF16)
    wtb = ctx.sb("g_wtb", [1, D], BF16)
    wdb = ctx.sb("g_wdb", [1, D], BF16)
    wg = ctx.sb("g_wg", [128, 8 * 6], BF16)
    val0 = ctx.sb("g_val0", [128, E * D], BF16)
    scale128 = ctx.sb("g_s128", [128, 1], F32)
    at = ctx.sb("g_at", [16, 600], BF16)
    bt = [ctx.sb("g_bt%d" % e, [16, D], BF16) for e in range(E)]
    ptr = ctx.ps("g_ptr", [128, 1024], BF16)

    prep_es = ExitStack()
    wld = prep_es.enter_context(nc.sbuf_tensor("g_wld", [128, D], F32))
    ssb = prep_es.enter_context(nc.sbuf_tensor("g_ssb", [1, 1], F32))

    ctx.sem("gp_ld")
    ctx.sem("gp_tok")
    ctx.sem("gp_tev")
    ctx.sem("gp_trs"); ctx.sem("gp_ev")
    ctx.sem("gp_val"); ctx.sem("gp_vev")
    ctx.sem("gp_s")
    g = nc.gpsimd

    def gld(out_ap, in_ap, **kw):
        ctx.wait(nc.gpsimd, "gp_ld")
        ctx.inc(g.dma_start(out=out_ap, in_=in_ap, **kw), "gp_ld", 16)

    gld(wtb[:], w["l0g_wt_b"].unsqueeze(0))
    gld(wdb[:], w["l0g_wd_b"].unsqueeze(0))
    gld(ssb[:], w["l0g_scale"].unsqueeze(0))
    gld(wg[:].rearrange("p (a c) -> p a c", c=6),
        w["l0g_w_gate"].rearrange("(a b) c -> b a c", b=128))
    gld(at[:], w["l0g_A"].rearrange("e l r -> r (e l)"), allow_slow_non_contiguous=True)
    for e in range(E):
        gld(bt[e][:], w["l0g_B"][e])

    prep_ps = ExitStack()
    psc = prep_ps.enter_context(nc.psum_tensor("g_psc", [128, 8], F32))
    ctx.wait(nc.tensor, "gp_ld")
    tin = nc.tensor.matmul(psc[:, 0:1], ktiles["ones_row_f"][:], ssb[:], start=True, stop=True)
    ctx.inc(tin, "gp_s", 1)
    ctx.wait(nc.vector, "gp_s")
    ctx.inc(ctx.V.tensor_copy(scale128[:], psc[:, 0:1]), "gp_s", 1)

    ptok = prep_ps.enter_context(nc.psum_tensor("g_ptok", [128, 768], F32))
    for kc in range(8):
        for e in range(E):
            tin = nc.tensor.matmul(ptok[:, e * 128:e * 128 + 100], bt[e][:, kc * 128:(kc + 1) * 128],
                                   at[:, e * 100:(e + 1) * 100], start=True, stop=True)
        ctx.inc(tin, "gp_tok", 1)
        ctx.wait(nc.vector, "gp_tok")
        ctx.inc(ctx.V.tensor_copy(
            tokfm[:, kc * 600:(kc + 1) * 600].rearrange("p (e l) -> p e l", l=100),
            ptok[:].rearrange("p (e l) -> p e l", l=128)[:, :, 0:100]), "gp_tev", 1)
        if kc < 7:
            ctx.wait(nc.tensor, "gp_tev")
    ptrf = prep_ps.enter_context(nc.psum_tensor("g_ptrf", [128, 1024], F32))
    def prep_t2(wap, dst):
        for i in range(8):
            if ctx.val("gp_trs") > 0:
                ctx.wait(nc.gpsimd, "gp_trs", ctx.val("gp_trs"))
            gld(wld[:], wap[i * 128:(i + 1) * 128, :])
            ctx.wait(nc.tensor, "gp_ld")
            if ctx.val("gp_ev") > 0:
                ctx.wait(nc.tensor, "gp_ev")
            for c in range(8):
                tin = nc.tensor.transpose(ptrf[:, c * 128:c * 128 + 128],
                                          wld[:, c * 128:c * 128 + 128], ktiles["ident_f"][:])
            ctx.inc(tin, "gp_trs", 1)
            ctx.wait(nc.vector, "gp_trs")
            dv = dst[:, :].rearrange("p (cc x) -> p cc x", x=D)
            ctx.inc(ctx.V.tensor_copy(dv[:, :, i * 128:(i + 1) * 128],
                                      ptrf[:].rearrange("p (cc x) -> p cc x", x=128)), "gp_ev", 1)
    prep_t2(w["l0g_wt_w"], wtT)

    pval = prep_ps.enter_context(nc.psum_tensor("g_pval", [128, 1024], F32))
    ctx.wait(nc.tensor, "gp_ev")
    for e in range(E):
        ctx.wait(nc.tensor, "gp_tev")
        for b in range(2):
            for kc in range(8):
                nc.tensor.matmul(pval[0:100, b * 512:b * 512 + 512],
                                 tokfm[:, kc * 600 + e * 100:kc * 600 + (e + 1) * 100],
                                 wtT[:, kc * D + b * 512:kc * D + b * 512 + 512],
                                 start=(kc == 0), stop=False)
            tin = nc.tensor.matmul(pval[0:100, b * 512:b * 512 + 512],
                                   ktiles["ones_row_b"][:, 0:100],
                                   wtb[:, b * 512:b * 512 + 512], start=False, stop=True)
        ctx.inc(tin, "gp_val", 1)
        ctx.wait(nc.scalar, "gp_val")
        ctx.inc(ctx.S.copy(val0[0:100, e * D:(e + 1) * D], pval[0:100, :]), "gp_vev", 1)
        if e < E - 1:
            ctx.wait(nc.tensor, "gp_vev")
    ctx.wait(nc.vector, "gp_vev")
    ctx.inc(ctx.V.memset(val0[0:1, :], 0.0), "gp_vev", 1)

    prep_t2(w["l0g_wd_w"], wdT)
    prep_es.close()
    prep_ps.close()

    if DBG.get("l0g"):
        ctx.tap("s128", scale128[:], ["gp_s"])

    passb_es = ExitStack()
    _s = lambda n, sh_, dt_: passb_es.enter_context(nc.sbuf_tensor(n, sh_, dt_))
    xb = [_s("g_xb0", [128, D], F32), _s("g_xb1", [128, D], F32)]
    xln = [_s("g_xln0", [128, D], BF16), _s("g_xln1", [128, D], BF16)]
    xlnf = [_s("g_xlnf0", [128, D], F32), _s("g_xlnf1", [128, D], F32)]
    xfm = [_s("g_xfm0", [128, D], BF16), _s("g_xfm1", [128, D], BF16)]
    gl = _s("g_gl", [128, 8], F32)
    gmx = _s("g_gmx", [128, 1], F32)
    gates = _s("g_gates", [128, 8], F32)
    s1 = _s("g_s1", [128, 600], F32)
    ebf = _s("g_ebf", [128, 600], BF16)
    sums = _s("g_sums", [128, 8], F32)
    grow = _s("g_grow", [128, 8], F32)
    ag = _s("g_ag", [128, 600], BF16)
    atg = _s("g_atg", [128, 768], BF16)
    dsx = _s("g_dsx", [128, D], BF16)
    dsxf = _s("g_dsxf", [128, D], BF16)
    esb = _s("g_esb", [128, D], F32)
    emax = _s("g_emax", [128, D], F32)
    fout = [_s("g_fo0", [128, D], F32), _s("g_fo1", [128, D], F32)]
    pz = ctx.ps("g_pz", [128, 600], F32)
    pl = ctx.ps("g_pl", [128, 8], F32)
    pds = ctx.ps("g_pds", [128, 1024], F32)
    pdl = ctx.ps("g_pdl", [128, 1024], F32)

    ctx.sem("gb_ld0"); ctx.sem("gb_ld1")
    ctx.sem("gb_ln"); ctx.sem("gb_trx"); ctx.sem("gb_fm")
    ctx.sem("gb_z"); ctx.sem("gb_l"); ctx.sem("gb_gev"); ctx.sem("gb_sm")
    ctx.sem("gb_ta"); ctx.sem("gb_tat"); ctx.sem("gb_taev"); ctx.sem("gb_ds"); ctx.sem("gb_dsx")
    ctx.sem("gb_trd"); ctx.sem("gb_dfm"); ctx.sem("gb_dl"); ctx.sem("gb_ee"); ctx.sem("gb_eexp")
    ctx.sem("gb_foD"); ctx.sem("gb_foA"); ctx.sem("gb_fo2"); ctx.sem("gb_fsA"); ctx.sem("gb_fsD")
    ctx.sem("gb_st0"); ctx.sem("gb_st1")

    for i in range(NT):
        p = i % 2
        if i == 0:
            ctx.wait(nc.sync, "gp_trs")
            ctx.wait(nc.sync, "gp_s")
        if i >= 2:
            ctx.wait(nc.sync, "gb_ln", ctx.val("gb_ln") - 1)
        ctx.inc(nc.sync.dma_start(out=xb[p][:], in_=x_in[i * 128:(i + 1) * 128, :]),
                "gb_ld%d" % p, 16)
        ctx.wait(nc.vector, "gb_ld%d" % p)
        if i == 0:
            ctx.wait(nc.vector, "gAfinA", 1)
        if i >= 2:
            ctx.wait(nc.vector, "gb_ds", ctx.val("gb_ds") - 1)
        ctx.V.tensor_scalar(xlnf[p][:], xb[p][:], mean_a[:, i:i + 1],
                            rstd_a[:, i:i + 1], ALU.subtract, ALU.mult)
        ctx.inc(ctx.V.tensor_copy(xln[p][:], xlnf[p][:]), "gb_ln", 1)
        ctx.wait(nc.tensor, "gb_ln")
        if i == 0:
            ctx.wait(nc.tensor, "gp_vev")
            ctx.wait(nc.tensor, "gp_ev")
            ctx.wait(nc.tensor, "gp_s")
        ctx.wait(nc.tensor, "gb_dfm")
        for c in range(8):
            tin = nc.tensor.transpose(ptr[:, c * 128:c * 128 + 128],
                                      xln[p][:, c * 128:c * 128 + 128], ktiles["ident_b"][:])
        ctx.inc(tin, "gb_trx", 1)
        ctx.wait(nc.vector, "gb_trx")
        if i >= 2:
            ctx.wait(nc.vector, "gb_dl", ctx.val("gb_dl") - 1)
        ctx.inc(ctx.V.tensor_copy(xfm[p][:], ptr[:]), "gb_fm", 1)
        ctx.wait(nc.tensor, "gb_fm")
        if i >= 1:
            ctx.wait(nc.tensor, "gb_gev", ctx.val("gb_gev"))
            ctx.wait(nc.tensor, "gb_sm", ctx.val("gb_sm"))
        for kc in range(8):
            tin = nc.tensor.matmul(pl[:, 0:6], xfm[p][:, kc * 128:kc * 128 + 128],
                                   wg[:, kc * 6:kc * 6 + 6], start=(kc == 0), stop=(kc == 7))
        ctx.inc(tin, "gb_l", 1)
        for kc in range(8):
            nc.tensor.matmul(pz[:, 0:512], xfm[p][:, kc * 128:kc * 128 + 128],
                             tokfm[:, kc * 600:kc * 600 + 512], start=(kc == 0), stop=(kc == 7))
            tin = nc.tensor.matmul(pz[:, 512:600], xfm[p][:, kc * 128:kc * 128 + 128],
                                   tokfm[:, kc * 600 + 512:kc * 600 + 600],
                                   start=(kc == 0), stop=(kc == 7))
        ctx.inc(tin, "gb_z", 1)
        ctx.wait(nc.scalar, "gb_l")
        if i >= 1:
            ctx.wait(nc.scalar, "gb_sm", ctx.val("gb_sm"))
        ctx.inc(ctx.S.copy(gl[:, 0:6], pl[:, 0:6]), "gb_gev", 1)
        ctx.wait(nc.vector, "gb_gev")
        ctx.V.reduce_max(gmx[:], gl[:, 0:6], axis=AX.X)
        ctx.V.tensor_tensor(gates[:, 0:6], gl[:, 0:6], gmx[:].broadcast_to([128, 6]), ALU.is_ge)
        ctx.wait(nc.vector, "gb_z")
        ctx.V.reduce_max(sums[:, 0:6], pz[:].rearrange("p (e l) -> p e l", e=6), axis=AX.X)
        ctx.inc(ctx.V.tensor_tensor(s1[:].rearrange("p (e l) -> p e l", e=6),
                                    pz[:].rearrange("p (e l) -> p e l", e=6),
                                    sums[:, 0:6].unsqueeze(2).broadcast_to([128, 6, 100]),
                                    ALU.subtract), "gb_sm", 1)
        ctx.wait(nc.scalar, "gb_sm")
        ctx.inc(ctx.S.activation(ebf[:], s1[:], AF.Exp, scale=1.0 / 32.0), "gb_eexp", 1)
        ctx.wait(nc.vector, "gb_eexp")
        ctx.V.reduce_sum(sums[:, 0:6], ebf[:].rearrange("p (e l) -> p e l", e=6), axis=AX.X)
        ctx.V.reciprocal(grow[:, 0:6], sums[:, 0:6])
        ctx.V.tensor_mul(grow[:, 0:6], grow[:, 0:6], gates[:, 0:6])
        if i >= 1:
            ctx.wait(nc.vector, "gb_tat", ctx.val("gb_tat"))
        ctx.inc(ctx.V.tensor_tensor(ag[:].rearrange("p (e l) -> p e l", e=6),
                                    ebf[:].rearrange("p (e l) -> p e l", e=6),
                                    grow[:, 0:6].unsqueeze(2).broadcast_to([128, 6, 100]),
                                    ALU.mult), "gb_ta", 1)
        ctx.wait(nc.tensor, "gb_ta")
        ctx.wait(nc.tensor, "gb_fm")
        for e in range(E):
            tin = nc.tensor.transpose(ptr[0:100, e * 128:e * 128 + 128],
                                      ag[:, e * 100:(e + 1) * 100], ktiles["ident_b"][:])
        ctx.inc(tin, "gb_tat", 1)
        ctx.wait(nc.vector, "gb_tat")
        if i >= 1:
            ctx.wait(nc.vector, "gb_ds", ctx.val("gb_ds"))
        ctx.inc(ctx.V.tensor_copy(atg[0:100, :], ptr[0:100, 0:768]), "gb_taev", 1)
        ctx.wait(nc.tensor, "gb_taev")
        if i >= 1:
            ctx.wait(nc.tensor, "gb_dsx", ctx.val("gb_dsx"))
        for e in range(E):
            for b in range(2):
                tin = nc.tensor.matmul(pds[:, b * 512:b * 512 + 512],
                                       atg[0:100, e * 128:e * 128 + 128],
                                       val0[0:100, e * D + b * 512:e * D + b * 512 + 512],
                                       start=(e == 0), stop=(e == E - 1))
        ctx.inc(tin, "gb_ds", 1)
        ctx.wait(nc.vector, "gb_ds")
        if i >= 1:
            ctx.wait(nc.vector, "gb_trd", ctx.val("gb_trd"))
        ctx.inc(ctx.V.tensor_tensor(dsx[:], pds[:], xln[p][:], ALU.add), "gb_dsx", 1)
        ctx.wait(nc.tensor, "gb_dsx")
        ctx.wait(nc.tensor, "gb_taev")
        for c in range(8):
            tin = nc.tensor.transpose(ptr[:, c * 128:c * 128 + 128],
                                      dsx[:, c * 128:c * 128 + 128], ktiles["ident_b"][:])
        ctx.inc(tin, "gb_trd", 1)
        ctx.wait(nc.vector, "gb_trd")
        if i >= 1:
            ctx.wait(nc.vector, "gb_dl", ctx.val("gb_dl"))
        ctx.inc(ctx.V.tensor_copy(dsxf[:], ptr[:]), "gb_dfm", 1)
        ctx.wait(nc.tensor, "gb_dfm")
        if i >= 1:
            ctx.wait(nc.tensor, "gb_ee", ctx.val("gb_ee"))
        for b in range(2):
            for kc in range(8):
                nc.tensor.matmul(pdl[:, b * 512:b * 512 + 512], dsxf[:, kc * 128:kc * 128 + 128],
                                 wdT[:, kc * D + b * 512:kc * D + b * 512 + 512],
                                 start=(kc == 0), stop=False)
            tin = nc.tensor.matmul(pdl[:, b * 512:b * 512 + 512], ktiles["ones_row_b"][:],
                                   wdb[:, b * 512:b * 512 + 512], start=False, stop=True)
        ctx.inc(tin, "gb_dl", 1)
        ctx.wait(nc.scalar, "gb_dl")
        if i >= 1:
            ctx.wait(nc.scalar, "gb_fo2", ctx.val("gb_fo2"))
        ctx.inc(ctx.S.activation(esb[:], pdl[:], AF.Exp, scale=scale128[:]), "gb_ee", 1)
        ctx.wait(nc.vector, "gb_ee")
        ctx.inc(ctx.V.tensor_scalar(emax[:], esb[:], 1.1920929e-07, None, ALU.max), "gb_foD", 1)
        ctx.wait(nc.scalar, "gb_foD")
        ctx.inc(ctx.S.activation(esb[:], emax[:], AF.Ln), "gb_foA", 1)
        ctx.wait(nc.vector, "gb_foA")
        if i >= 2:
            ctx.wait(nc.vector, "gb_st%d" % p)
        ctx.inc(ctx.V.tensor_add(fout[p][:], esb[:], xlnf[p][:]), "gb_fo2", 1)
        ctx.wait(nc.scalar, "gb_fo2")
        ctx.inc(ctx.S.activation((sh["scr"] if p == 0 else sh["scr2"])[:], fout[p][:],
                                 AF.Square, accum_out=sh["ssq"][:, i:i + 1]), "gb_fsA", 1)
        ctx.inc(ctx.V.reduce_sum(sh["nsum"][:, i:i + 1], fout[p][:], axis=AX.X,
                                 negate=True), "gb_fsD", 1)
        if i == 0 and DBG.get("l0g"):
            ctx.tap("xln0", xb[p][:], ["gb_ln"])
            ctx.tap("gl0", gl[:, 0:6], ["gb_gev"])
            ctx.tap("gates0", gates[:, 0:6], ["gb_sm"])
            ctx.tap("s10", s1[:], ["gb_sm"])
            ctx.tap("esb0", esb[:], ["gb_ee"])
            ctx.tap("fout0", fout[p][:], ["gb_foA"])
        ctx.wait(nc.gpsimd, "gb_fsA")
        ctx.wait(nc.gpsimd, "gb_fsD")
        ctx.inc(nc.gpsimd.dma_start(out=f_raw[i * 128:(i + 1) * 128, :], in_=fout[p][:]),
                "gb_st%d" % p, 16)

    emit_stats_finish(ctx, sh, mean_f, rstd_f, "gb_")
    passb_es.close()
    ctx.wait(nc.gpsimd, "gb_st0")
    ctx.wait(nc.gpsimd, "gb_st1")
    barrier(ctx, "g_bar")
    emit_residual_pass(ctx, sh, x_in, f_raw, x_out, mean_f, rstd_f, "gc_", "gA")


def emit_l1_attn(ctx, ktiles, sh, x_in, x_out, w):
    nc = ctx.nc
    H, HD = 16, 64
    NTT = 16
    NIT = 4

    mean_a = ctx.sb("a_meanA", [128, NT], F32)
    rstd_a = ctx.sb("a_rstdA", [128, NT], F32)
    mean_f = ctx.sb("a_meanF", [128, NT], F32)
    rstd_f = ctx.sb("a_rstdF", [128, NT], F32)
    f_raw = sh["f_raw"]

    emit_ln_stats(ctx, sh, x_in, mean_a, rstd_a, "aA")

    batt = ctx.sb("a_batt", [1, 3072], BF16)
    ctx.sem("ap_ld")
    g = nc.gpsimd

    def ald(out_ap, in_ap):
        ctx.wait(nc.gpsimd, "ap_ld")
        ctx.inc(g.dma_start(out=out_ap, in_=in_ap), "ap_ld", 16)

    ald(batt[:], w["l1a_b_attn"].unsqueeze(0))

    Qf = ctx.sb("a_Qf", [128, 8 * T], BF16)
    Kf = ctx.sb("a_Kf", [128, 8 * T], BF16)
    Vt = ctx.sb("a_Vt", [128, NTT * D], BF16)
    mN = ctx.sb("a_mN", [128, 16], F32)
    tau = ctx.sb("a_tau", [128, 16], F32)
    taup = ctx.sb("a_taup", [128, 16], F32)
    sprev = ctx.sb("a_sprev", [128, 16], F32)
    rs = ctx.sb("a_rs", [128, 16], F32)
    kcnt = ctx.sb("a_kcnt", [128, 16], F32)
    t16a = ctx.sb("a_t16a", [128, 16], F32)
    t16b = ctx.sb("a_t16b", [128, 16], F32)
    xbq = [ctx.sb("a_xb0", [128, D], F32)]
    xbq.append(xbq[0])
    xlnq = [ctx.sb("a_xln0", [128, D], BF16)]
    xlnq.append(xlnq[0])
    xfmq = [ctx.sb("a_xfm0", [128, D], BF16)]
    xfmq.append(xfmq[0])
    fout = [ctx.sb("a_fo0", [128, D], F32), ctx.sb("a_fo1", [128, D], F32)]

    ctx.sem("aq_ld0"); ctx.sem("aq_ld1")
    ctx.sem("aq_ln"); ctx.sem("aq_trx"); ctx.sem("aq_fm")
    ctx.sem("aq_mm"); ctx.sem("aq_ev")
    ctx.sem("ah_z"); ctx.sem("ah_zev"); ctx.sem("ah_m1")
    ctx.sem("ah_zm"); ctx.sem("ah_m2"); ctx.sem("ah_it"); ctx.sem("ah_tt")
    ctx.sem("ah_tev"); ctx.sem("ah_z2"); ctx.sem("ah_pr"); ctx.sem("ah_y")
    ctx.sem("ah_yev")
    ctx.sem("aj_trs"); ctx.sem("aj_fm"); ctx.sem("aj_om"); ctx.sem("aj_oe")
    ctx.sem("a_fsA"); ctx.sem("a_fsD"); ctx.sem("a_st0"); ctx.sem("a_st1")

    pr_mark = [0, 0]
    y_mark = [0, 0]
    for b in range(BPC):
        b2_ps = ExitStack()
        pq = b2_ps.enter_context(nc.psum_tensor("a_pq%d" % b, [128, 128], F32))
        pk = b2_ps.enter_context(nc.psum_tensor("a_pk%d" % b, [128, 128], F32))
        pv = b2_ps.enter_context(nc.psum_tensor("a_pv%d" % b, [128, 1024], F32))
        ptr = b2_ps.enter_context(nc.psum_tensor("a_ptr%d" % b, [128, 1024], BF16))
        watt = b2_ps.enter_context(nc.sbuf_tensor("a_watt%d" % b, [128, 8 * 3072], BF16))
        for kc in range(8):
            ald(watt[:, kc * 3072:(kc + 1) * 3072],
                w["l1a_w_attn"][kc * 128:(kc + 1) * 128, :])
        for tt in range(NTT):
            i = b * NTT + tt
            p = i % 2
            if i >= 1:
                ctx.wait(nc.sync, "aq_ln", ctx.val("aq_ln"))
            ctx.inc(nc.sync.dma_start(out=xbq[p][:], in_=x_in[i * 128:(i + 1) * 128, :]),
                    "aq_ld%d" % p, 16)
            ctx.wait(nc.vector, "aq_ld%d" % p)
            if i == 0:
                ctx.wait(nc.vector, "aAfinA", 1)
            if i >= 1:
                ctx.wait(nc.vector, "aq_trx", ctx.val("aq_trx"))
            ctx.inc(ctx.V.tensor_scalar(xlnq[p][:], xbq[p][:], mean_a[:, i:i + 1],
                                        rstd_a[:, i:i + 1], ALU.subtract, ALU.mult), "aq_ln", 1)
            ctx.wait(nc.tensor, "aq_ln")
            if tt == 0:
                ctx.wait(nc.tensor, "ap_ld")
            ctx.wait(nc.tensor, "aq_fm")
            for c in range(8):
                tin = nc.tensor.transpose(ptr[:, c * 128:c * 128 + 128],
                                          xlnq[p][:, c * 128:c * 128 + 128], ktiles["ident_b"][:])
            ctx.inc(tin, "aq_trx", 1)
            ctx.wait(nc.vector, "aq_trx")
            if i >= 1:
                ctx.wait(nc.vector, "aq_mm", ctx.val("aq_mm"))
            ctx.inc(ctx.V.tensor_copy(xfmq[p][:], ptr[:]), "aq_fm", 1)
            ctx.wait(nc.tensor, "aq_fm")
            for c in range(8):
                if not (b == 0 and tt == 0 and c == 0):
                    ctx.wait(nc.tensor, "aq_ev", ctx.val("aq_ev"))
                for kc in range(8):
                    nc.tensor.matmul(pq[:], watt[:, kc * 3072 + c * 128:kc * 3072 + c * 128 + 128],
                                     xfmq[p][:, kc * 128:kc * 128 + 128],
                                     start=(kc == 0), stop=False)
                nc.tensor.matmul(pq[:], batt[0:1, c * 128:c * 128 + 128],
                                 ktiles["ones_row_b"][:], start=False, stop=True)
                for kc in range(8):
                    nc.tensor.matmul(pk[:], watt[:, kc * 3072 + 1024 + c * 128:kc * 3072 + 1024 + c * 128 + 128],
                                     xfmq[p][:, kc * 128:kc * 128 + 128],
                                     start=(kc == 0), stop=False)
                tin = nc.tensor.matmul(pk[:], batt[0:1, 1024 + c * 128:1024 + c * 128 + 128],
                                       ktiles["ones_row_b"][:], start=False, stop=True)
                ctx.inc(tin, "aq_mm", 1)
                ctx.wait(nc.scalar, "aq_mm")
                ctx.S.activation(Qf[:, c * T + tt * 128:c * T + tt * 128 + 128], pq[:],
                                 AF.Identity, scale=0.125)
                ctx.inc(ctx.S.copy(Kf[:, c * T + tt * 128:c * T + tt * 128 + 128], pk[:]),
                        "aq_ev", 1)
            ctx.wait(nc.tensor, "aq_ev")
            for bb in range(2):
                for kc in range(8):
                    nc.tensor.matmul(pv[:, bb * 512:bb * 512 + 512],
                                     xfmq[p][:, kc * 128:kc * 128 + 128],
                                     watt[:, kc * 3072 + 2048 + bb * 512:kc * 3072 + 2048 + bb * 512 + 512],
                                     start=(kc == 0), stop=False)
                tin = nc.tensor.matmul(pv[:, bb * 512:bb * 512 + 512], ktiles["ones_row_b"][:],
                                       batt[0:1, 2048 + bb * 512:2048 + bb * 512 + 512],
                                       start=False, stop=True)
            ctx.inc(tin, "aq_mm", 1)
            ctx.wait(nc.scalar, "aq_mm")
            ctx.inc(ctx.S.copy(Vt[:, tt * D:(tt + 1) * D], pv[:]), "aq_ev", 1)
        ctx.wait(nc.tensor, "aq_ev")
        ctx.wait(nc.gpsimd, "ap_ld")
        b2_ps.close()
        barrier(ctx, "a_b2_%d" % b)

        ah_ps = ExitStack()
        _a = lambda n, sh_, dt_: ah_ps.enter_context(nc.sbuf_tensor(n + str(b), sh_, dt_))
        zsb = [_a("a_zb0", [128, T], BF16)]
        zsb.append(zsb[0])
        zmm = _a("a_zm", [128, T], BF16)
        Cb2 = [_a("a_Cb0", [128, 16 * 256], BF16), _a("a_Cb1", [128, 16 * 256], BF16)]
        ptil = [_a("a_pt0", [128, 512], BF16), _a("a_pt1", [128, 512], BF16)]
        nth = _a("a_nth", [1, T], BF16)
        ntl = _a("a_ntl", [1, T], BF16)
        dmp = [_a("a_dm0", [128, 256], BF16), _a("a_dm1", [128, 256], BF16)]
        pz1 = ah_ps.enter_context(nc.psum_tensor("a_pz1%d" % b, [128, 2048], F32))
        pz2 = [ah_ps.enter_context(nc.psum_tensor("a_pz2%d_%d" % (b, j), [128, 512], F32))
               for j in range(2)]
        pyt = ah_ps.enter_context(nc.psum_tensor("a_pyt%d" % b, [64, 512], F32))
        ptt = ah_ps.enter_context(nc.psum_tensor("a_ptt%d" % b, [1, 512], F32))
        zb_markD = [0, 0]
        zb_markP = [0, 0]

        def mk_p1(h):
            hp_ = (h % 2) * 64
            hc_ = h // 2
            CbH = Cb2[h % 2]
            ops = []
            for qtb in range(NTT):
                KT = (qtb + 1) * 128
                par = qtb % 2
                W = qtb + 1

                def u_pe(qtb=qtb, KT=KT, hp_=hp_, hc_=hc_, h=h):
                    if h + qtb > 0 or b > 0:
                        ctx.wait(nc.tensor, "ah_zev", ctx.val("ah_zev"))
                    nmm = (KT + 511) // 512
                    for mi in range(nmm):
                        k0 = mi * 512
                        kn = min(512, KT - k0)
                        nc.tensor.matmul(pz1[:, k0:k0 + kn],
                                         Qf[hp_:hp_ + 64, hc_ * T + qtb * 128:hc_ * T + qtb * 128 + 128],
                                         Kf[hp_:hp_ + 64, hc_ * T + k0:hc_ * T + k0 + kn],
                                         start=True, stop=(mi != nmm - 1))
                    tin = nc.tensor.matmul(pz1[:, KT - 128:KT], ktiles["negI128_b"][:],
                                           ktiles["uts128_b"][:], start=False, stop=True)
                    ctx.inc(tin, "ah_z", 1)
                ops.append(u_pe)

                def u_ev(KT=KT, par=par):
                    ctx.wait(nc.scalar, "ah_z")
                    ctx.wait(nc.scalar, "ah_m1", max(zb_markD))
                    ctx.wait(nc.scalar, "ah_zm", max(zb_markP))
                    ctx.inc(ctx.S.copy(zsb[par][:, 0:KT], pz1[:, 0:KT]), "ah_zev", 1)
                ops.append(u_ev)

                def u_dv(qtb=qtb, KT=KT, par=par, W=W, CbH=CbH):
                    ctx.wait(nc.vector, "ah_zev")
                    ctx.inc(ctx.V.reduce_max(CbH[:, qtb * 256:qtb * 256 + 128],
                                             zsb[par][:, 0:KT].rearrange("p (n w) -> p n w", w=W),
                                             axis=AX.X), "ah_m1", 1)
                    zb_markD[par] = ctx.val("ah_m1")
                    ctx.V.tensor_tensor(zmm[:, 0:KT].rearrange("p (n w) -> p n w", w=W),
                                        zsb[par][:, 0:KT].rearrange("p (n w) -> p n w", w=W),
                                        CbH[:, qtb * 256:qtb * 256 + 128].unsqueeze(2).broadcast_to([128, 128, W]),
                                        ALU.is_equal)
                    ctx.inc(ctx.V.scalar_tensor_tensor(zmm[:, 0:KT], zmm[:, 0:KT], -2e9,
                                                       zsb[par][:, 0:KT], ALU.mult, ALU.add), "ah_zm", 1)
                    zb_markP[par] = ctx.val("ah_zm")
                    ctx.inc(ctx.V.reduce_max(CbH[:, qtb * 256 + 128:qtb * 256 + 256],
                                             zmm[:, 0:KT].rearrange("p (n w) -> p n w", w=W),
                                             axis=AX.X), "ah_m2", 1)
                ops.append(u_dv)
            return ops

        def mk_itp2(h):
            hp_ = (h % 2) * 64
            hc_ = h // 2
            CbH = Cb2[h % 2]
            ops = []

            def u_init():
                if ctx.val("ah_tt"):
                    ctx.wait(nc.vector, "ah_tt", ctx.val("ah_tt"))
                ctx.V.reduce_max(mN[:], CbH[:].rearrange("p (n s) -> p n s", s=256),
                                 axis=AX.X)
                ctx.V.tensor_scalar(tau[:], mN[:], -1.0, None, ALU.add)
            ops.append(u_init)
            for it in range(NIT):
                for blkp in range(8):
                    def u_blk(it=it, blkp=blkp, CbH=CbH):
                        for blk in (2 * blkp, 2 * blkp + 1):
                            dp = blk % 2
                            ctx.V.tensor_scalar(dmp[dp][:], CbH[:, blk * 256:blk * 256 + 256],
                                                tau[:, blk:blk + 1], 0.0, ALU.max, ALU.add,
                                                accum_out=rs[:, blk:blk + 1])
                            ctx.V.tensor_scalar(dmp[dp][:], CbH[:, blk * 256:blk * 256 + 256],
                                                tau[:, blk:blk + 1], 0.0, ALU.is_gt, ALU.add,
                                                accum_out=kcnt[:, blk:blk + 1])
                    ops.append(u_blk)

                def u_up(it=it):
                    ctx.V.scalar_tensor_tensor(t16a[:], tau[:], -256.0, rs[:], ALU.mult, ALU.add)
                    ctx.V.tensor_scalar(t16a[:], t16a[:], -1.0, None, ALU.add)
                    ctx.V.tensor_scalar(t16b[:], kcnt[:], 1.0, None, ALU.max)
                    ctx.V.reciprocal(t16b[:], t16b[:])
                    ctx.V.tensor_mul(t16b[:], t16a[:], t16b[:])
                    ins = ctx.V.tensor_add(tau[:], tau[:], t16b[:])
                    if it == NIT - 1:
                        ctx.inc(ins, "ah_it", 1)
                ops.append(u_up)
            for grp in range(4):
                def u_tr(grp=grp, h=h):
                    if grp == 0:
                        ctx.wait(nc.tensor, "ah_it")
                    if h > 0 or grp > 0 or b > 0:
                        ctx.wait(nc.tensor, "ah_tev", ctx.val("ah_tev"))
                    for j in range(4):
                        blk = grp * 4 + j
                        tin = nc.tensor.transpose(ptt[0:1, j * 128:j * 128 + 128],
                                                  tau[:, blk:blk + 1], ktiles["ident_f"][:])
                    ctx.inc(tin, "ah_tt", 1)
                    ctx.wait(nc.vector, "ah_tt")
                    if (h > 0 or b > 0) and grp == 0:
                        ctx.wait(nc.vector, "ah_y", ctx.val("ah_y"))
                    ctx.V.tensor_scalar(nth[0:1, grp * 512:grp * 512 + 512], ptt[:],
                                        -1.0, None, ALU.mult)
                    ctx.inc(ctx.V.scalar_tensor_tensor(ntl[0:1, grp * 512:grp * 512 + 512],
                                                       ptt[:], -1.0,
                                                       nth[0:1, grp * 512:grp * 512 + 512],
                                                       ALU.mult, ALU.subtract), "ah_tev", 1)
                ops.append(u_tr)
            for sq in range(3, -1, -1):
                q0 = sq * 512
                nktc = 4 * sq + 4
                for ktc in range(nktc):
                    def u_z2(sq=sq, q0=q0, ktc=ktc, nktc=nktc, hp_=hp_, hc_=hc_, h=h):
                        par2 = ktc % 2
                        if sq == 3 and ktc == 0:
                            ctx.wait(nc.tensor, "ah_tev")
                        ctx.wait(nc.tensor, "ah_pr", pr_mark[par2])
                        if ktc == 0:
                            ctx.wait(nc.tensor, "ah_yev", ctx.val("ah_yev"))
                        nc.tensor.matmul(pz2[par2][:],
                                         Kf[hp_:hp_ + 64, hc_ * T + ktc * 128:hc_ * T + ktc * 128 + 128],
                                         Qf[hp_:hp_ + 64, hc_ * T + q0:hc_ * T + q0 + 512],
                                         start=True, stop=False)
                        nc.tensor.matmul(pz2[par2][:], ktiles["ones_row_b"][:],
                                         nth[0:1, q0:q0 + 512], start=False, stop=False)
                        dtile = ktc - 4 * sq
                        if dtile > 0:
                            nc.tensor.matmul(pz2[par2][:, 0:dtile * 128], ktiles["ones_row_b"][:],
                                             ktiles["neg1e9row_b"][0:1, 0:dtile * 128],
                                             start=False, stop=False)
                        if 0 <= dtile < 4:
                            nc.tensor.matmul(pz2[par2][:], ktiles["ones_row_b"][:],
                                             ntl[0:1, q0:q0 + 512], start=False, stop=False)
                            tin = nc.tensor.matmul(pz2[par2][:, dtile * 128:dtile * 128 + 128],
                                                   ktiles["negI128_b"][:], ktiles["lts128_b"][:],
                                                   start=False, stop=True)
                        else:
                            tin = nc.tensor.matmul(pz2[par2][:], ktiles["ones_row_b"][:],
                                                   ntl[0:1, q0:q0 + 512], start=False, stop=True)
                        ctx.inc(tin, "ah_z2", 1)
                        if ktc > 0:
                            pk_ = (ktc - 1) % 2
                            ctx.wait(nc.tensor, "ah_pr", pr_mark[pk_])
                            tin = nc.tensor.matmul(pyt[:],
                                                   Vt[:, (ktc - 1) * D + h * HD:(ktc - 1) * D + h * HD + HD],
                                                   ptil[pk_][:],
                                                   start=(ktc == 1), stop=False)
                            ctx.inc(tin, "ah_y", 1)
                            y_mark[pk_] = ctx.val("ah_y")
                        ctx.wait(nc.scalar, "ah_z2")
                        ctx.wait(nc.scalar, "ah_y", y_mark[par2])
                        ctx.inc(ctx.S.activation(ptil[par2][:], pz2[par2][:], AF.Relu), "ah_pr", 1)
                        pr_mark[par2] = ctx.val("ah_pr")
                    ops.append(u_z2)

                def u_ylast(sq=sq, q0=q0, nktc=nktc, hp_=hp_, hc_=hc_, h=h):
                    pk_ = (nktc - 1) % 2
                    ctx.wait(nc.tensor, "ah_pr", pr_mark[pk_])
                    tin = nc.tensor.matmul(pyt[:],
                                           Vt[:, (nktc - 1) * D + h * HD:(nktc - 1) * D + h * HD + HD],
                                           ptil[pk_][:],
                                           start=(nktc == 1), stop=True)
                    ctx.inc(tin, "ah_y", 1)
                    y_mark[pk_] = ctx.val("ah_y")
                    ctx.wait(nc.scalar, "ah_y")
                    ctx.inc(ctx.S.copy(Qf[hp_:hp_ + 64, hc_ * T + q0:hc_ * T + q0 + 512],
                                       pyt[:]), "ah_yev", 1)
                ops.append(u_ylast)
            return ops

        prevA = None
        for h in range(H):
            p1ops = mk_p1(h)
            if prevA is None or DBG.get("a_seq"):
                if prevA is not None:
                    for f in prevA:
                        f()
                    prevA = None
                for f in p1ops:
                    f()
            else:
                na, nb = len(prevA), len(p1ops)
                seq = [(i / na, 0, f) for i, f in enumerate(prevA)]
                seq += [(i / nb, 1, f) for i, f in enumerate(p1ops)]
                for _, _, f in sorted(seq, key=lambda x: (x[0], x[1])):
                    f()
            prevA = mk_itp2(h)
        for f in prevA:
            f()
        ctx.wait(nc.tensor, "ah_yev")
        ah_ps.close()
        barrier(ctx, "a_at_%d" % b)

        pj_ps = ExitStack()
        wpro = pj_ps.enter_context(nc.sbuf_tensor("a_wpro%d" % b, [128, 8 * D], BF16))
        bpro = pj_ps.enter_context(nc.sbuf_tensor("a_bpro%d" % b, [1, D], BF16))
        ctx.wait(nc.gpsimd, "ap_ld")
        ctx.inc(g.dma_start(out=bpro[:], in_=w["l1a_b_proj"].unsqueeze(0)), "ap_ld", 16)
        for kc in range(8):
            ctx.wait(nc.gpsimd, "ap_ld")
            ctx.inc(g.dma_start(out=wpro[:, kc * D:(kc + 1) * D],
                                in_=w["l1a_w_proj"][kc * 128:(kc + 1) * 128, :]), "ap_ld", 16)
        po = pj_ps.enter_context(nc.psum_tensor("a_po%d" % b, [128, 1024], F32))
        for tt in range(NTT):
            i = b * NTT + tt
            p = i % 2
            ctx.wait(nc.tensor, "aj_fm")
            if tt == 0:
                ctx.wait(nc.tensor, "ap_ld")
            if ctx.val("aj_oe") > 0:
                ctx.wait(nc.tensor, "aj_oe", ctx.val("aj_oe"))
            for bb in range(2):
                for kc in range(8):
                    nc.tensor.matmul(po[:, bb * 512:bb * 512 + 512],
                                     Qf[:, kc * T + tt * 128:kc * T + tt * 128 + 128],
                                     wpro[:, kc * D + bb * 512:kc * D + bb * 512 + 512],
                                     start=(kc == 0), stop=False)
                tin = nc.tensor.matmul(po[:, bb * 512:bb * 512 + 512], ktiles["ones_row_b"][:],
                                       bpro[0:1, bb * 512:bb * 512 + 512], start=False, stop=True)
            ctx.inc(tin, "aj_om", 1)
            ctx.wait(nc.vector, "aj_om")
            if i >= 2:
                ctx.wait(nc.vector, "a_st%d" % p)
            ctx.inc(ctx.V.tensor_copy(fout[p][:], po[:]), "aj_oe", 1)
            ctx.wait(nc.scalar, "aj_oe")
            ctx.inc(ctx.S.activation((sh["scr"] if p == 0 else sh["scr2"])[:], fout[p][:],
                                     AF.Square, accum_out=sh["ssq"][:, i:i + 1]), "a_fsA", 1)
            ctx.wait(nc.vector, "aj_oe")
            ctx.inc(ctx.V.reduce_sum(sh["nsum"][:, i:i + 1], fout[p][:], axis=AX.X,
                                     negate=True), "a_fsD", 1)
            ctx.wait(nc.gpsimd, "a_fsA")
            ctx.wait(nc.gpsimd, "a_fsD")
            ctx.inc(nc.gpsimd.dma_start(out=f_raw[i * 128:(i + 1) * 128, :], in_=fout[p][:]),
                    "a_st%d" % p, 16)
        ctx.wait(nc.tensor, "aj_oe")
        ctx.wait(nc.gpsimd, "a_st0")
        ctx.wait(nc.gpsimd, "a_st1")
        pj_ps.close()
        barrier(ctx, "a_pj_%d" % b)

    emit_stats_finish(ctx, sh, mean_f, rstd_f, "a_")
    ctx.wait(nc.gpsimd, "a_st0")
    ctx.wait(nc.gpsimd, "a_st1")
    barrier(ctx, "a_bar")
    emit_residual_pass(ctx, sh, x_in, f_raw, x_out, mean_f, rstd_f, "ac_", "aA")


def emit_l0_attn(ctx, ktiles, sh, x_in, x_out, w):
    nc = ctx.nc
    NH = 8
    NTT = 16
    LN128 = float(0.5 * np.log(128.0))

    mean_a = ctx.sb("d_meanA", [128, NT], F32)
    rstd_a = ctx.sb("d_rstdA", [128, NT], F32)
    mean_f = ctx.sb("d_meanF", [128, NT], F32)
    rstd_f = ctx.sb("d_rstdF", [128, NT], F32)
    f_raw = sh["f_raw"]

    emit_ln_stats(ctx, sh, x_in, mean_a, rstd_a, "dA")
    if DBG.get("l0a"):
        ctx.wait(nc.gpsimd, "dAfinA", 1)
        ctx.tap("meanA", mean_a[:])
        ctx.tap("rstdA", rstd_a[:])
        ctx.tap("nsumA", sh["nsum"][:])
        ctx.tap("ssqA", sh["ssq"][:])

    wqkvz = ctx.sb("d_wqkvz", [128, 8 * 4096], BF16)
    wba = ctx.sb("d_wba", [128, 8 * 16], BF16)
    woutb = ctx.sb("d_wout", [128, 8 * 1024], BF16)
    cw = ctx.sb("d_cw", [128, 24 * 4], F32)
    alog = ctx.sb("d_alog", [1, 8], F32)
    dtbf = ctx.sb("d_dtbf", [1, 8], F32)
    dtbr = ctx.sb("d_dtbr", [1, 16], BF16)
    nwc = ctx.sb("d_nwc", [128, 1], F32)
    nEal = ctx.sb("d_nEal", [128, 8], F32)
    ealr = ctx.sb("d_ealr", [1, 8], F32)
    ealb = ctx.sb("d_ealb", [1, 8], BF16)
    ctx.sem("dp_ld"); ctx.sem("dp_c")
    g = nc.gpsimd

    def dld(out_ap, in_ap, **kw):
        ctx.wait(nc.gpsimd, "dp_ld")
        ctx.inc(g.dma_start(out=out_ap, in_=in_ap, **kw), "dp_ld", 16)

    for kc in range(8):
        dld(wqkvz[:, kc * 4096:(kc + 1) * 4096], w["l0a_W_qkvz"][kc * 128:(kc + 1) * 128, :])
    for kc in range(8):
        dld(wba[:, kc * 16:(kc + 1) * 16], w["l0a_W_ba"][kc * 128:(kc + 1) * 128, :])
    dld(cw[:].rearrange("p (c k) -> p c k", k=4),
        w["l0a_conv_w"].rearrange("(c p) o k -> p c (o k)", p=128))
    dld(alog[:], w["l0a_A_log"].unsqueeze(0))
    dld(dtbf[:], w["l0a_dt_bias"].unsqueeze(0))
    dld(nwc[:], w["l0a_norm_weight"].unsqueeze(1), allow_slow_non_contiguous=True)
    ctx.wait(nc.vector, "dp_ld")
    ctx.V.memset(dtbr[:], 0.0)
    ctx.V.tensor_copy(
        dtbr[0:1, :].rearrange("o (a two) -> o a two", two=2)[:, :, 1:2],
        dtbf[:].unsqueeze(2))
    ctx.inc(ctx.V.tensor_copy(ealb[:], ealr[:]) if False else ctx.V.engine_nop(), "dp_c", 1) \
        if False else None
    ctx.wait(nc.scalar, "dp_ld")
    ctx.inc(ctx.S.activation(ealr[:], alog[:], AF.Exp), "dp_c", 1)
    ctx.wait(nc.vector, "dp_c")
    ctx.inc(ctx.V.tensor_copy(ealb[:], ealr[:]), "dp_c", 1)
    pEa_es = ExitStack()
    pEa = pEa_es.enter_context(nc.psum_tensor("d_pEa", [128, 8], F32))
    ctx.wait(nc.tensor, "dp_c")
    ctx.inc(nc.tensor.matmul(pEa[:], ktiles["ones_row_b"][:], ealb[:],
                             start=True, stop=True), "dp_c", 1)
    ctx.wait(nc.vector, "dp_c")
    ctx.inc(ctx.V.tensor_scalar(nEal[:], pEa[:], -1.0, None, ALU.mult), "dp_c", 1)
    wldd_es = ExitStack()
    wldd = wldd_es.enter_context(nc.sbuf_tensor("d_wldd", [128, 1024], F32))
    for hh in range(8):
        if hh > 0:
            ctx.wait(nc.gpsimd, "dp_c", ctx.val("dp_c"))
        dld(wldd[:], w["l0a_W_out"][hh * 128:(hh + 1) * 128, :])
        ctx.wait(nc.vector, "dp_ld")
        ctx.inc(ctx.V.tensor_scalar(woutb[:, hh * 1024:(hh + 1) * 1024], wldd[:],
                                    nwc[:], None, ALU.mult), "dp_c", 1)
    ctx.wait(nc.tensor, "dp_c")
    ctx.wait(nc.gpsimd, "dp_ld")
    ctx.wait(nc.vector, "dp_c")
    wldd_es.close()
    pEa_es.close()
    barrier(ctx, "d_prep")

    xbd = [ctx.sb("d_xb0", [128, D], F32), ctx.sb("d_xb1", [128, D], F32)]
    xlnd = ctx.sb("d_xln", [128, D], BF16)
    xfm256 = ctx.sb("d_xfm", [128, 8 * 256], BF16)
    rawb = ctx.sb("d_raw", [128, 24 * 260], BF16)
    qkvc = ctx.sb("d_qkvc", [128, 24 * 256], BF16)
    szfm = ctx.sb("d_szfm", [128, 8 * 256], BF16)
    batm = ctx.sb("d_batm", [128, 16], F32)
    t8a = ctx.sb("d_t8a", [128, 8], F32)
    t8b = ctx.sb("d_t8b", [128, 8], F32)
    t8c = ctx.sb("d_t8c", [128, 8], F32)
    t8d = ctx.sb("d_t8d", [128, 8], F32)
    gcb = ctx.sb("d_gcb", [128, 16 * 8], F32)
    rvb = ctx.sb("d_rvb", [128, 16 * 8], F32)
    glb = ctx.sb("d_glb", [128, 16 * 8], F32)
    lnbb = ctx.sb("d_lnbb", [128, 16 * 8], F32)
    betb = ctx.sb("d_betb", [128, 16 * 8], F32)
    Sst = [ctx.sb("d_S%d" % hh, [128, 128], F32) for hh in range(NH)]
    Sbf = [ctx.sb("d_Sb%d" % hh, [128, 128], BF16) for hh in range(NH)]
    dtile = [ctx.sb("d_dt%d" % t_, [128, 128], BF16) for t_ in range(4)]
    ogbuf = ctx.sb("d_og", [128, 8 * 128], BF16)
    cgB = ctx.sb("d_cgB", [64, 8], F32)
    crB = ctx.sb("d_crB", [64, 8], F32)
    clB = ctx.sb("d_clB", [64, 8], F32)
    cglB = ctx.sb("d_cglB", [64, 8], F32)
    cbtB = ctx.sb("d_cbtB", [64, 8], F32)
    SL = []
    for s in range(2):
        t = "ds%d_" % s
        SL.append(dict(
            ktm=ctx.sb(t + "ktm", [64, 128], BF16),
            vtm=ctx.sb(t + "vtm", [64, 128], BF16),
            vbeta=ctx.sb(t + "vbeta", [64, 128], BF16),
            kbeg=ctx.sb(t + "kbeg", [64, 128], BF16),
            kglb=ctx.sb(t + "kgl", [64, 128], BF16),
            vnew=ctx.sb(t + "vnew", [64, 128], BF16),
            vnb=ctx.sb(t + "vn", [64, 128], BF16),
            kcum=ctx.sb(t + "kcum", [128, 64], BF16),
            qeb=ctx.sb(t + "qe", [128, 64], BF16),
            sqk=ctx.sb(t + "sqk", [64, 128], BF16),
            sqq=ctx.sb(t + "sqq", [128, 64], BF16),
            sqo=ctx.sb(t + "sqo", [128, 64], BF16),
            og1=ctx.sb(t + "og1", [128, 64], BF16),
            tri=ctx.sb(t + "tri", [64, 64], BF16),
            triT=ctx.sb(t + "triT", [64, 64], BF16),
            attnT=ctx.sb(t + "attnT", [64, 64], BF16),
            dcy=ctx.sb(t + "dcy", [64, 64], F32),
            dcy2=ctx.sb(t + "dcy2", [64, 64], F32),
            Pb=[ctx.sb(t + "P%d" % j, [64, 64], BF16) for j in range(2)],
            Qb=[ctx.sb(t + "Q%d" % j, [64, 64], BF16) for j in range(2)],
            Xb=[ctx.sb(t + "X%d" % j, [64, 64], BF16) for j in range(2)],
            Wb=[ctx.sb(t + "W%d" % j, [64, 64], BF16) for j in range(2)],
            rows=ctx.sb(t + "rows", [1, 192], F32),
            r3r=ctx.sb(t + "r3r", [1, 64], F32),
            r1b=ctx.sb(t + "r1b", [1, 64], BF16),
            r2b=ctx.sb(t + "r2b", [1, 64], BF16),
            r3b=ctx.sb(t + "r3b", [1, 64], BF16),
            r1l=ctx.sb(t + "r1l", [1, 64], BF16),
            r2l=ctx.sb(t + "r2l", [1, 64], BF16),
            r3l=ctx.sb(t + "r3l", [1, 64], BF16),
            r3e=ctx.sb(t + "r3e", [1, 64], BF16),
            ssk=ctx.sb(t + "ssk", [64, 1], F32),
            lnrkc=ctx.sb(t + "lnrk", [64, 1], F32),
            lnqr=ctx.sb(t + "lnqr", [1, 64], F32),
            r1c=ctx.sb(t + "r1c", [64, 1], F32),
            r2mc=ctx.sb(t + "r2mc", [64, 1], F32),
            e1c=ctx.sb(t + "e1c", [64, 1], F32),
            cta=ctx.sb(t + "cta", [64, 1], F32),
            ctb=ctx.sb(t + "ctb", [64, 1], F32),
            eglc=ctx.sb(t + "eglc", [128, 1], F32),
            rsr=ctx.sb(t + "rsr", [1, 64], F32),
            rsb=ctx.sb(t + "rsb", [1, 64], BF16),
        ))
    fout = [ctx.sb("d_fo0", [128, D], F32), ctx.sb("d_fo1", [128, D], F32)]

    ctx.sem("dq_ld0"); ctx.sem("dq_ld1"); ctx.sem("dq_ln"); ctx.sem("dq_trx")
    ctx.sem("dq_fm"); ctx.sem("dq_mm"); ctx.sem("dq_ev"); ctx.sem("dq_cv")
    ctx.sem("dq_g"); ctx.sem("dq_halo")
    ctx.sem("pm0"); ctx.sem("pm1"); ctx.sem("pv0"); ctx.sem("pv1")
    ctx.sem("pa0"); ctx.sem("pa1")
    ctx.sem("do_mm"); ctx.sem("do_ev")
    ctx.sem("d_fsA"); ctx.sem("d_fsD"); ctx.sem("d_st0"); ctx.sem("d_st1")

    def dmm(out_ap, lhsT, rhs, start=True, stop=True):
        return nc.tensor.matmul(out_ap, lhsT, rhs, start=start, stop=stop)

    def pe_sync_dve():
        ctx.finalize_pending("dve")
        last = ctx.last.get("dve")
        if last is not None:
            nc.tensor.wait_ge(ctx.sems[last[0]][0], last[1])

    def act_sync_dve():
        ctx.finalize_pending("dve")
        last = ctx.last.get("dve")
        if last is not None:
            nc.scalar.wait_ge(ctx.sems[last[0]][0], last[1])

    env = dict(nc=nc, dmm=dmm, qkvc=qkvc, szfm=szfm, gcb=gcb, rvb=rvb, glb=glb,
               lnbb=lnbb, betb=betb, cgB=cgB, crB=crB, clB=clB, cglB=cglB,
               cbtB=cbtB, SL=SL, Sst=Sst, Sbf=Sbf, ogbuf=ogbuf, LN128=LN128)

    for b in range(BPC):
        for hh in range(NH):
            ctx.V.memset(Sst[hh][:], 0.0)
            ins = ctx.V.memset(Sbf[hh][:], 0.0)
            if hh == NH - 2:
                ctx.inc(ins, "pv0", 1)
            elif hh == NH - 1:
                ctx.inc(ins, "pv1", 1)
        ctx.inc(ctx.V.memset(rawb[:].rearrange("p (c x) -> p c x", x=260)[:, :, 0:3], 0.0),
                "dq_halo", 1)
        for q in range(8):
            q_ps = ExitStack()
            pqz = q_ps.enter_context(nc.psum_tensor("d_pqz%d_%d" % (b, q), [128, 256], F32))
            pba = q_ps.enter_context(nc.psum_tensor("d_pba%d_%d" % (b, q), [128, 256], F32))
            pcv = q_ps.enter_context(nc.psum_tensor("d_pcv%d_%d" % (b, q), [128, 256], F32))
            ptrq = q_ps.enter_context(nc.psum_tensor("d_ptrq%d_%d" % (b, q), [128, 1024], BF16))
            for halftile in range(2):
                tt = 2 * q + halftile
                i = b * NTT + tt
                p = i % 2
                if ctx.val("dq_ln") > 0:
                    ctx.wait(nc.sync, "dq_ln", ctx.val("dq_ln") - 1)
                ctx.inc(nc.sync.dma_start(out=xbd[p][:], in_=x_in[i * 128:(i + 1) * 128, :]),
                        "dq_ld%d" % p, 16)
                ctx.wait(nc.vector, "dq_ld%d" % p)
                if i == 0:
                    ctx.wait(nc.vector, "dAfinA", 1)
                if ctx.val("dq_trx") > 0:
                    ctx.wait(nc.vector, "dq_trx", ctx.val("dq_trx"))
                ctx.inc(ctx.V.tensor_scalar(xlnd[:], xbd[p][:], mean_a[:, i:i + 1],
                                            rstd_a[:, i:i + 1], ALU.subtract, ALU.mult),
                        "dq_ln", 1)
                ctx.wait(nc.tensor, "dq_ln")
                if i == 0:
                    ctx.wait(nc.tensor, "dp_ld")
                    ctx.wait(nc.tensor, "dp_c")
                if ctx.val("dq_fm") > 0:
                    ctx.wait(nc.tensor, "dq_fm", ctx.val("dq_fm"))
                for c in range(8):
                    tin = nc.tensor.transpose(ptrq[:, c * 128:c * 128 + 128],
                                              xlnd[:, c * 128:c * 128 + 128],
                                              ktiles["ident_b"][:])
                ctx.inc(tin, "dq_trx", 1)
                ctx.wait(nc.vector, "dq_trx")
                if ctx.val("dq_mm") > 0:
                    ctx.wait(nc.vector, "dq_mm", ctx.val("dq_mm"))
                ctx.inc(ctx.V.tensor_copy(
                    xfm256[:].rearrange("p (kc t) -> p kc t", t=256)[:, :, halftile * 128:halftile * 128 + 128],
                    ptrq[:].rearrange("p (kc t) -> p kc t", t=128)), "dq_fm", 1)
                if DBG.get("l0a") and b == 0 and q == 0 and halftile == 0:
                    barrier(ctx, "dbg_xfm0")
                    dbgtile = ctx.sb("dbg_ptrq", [128, 1024], BF16)
                    ctx.V.tensor_copy(dbgtile[:], ptrq[:])
                    dbgx = ctx.sb("dbg_xlncp", [128, 1024], BF16)
                    ctx.inc(ctx.V.tensor_copy(dbgx[:], xlnd[:]), "dq_fm", 1)
                    barrier(ctx, "dbg_xfm0c")
                    ctx.tap("xfm_t0", xfm256[:])
                    ctx.tap("ptrq0", dbgtile[:])
                    ctx.tap("xln0", dbgx[:])
                    ctx.wait(nc.gpsimd, "dbg_st")
                    barrier(ctx, "dbg_xfm0b")
            ctx.wait(nc.tensor, "dq_fm")
            ctx.wait(nc.tensor, "dq_halo")
            for c in range(32):
                h = c // 4
                role = c % 4
                if ctx.val("dq_ev") > 0:
                    ctx.wait(nc.tensor, "dq_ev", ctx.val("dq_ev"))
                for kc in range(8):
                    tin = dmm(pqz[:], wqkvz[:, kc * 4096 + c * 128:kc * 4096 + c * 128 + 128],
                              xfm256[:, kc * 256:(kc + 1) * 256],
                              start=(kc == 0), stop=(kc == 7))
                ctx.inc(tin, "dq_mm", 1)
                ctx.wait(nc.scalar, "dq_mm")
                if c == 0:
                    ctx.wait(nc.scalar, "dq_halo")
                if role == 3:
                    ctx.inc(ctx.S.activation(szfm[:, h * 256:(h + 1) * 256], pqz[:], AF.Silu),
                            "dq_ev", 1)
                else:
                    m = role * 8 + h
                    ctx.inc(ctx.S.copy(rawb[:, m * 260 + 3:m * 260 + 259], pqz[:]), "dq_ev", 1)
            for halftile in range(2):
                tt = 2 * q + halftile
                if ctx.val("dq_g") > 0:
                    ctx.wait(nc.tensor, "dq_g", ctx.val("dq_g"))
                for kc in range(8):
                    dmm(pba[:, 0:16],
                        xfm256[:, kc * 256 + halftile * 128:kc * 256 + halftile * 128 + 128],
                        wba[:, kc * 16:(kc + 1) * 16], start=(kc == 0), stop=False)
                tin = dmm(pba[:, 0:16], ktiles["ones_row_b"][:], dtbr[:],
                          start=False, stop=True)
                ctx.inc(tin, "dq_mm", 1)
                ctx.wait(nc.vector, "dq_mm")
                ctx.V.tensor_copy(batm[:], pba[:, 0:16])
                bav = batm[:].rearrange("p (a two) -> p a two", two=2)
                bcol = t8c
                acol = t8d
                ctx.V.tensor_copy(bcol[:], bav[:, :, 0:1].squeeze(2))
                ctx.V.tensor_copy(acol[:], bav[:, :, 1:2].squeeze(2))
                bcol = bcol[:]
                acol = acol[:]
                lnbt = lnbb[:, tt * 8:(tt + 1) * 8]
                ctx.V.tensor_scalar(t8a[:], bcol, -1.0, None, ALU.mult)
                ctx.V.tensor_max(t8a[:], t8a[:], bcol)
                act_sync_dve()
                ctx.S.activation(t8a[:], t8a[:], AF.Exp, scale=-1.0)
                ctx.inc(ctx.S.activation(t8a[:], t8a[:], AF.Ln, bias=1.0), "dq_g", 1)
                ctx.wait(nc.vector, "dq_g")
                ctx.V.tensor_scalar(t8b[:], bcol, 0.0, None, ALU.min)
                ctx.V.tensor_sub(lnbt, t8b[:], t8a[:])
                ctx.V.tensor_scalar(t8a[:], acol, -1.0, None, ALU.mult)
                ctx.V.tensor_max(t8a[:], t8a[:], acol)
                act_sync_dve()
                ctx.S.activation(t8a[:], t8a[:], AF.Exp, scale=-1.0)
                ctx.inc(ctx.S.activation(t8a[:], t8a[:], AF.Ln, bias=1.0), "dq_g", 1)
                ctx.wait(nc.vector, "dq_g")
                ctx.V.tensor_scalar(t8b[:], acol, 0.0, None, ALU.max)
                ctx.V.tensor_add(t8a[:], t8a[:], t8b[:])
                ctx.V.tensor_mul(t8a[:], t8a[:], nEal[:])
                act_sync_dve()
                ctx.inc(ctx.S.activation(betb[:, tt * 8:(tt + 1) * 8], lnbt, AF.Exp), "dq_g", 1)
                pe_sync_dve()
                dmm(pba[:, 16:24], ktiles["bd_ut_f"][:], t8a[:])
                dmm(pba[:, 24:32], ktiles["bd_lts_f"][:], t8a[:])
                tin = dmm(pba[:, 32:40], ktiles["bd_on_f"][:], t8a[:])
                ctx.inc(tin, "dq_mm", 1)
                ctx.wait(nc.vector, "dq_mm")
                ctx.V.tensor_copy(gcb[:, tt * 8:(tt + 1) * 8], pba[:, 16:24])
                ctx.V.tensor_copy(rvb[:, tt * 8:(tt + 1) * 8], pba[:, 24:32])
                ctx.inc(ctx.V.tensor_copy(glb[:, tt * 8:(tt + 1) * 8], pba[:, 32:40]),
                        "dq_g", 1)
            ctx.wait(nc.vector, "dq_ev")
            for m in range(24):
                if ctx.val("dq_cv") > 0:
                    ctx.wait(nc.vector, "dq_cv", ctx.val("dq_cv"))
                for tp in range(4):
                    ctx.V.tensor_scalar(dtile[tp][:], ktiles["ident_b"][:],
                                        cw[:, m * 4 + tp:m * 4 + tp + 1], None, ALU.mult)
                pe_sync_dve()
                for tp in range(4):
                    tin = dmm(pcv[:], dtile[tp][:], rawb[:, m * 260 + tp:m * 260 + tp + 256],
                              start=(tp == 0), stop=(tp == 3))
                ctx.inc(tin, "dq_cv", 1)
                ctx.wait(nc.scalar, "dq_cv")
                ctx.inc(ctx.S.activation(qkvc[:, m * 256:(m + 1) * 256], pcv[:], AF.Silu),
                        "dq_ev", 1)
                ctx.wait(nc.tensor, "dq_ev")
            ctx.wait(nc.vector, "dq_cv")
            ctx.wait(nc.vector, "dq_ev")
            rawv = rawb[:].rearrange("p (c x) -> p c x", x=260)
            ctx.inc(ctx.V.tensor_copy(rawv[:, :, 0:3], rawv[:, :, 256:259]), "dq_halo", 1)
            q_ps.close()
            barrier(ctx, "d_q%d_%d" % (b, q))
            if DBG.get("l0a") and b == 0 and q == 0:
                ctx.tap("qkvc", qkvc[:])
                ctx.tap("szfm", szfm[:])
                ctx.tap("rawb", rawb[:])
                ctx.tap("gcb", gcb[:, 0:16])
                ctx.tap("lnbb", lnbb[:, 0:16])
                ctx.tap("rvb", rvb[:, 0:16])
                ctx.tap("glb", glb[:, 0:16])
                ctx.tap("betb", betb[:, 0:16])
                ctx.tap("xfm256", xfm256[:])
                ctx.tap("xlnd", xlnd[:])
                ctx.tap("wq0", wqkvz[:, 0:256])
                ctx.wait(nc.gpsimd, "dbg_st")
                barrier(ctx, "dbg_stage0")
                if DBG.get("l0a_stop"):
                    return

            d_ps = ExitStack()
            PS = []
            for s in range(2):
                pbig = d_ps.enter_context(
                    nc.psum_tensor("dB%d_%d_%d" % (b, q, s), [64, 512], F32))
                pTbs = d_ps.enter_context(
                    nc.psum_tensor("dT%d_%d_%d" % (b, q, s), [64, 256], BF16))
                pKSO = d_ps.enter_context(
                    nc.psum_tensor("dK%d_%d_%d" % (b, q, s), [128, 384], F32))
                PS.append(dict(
                    A=pbig[:, 0:64], B=pbig[:, 64:128], D=pbig[:, 128:192],
                    I0=pbig[:, 192:256], I1=pbig[:, 256:320], V=pbig[:, 192:320],
                    qrow=pbig[0:1, 320:384], tgc=pbig[0:1, 320:384],
                    tr1=pbig[0:1, 384:448], tr2m=pbig[0:1, 448:512],
                    rowsv=pbig[0:1, 320:512],
                    K=pKSO[:, 0:64], S=pKSO[:, 64:192], O=pKSO[:, 192:256],
                    X=pKSO[0:64, 64:128],
                    ss=pKSO[0:1, 256:320], qB=pKSO[:, 320:384],
                    Tk=pTbs[:, 0:128], Tv=pTbs[:, 128:256]))
            env["PS"] = PS
            pout = d_ps.enter_context(
                nc.psum_tensor("dO%d_%d" % (b, q), [128, 1024], F32))
            for grp in range(2):
                ig = b * NTT + 2 * q + grp
                pg = ig % 2
                for c64 in range(2):
                    for hp in range(4):
                        emit_delta_pair(ctx, ktiles, env, q, grp, hp, c64)
                ctx.wait(nc.tensor, "pv0")
                ctx.wait(nc.tensor, "pv1")
                if ctx.val("do_ev") > 0:
                    ctx.wait(nc.tensor, "do_ev", ctx.val("do_ev"))
                for bb in range(2):
                    for hh2 in range(8):
                        tin = dmm(pout[:, bb * 512:bb * 512 + 512],
                                  ogbuf[:, hh2 * 128:(hh2 + 1) * 128],
                                  woutb[:, hh2 * 1024 + bb * 512:hh2 * 1024 + bb * 512 + 512],
                                  start=(hh2 == 0), stop=(hh2 == 7))
                ctx.inc(tin, "do_mm", 1)
                ctx.wait(nc.vector, "do_mm")
                if ctx.val("d_st%d" % pg) > 0:
                    ctx.wait(nc.vector, "d_st%d" % pg)
                ctx.V.tensor_copy(fout[pg][:, 0:512], pout[:, 0:512])
                ctx.inc(ctx.V.tensor_copy(fout[pg][:, 512:1024], pout[:, 512:1024]),
                        "do_ev", 1)
                ctx.wait(nc.scalar, "do_ev")
                ctx.inc(ctx.S.activation((sh["scr"] if pg == 0 else sh["scr2"])[:], fout[pg][:],
                                         AF.Square, accum_out=sh["ssq"][:, ig:ig + 1]), "d_fsA", 1)
                ctx.wait(nc.vector, "d_fsA")
                ctx.inc(ctx.V.reduce_sum(sh["nsum"][:, ig:ig + 1], fout[pg][:], axis=AX.X,
                                         negate=True), "d_fsD", 1)
                ctx.wait(nc.gpsimd, "d_fsA")
                ctx.wait(nc.gpsimd, "d_fsD")
                ctx.inc(nc.gpsimd.dma_start(out=f_raw[ig * 128:(ig + 1) * 128, :],
                                            in_=fout[pg][:]), "d_st%d" % pg, 16)
            d_ps.close()
            barrier(ctx, "d_e%d_%d" % (b, q))

    emit_stats_finish(ctx, sh, mean_f, rstd_f, "d_")
    ctx.wait(nc.gpsimd, "d_st0")
    ctx.wait(nc.gpsimd, "d_st1")
    barrier(ctx, "d_bar")
    emit_residual_pass(ctx, sh, x_in, f_raw, x_out, mean_f, rstd_f, "dc_", "dA")


def emit_delta_pair(ctx, ktiles, L, q, grp, hp, c64):
    nc = L["nc"]
    V, S = ctx.V, ctx.S
    dmm = L["dmm"]
    toff = (grp * 2 + c64) * 64
    tloc = 2 * q + (toff // 128)
    po = toff % 128
    Sst, Sbf, ogbuf = L["Sst"], L["Sbf"], L["ogbuf"]
    SL, PS = L["SL"], L["PS"]
    LN128 = L["LN128"]
    sm = ("pm0", "pm1")
    sv = ("pv0", "pv1")
    sa = ("pa0", "pa1")
    heads = (2 * hp, 2 * hp + 1)
    qfv, kfv, vfv = {}, {}, {}
    for s in (0, 1):
        hh = heads[s]
        qfv[s] = L["qkvc"][:, hh * 256 + toff:hh * 256 + toff + 64]
        kfv[s] = L["qkvc"][:, (8 + hh) * 256 + toff:(8 + hh) * 256 + toff + 64]
        vfv[s] = L["qkvc"][:, (16 + hh) * 256 + toff:(16 + hh) * 256 + toff + 64]

    if hp == 0:
        for dst, srcb in ((L["cgB"], L["gcb"]), (L["crB"], L["rvb"]),
                          (L["clB"], L["lnbb"]), (L["cglB"], L["glb"]),
                          (L["cbtB"], L["betb"])):
            V.tensor_copy(dst[:], srcb[po:po + 64, tloc * 8:tloc * 8 + 8])

    for s in (0, 1):
        P = PS[s]
        if ctx.val(sv[s]):
            ctx.wait(nc.tensor, sv[s])
        if ctx.val(sa[s]):
            ctx.wait(nc.tensor, sa[s])
        dmm(P["A"], kfv[s], kfv[s])
        ctx.inc(dmm(P["B"], kfv[s], qfv[s]), sm[s], 1)
        nc.tensor.transpose(P["Tk"], kfv[s], ktiles["ident_b"][:])
        ctx.inc(nc.tensor.transpose(P["Tv"], vfv[s], ktiles["ident_b"][:]), sm[s], 1)
    for s in (0, 1):
        B = SL[s]; P = PS[s]
        ctx.wait(nc.scalar, sm[s])
        S.copy(B["ktm"][:], P["Tk"])
        ctx.inc(S.copy(B["vtm"][:], P["Tv"]), sa[s], 1)
    for s in (0, 1):
        B = SL[s]
        ctx.wait(nc.vector, sa[s])
        V.scalar_tensor_tensor(B["sqk"][:], B["ktm"][:], 1.0, B["ktm"][:],
                               ALU.mult, ALU.mult, accum_out=B["ssk"][:])
        ctx.inc(V.scalar_tensor_tensor(B["sqq"][:], qfv[s], 1.0, qfv[s],
                                       ALU.mult, ALU.mult), sv[s], 1)
    for s in (0, 1):
        B = SL[s]
        ctx.wait(nc.scalar, sv[s])
        S.activation(B["lnrkc"][:], B["ssk"][:], AF.Ln, bias=1e-6)
        ctx.inc(S.activation(B["lnrkc"][:], B["lnrkc"][:], AF.Identity, scale=-0.5),
                sa[s], 1)
    for s in (0, 1):
        B = SL[s]; P = PS[s]
        ctx.wait(nc.tensor, sv[s])
        ctx.inc(dmm(P["qrow"], ktiles["ones_col_b"][:], B["sqq"][:]), sm[s], 1)
    for s in (0, 1):
        B = SL[s]; P = PS[s]
        ctx.wait(nc.scalar, sm[s])
        S.activation(B["lnqr"][:], P["qrow"], AF.Ln, bias=1e-6)
        ctx.inc(S.activation(B["lnqr"][:], B["lnqr"][:], AF.Identity, scale=-0.5,
                             bias=-LN128), sa[s], 1)
    for s in (0, 1):
        hh = heads[s]; B = SL[s]
        ctx.wait(nc.vector, sa[s])
        V.tensor_tensor(B["r1c"][:], L["cgB"][:, hh:hh + 1], L["clB"][:, hh:hh + 1],
                        ALU.add)
        V.tensor_tensor(B["r1c"][:], B["r1c"][:], B["lnrkc"][:], ALU.add)
        ctx.inc(V.tensor_tensor(B["r2mc"][:], B["lnrkc"][:], L["cgB"][:, hh:hh + 1],
                                ALU.subtract), sv[s], 1)
    for s in (0, 1):
        hh = heads[s]; B = SL[s]; P = PS[s]
        ctx.wait(nc.tensor, sa[s])
        ctx.wait(nc.tensor, sv[s])
        idp = ktiles["ident_f"][0:64, 0:64]
        nc.tensor.transpose(P["tgc"], L["cgB"][:, hh:hh + 1], idp)
        nc.tensor.transpose(P["tr1"], B["r1c"][:], idp)
        ctx.inc(nc.tensor.transpose(P["tr2m"], B["r2mc"][:], idp), sm[s], 1)
    for s in (0, 1):
        B = SL[s]; P = PS[s]
        ctx.wait(nc.vector, sm[s])
        V.tensor_copy(B["rows"][:], P["rowsv"])
        gcr = B["rows"][0:1, 0:64]
        r1r = B["rows"][0:1, 64:128]
        r2r = B["rows"][0:1, 128:192]
        V.tensor_add(B["r3r"][:], gcr, B["lnqr"][:])
        V.tensor_copy(B["r1b"][:], r1r)
        V.tensor_copy(B["r2b"][:], r2r)
        V.tensor_copy(B["r3b"][:], B["r3r"][:])
        V.scalar_tensor_tensor(B["r1l"][:], r1r, 1.0, B["r1b"][:], ALU.mult,
                               ALU.subtract)
        V.scalar_tensor_tensor(B["r2l"][:], r2r, 1.0, B["r2b"][:], ALU.mult,
                               ALU.subtract)
        ctx.inc(V.scalar_tensor_tensor(B["r3l"][:], B["r3r"][:], 1.0, B["r3b"][:],
                                       ALU.mult, ALU.subtract), sv[s], 1)
    for s in (0, 1):
        B = SL[s]
        ctx.wait(nc.scalar, sv[s])
        S.activation(B["r3e"][:], B["r3r"][:], AF.Exp)
        ctx.inc(S.activation(B["e1c"][:], B["r1c"][:], AF.Exp), sa[s], 1)
    for k in range(3):
        t2, t2l = (("r2b", "r2l"), ("r1b", "r1l"), ("r3b", "r3l"))[k]
        msk = ("ut64i_b", "lt64i_b", "lt64s_b")[k]
        for s in (0, 1):
            B = SL[s]; P = PS[s]
            pD = P["D"] if k % 2 == 0 else P["I1"]
            ctx.wait(nc.tensor, sv[s])
            if k != 1:
                ctx.wait(nc.tensor, sa[s])
            dmm(pD, ktiles["ones_row_b"][0:1, 0:64], B[t2][:],
                start=True, stop=False)
            dmm(pD, ktiles["ones_row_b"][0:1, 0:64], B[t2l][:],
                start=False, stop=False)
            ctx.inc(dmm(pD, ktiles["negI64_b"][:], ktiles[msk][:],
                        start=False, stop=True), sm[s], 1)
        for s in (0, 1):
            B = SL[s]; P = PS[s]
            pD = P["D"] if k % 2 == 0 else P["I1"]
            dcyk = B["dcy"] if k % 2 == 0 else B["dcy2"]
            bias_col = B["r1c"] if k == 0 else B["r2mc"]
            ctx.wait(nc.scalar, sm[s])
            ctx.wait(nc.scalar, sv[s])
            ctx.inc(S.activation(dcyk[:], pD, AF.Exp, bias=bias_col[:]),
                    sa[s], 1)
        for s in (0, 1):
            B = SL[s]; P = PS[s]
            dcyk = B["dcy"] if k % 2 == 0 else B["dcy2"]
            dst = (B["tri"], B["triT"], B["attnT"])[k]
            asl = ("A", "A", "B")[k]
            ctx.wait(nc.vector, sa[s])
            ctx.inc(V.tensor_tensor(dst[:], P[asl], dcyk[:], ALU.mult), sv[s], 1)
    for s in (0, 1):
        B = SL[s]
        V.tensor_scalar(B["Pb"][0][:], B["triT"][:], -1.0, None, ALU.mult)
        V.tensor_scalar(B["Qb"][0][:], B["tri"][:], -1.0, None, ALU.mult)
        V.tensor_add(B["Xb"][0][:], B["Pb"][0][:], ktiles["eye64_b"][:])
        ctx.inc(V.tensor_add(B["Wb"][0][:], B["Qb"][0][:], ktiles["eye64_b"][:]),
                sv[s], 1)
    for s in (0, 1):
        B = SL[s]; P = PS[s]
        ctx.wait(nc.tensor, sv[s])
        ctx.wait(nc.tensor, sa[s])
        dmm(P["X"], ktiles["eye64_b"][:], B["Xb"][0][:], start=True, stop=False)
    cur, nxt = 0, 1
    NLEV = 4
    for s_ in range(NLEV):
        for s in (0, 1):
            B = SL[s]; P = PS[s]
            ctx.wait(nc.tensor, sv[s])
            ctx.wait(nc.tensor, sa[s])
            ctx.inc(dmm(P["I0"], B["Qb"][cur][:], B["Pb"][cur][:]), sm[s], 1)
        for s in (0, 1):
            B = SL[s]; P = PS[s]
            ctx.wait(nc.scalar, sm[s])
            ctx.inc(S.copy(B["Pb"][nxt][:], P["I0"]), sa[s], 1)
        for s in (0, 1):
            B = SL[s]; P = PS[s]
            ctx.wait(nc.tensor, sa[s])
            ctx.inc(dmm(P["X"], B["Wb"][cur][:], B["Pb"][nxt][:],
                        start=False, stop=(s_ == NLEV - 1)), sm[s], 1)
        if s_ < NLEV - 1:
            for s in (0, 1):
                B = SL[s]; P = PS[s]
                ctx.inc(dmm(P["I0"], B["Pb"][cur][:], B["Qb"][cur][:]), sm[s], 1)
            for s in (0, 1):
                B = SL[s]; P = PS[s]
                ctx.wait(nc.scalar, sm[s])
                ctx.inc(S.copy(B["Qb"][nxt][:], P["I0"]), sa[s], 1)
            for s in (0, 1):
                B = SL[s]; P = PS[s]
                ctx.wait(nc.tensor, sa[s])
                ctx.inc(dmm(P["I1"], B["Pb"][nxt][:], B["Wb"][cur][:]), sm[s], 1)
            for s in (0, 1):
                B = SL[s]; P = PS[s]
                ctx.wait(nc.vector, sm[s])
                ctx.inc(V.tensor_tensor(B["Wb"][nxt][:], P["I1"], B["Wb"][cur][:],
                                        ALU.add), sv[s], 1)
        cur, nxt = nxt, cur
    for s in (0, 1):
        B = SL[s]; P = PS[s]
        ctx.wait(nc.scalar, sm[s])
        ctx.inc(S.copy(B["Xb"][0][:], P["X"]), sa[s], 1)
    for s in (0, 1):
        hh = heads[s]; B = SL[s]
        ctx.wait(nc.vector, sa[s])
        V.tensor_scalar(B["vbeta"][:], B["vtm"][:], L["cbtB"][:, hh:hh + 1],
                        None, ALU.mult)
        ctx.inc(V.tensor_scalar(B["kbeg"][:], B["ktm"][:], B["e1c"][:],
                                None, ALU.mult), sv[s], 1)
    for s in (0, 1):
        B = SL[s]; P = PS[s]
        ctx.wait(nc.tensor, sv[s])
        ctx.wait(nc.tensor, sa[s])
        ctx.inc(dmm(P["V"], B["Xb"][0][:], B["vbeta"][:]), sm[s], 1)
    for s in (0, 1):
        B = SL[s]; P = PS[s]
        ctx.wait(nc.scalar, sm[s])
        ctx.inc(S.copy(B["vnew"][:], P["V"]), sa[s], 1)
    for s in (0, 1):
        B = SL[s]; P = PS[s]
        ctx.wait(nc.tensor, sa[s])
        ctx.inc(dmm(P["K"], B["kbeg"][:], B["Xb"][0][:]), sm[s], 1)
    for s in (0, 1):
        B = SL[s]; P = PS[s]
        ctx.wait(nc.scalar, sm[s])
        ctx.inc(S.copy(B["kcum"][:], P["K"]), sa[s], 1)
    for s in (0, 1):
        B = SL[s]; P = PS[s]
        ctx.inc(dmm(P["qB"], ktiles["ones_row_b"][0:1, :], B["r3e"][:]), sm[s], 1)
    for s in (0, 1):
        B = SL[s]; P = PS[s]
        ctx.wait(nc.vector, sm[s])
        ctx.inc(V.tensor_tensor(B["qeb"][:], qfv[s], P["qB"], ALU.mult), sv[s], 1)
    for s in (0, 1):
        hh = heads[s]; B = SL[s]; P = PS[s]
        ctx.wait(nc.tensor, sa[s])
        ctx.inc(dmm(P["V"], B["kcum"][:], Sbf[hh][:]), sm[s], 1)
    for s in (0, 1):
        B = SL[s]; P = PS[s]
        ctx.wait(nc.vector, sm[s])
        ctx.inc(V.tensor_tensor(B["vnb"][:], B["vnew"][:], P["V"], ALU.subtract),
                sv[s], 1)
    for s in (0, 1):
        hh = heads[s]; B = SL[s]; P = PS[s]
        ctx.wait(nc.tensor, sv[s])
        dmm(P["O"], Sbf[hh][:], B["qeb"][:], start=True, stop=False)
        ctx.inc(dmm(P["O"], B["vnb"][:], B["attnT"][:], start=False, stop=True),
                sm[s], 1)
    for s in (0, 1):
        hh = heads[s]; B = SL[s]
        V.tensor_tensor(B["cta"][:], L["crB"][:, hh:hh + 1], B["lnrkc"][:], ALU.add)
        V.tensor_copy(B["eglc"][0:64, :], L["cglB"][:, hh:hh + 1])
        ctx.inc(V.tensor_copy(B["eglc"][64:128, :], L["cglB"][:, hh:hh + 1]),
                sv[s], 1)
    for s in (0, 1):
        B = SL[s]
        ctx.wait(nc.scalar, sv[s])
        S.activation(B["ctb"][:], B["cta"][:], AF.Exp)
        ctx.inc(S.activation(B["eglc"][:], B["eglc"][:], AF.Exp), sa[s], 1)
    for s in (0, 1):
        B = SL[s]
        ctx.wait(nc.vector, sa[s])
        ctx.inc(V.tensor_scalar(B["kglb"][:], B["ktm"][:], B["ctb"][:],
                                None, ALU.mult), sv[s], 1)
    for s in (0, 1):
        B = SL[s]; P = PS[s]
        ctx.wait(nc.tensor, sv[s])
        ctx.inc(dmm(P["S"], B["kglb"][:], B["vnb"][:]), sm[s], 1)
    for s in (0, 1):
        hh = heads[s]; B = SL[s]; P = PS[s]
        ctx.wait(nc.vector, sm[s])
        ctx.inc(V.scalar_tensor_tensor(Sst[hh][:], Sst[hh][:], B["eglc"][:],
                                       P["S"], ALU.mult, ALU.add), sv[s], 1)
    for s in (0, 1):
        hh = heads[s]
        ctx.wait(nc.scalar, sv[s])
        ctx.inc(S.copy(Sbf[hh][:], Sst[hh][:]), sa[s], 1)
    for s in (0, 1):
        B = SL[s]; P = PS[s]
        ctx.wait(nc.scalar, sm[s])
        ctx.inc(S.activation(B["sqo"][:], P["O"], AF.Square), sa[s], 1)
    for s in (0, 1):
        B = SL[s]; P = PS[s]
        ctx.wait(nc.tensor, sa[s])
        ctx.inc(dmm(P["ss"], ktiles["ones_col_b"][:], B["sqo"][:]), sm[s], 1)
    for s in (0, 1):
        B = SL[s]; P = PS[s]
        ctx.wait(nc.scalar, sm[s])
        S.activation(B["rsr"][:], P["ss"], AF.Ln, bias=1e-6, scale=1.0 / 128.0)
        ctx.inc(S.activation(B["rsr"][:], B["rsr"][:], AF.Exp, scale=-0.5), sa[s], 1)
    for s in (0, 1):
        B = SL[s]
        ctx.wait(nc.vector, sa[s])
        ctx.inc(V.tensor_copy(B["rsb"][:], B["rsr"][:]), sv[s], 1)
    for s in (0, 1):
        B = SL[s]; P = PS[s]
        ctx.wait(nc.tensor, sv[s])
        ctx.inc(dmm(P["K"], ktiles["ones_row_b"][0:1, :], B["rsb"][:]), sm[s], 1)
    for s in (0, 1):
        hh = heads[s]; B = SL[s]; P = PS[s]
        szs = L["szfm"][:, hh * 256 + toff:hh * 256 + toff + 64]
        ctx.wait(nc.vector, sm[s])
        if ctx.val("do_mm"):
            ctx.wait(nc.vector, "do_mm")
        V.tensor_tensor(B["og1"][:], P["O"], szs, ALU.mult)
        ctx.inc(V.tensor_tensor(ogbuf[:, hh * 128 + c64 * 64:hh * 128 + c64 * 64 + 64],
                                B["og1"][:], P["K"], ALU.mult), sv[s], 1)


def build_program(phases):
    nc = bass.Bass("TRN2", target_bir_lowering=False, debug=False)
    x_in = nc.dram_tensor("x_in", [NTOK, D], F32, kind="ExternalInput").ap()
    x_out = nc.dram_tensor("x_out", [NTOK, D], F32, kind="ExternalOutput").ap()
    w = {}
    wdefs = {
        "l1f_w1": [4096, D], "l1f_b1": [4096], "l1f_w2": [D, 4096], "l1f_b2": [D],
        "l0g_A": [6, 100, 16], "l0g_B": [6, 16, D], "l0g_w_gate": [D, 6],
        "l0g_wt_w": [D, D], "l0g_wt_b": [D], "l0g_wd_w": [D, D], "l0g_wd_b": [D],
        "l0g_scale": [1],
        "l1a_w_attn": [D, 3072], "l1a_b_attn": [3072],
        "l1a_w_proj": [D, D], "l1a_b_proj": [D],
        "l0a_W_qkvz": [D, 4096], "l0a_W_ba": [D, 16], "l0a_conv_w": [3072, 1, 4],
        "l0a_dt_bias": [8], "l0a_A_log": [8], "l0a_norm_weight": [128],
        "l0a_W_out": [D, D],
    }
    need = set()
    for phname in phases:
        if phname == "l1_ffn":
            need |= {"l1f_w1", "l1f_b1", "l1f_w2", "l1f_b2"}
        if phname == "l0_ffn":
            need |= {"l0g_A", "l0g_B", "l0g_w_gate", "l0g_wt_w", "l0g_wt_b",
                     "l0g_wd_w", "l0g_wd_b", "l0g_scale"}
        if phname == "l1_attn":
            need |= {"l1a_w_attn", "l1a_b_attn", "l1a_w_proj", "l1a_b_proj"}
        if phname == "l0_attn":
            need |= {"l0a_W_qkvz", "l0a_W_ba", "l0a_conv_w", "l0a_dt_bias",
                     "l0a_A_log", "l0a_norm_weight", "l0a_W_out"}
    for k in sorted(need):
        w[k] = nc.dram_tensor(k, wdefs[k], F32, kind="ExternalInput").ap()

    es = ExitStack()
    with es:
        ctx = Ctx(nc, es)
        ctx.sem("g_bar_s")
        ctx.sem("g_bar_pe")
        with nc.body("main", valid_engines=mybir.ALL_ENGINES):
            ktiles = declare_and_load_consts(ctx)
            sh = alloc_shared(ctx)
            sh["f_raw"] = nc.dram_tensor("f_raw", [NTOK, D], F32, kind="Internal").ap()
            x_mid = [None, None]
            if len(phases) > 1:
                x_mid = [nc.dram_tensor(f"x_mid{i}", [NTOK, D], F32,
                                        kind="Internal").ap() for i in range(2)]
            for eng in (nc.sync, nc.vector, nc.scalar, nc.tensor, nc.gpsimd):
                ctx.wait(eng, "c_ld")
            cur_in = x_in
            for j, phname in enumerate(phases):
                is_last = (j == len(phases) - 1)
                dst = x_out if is_last else x_mid[j % 2]
                with ExitStack() as phase_es:
                    pctx = Ctx(nc, phase_es, parent=ctx)
                    if phname == "l1_ffn":
                        emit_l1_ffn(pctx, ktiles, sh, cur_in, dst, w)
                    elif phname == "l0_ffn":
                        emit_l0_ffn(pctx, ktiles, sh, cur_in, dst, w)
                    elif phname == "l1_attn":
                        emit_l1_attn(pctx, ktiles, sh, cur_in, dst, w)
                    elif phname == "l0_attn":
                        emit_l0_attn(pctx, ktiles, sh, cur_in, dst, w)
                    else:
                        raise ValueError(phname)
                barrier(ctx, f"ph{j}")
                if not is_last:
                    pctx.teardown_sems()
                cur_in = dst
    return nc, sorted(need)


def kernel_phases(phases, **inputs):
    x = np.ascontiguousarray(np.asarray(inputs["L0_attn_x"], np.float32))
    nc, wnames = build_program(phases)
    wmap = {
        "l1f_w1": inputs["L1_ffn_w1"], "l1f_b1": inputs["L1_ffn_b1"],
        "l1f_w2": inputs["L1_ffn_w2"], "l1f_b2": inputs["L1_ffn_b2"],
        "l0g_A": inputs["L0_ffn_A"], "l0g_B": inputs["L0_ffn_B"],
        "l0g_w_gate": inputs["L0_ffn_w_gate"], "l0g_wt_w": inputs["L0_ffn_wt_w"],
        "l0g_wt_b": inputs["L0_ffn_wt_b"], "l0g_wd_w": inputs["L0_ffn_wd_w"],
        "l0g_wd_b": inputs["L0_ffn_wd_b"], "l0g_scale": inputs["L0_ffn_scale"],
        "l1a_w_attn": inputs["L1_attn_w_attn"], "l1a_b_attn": inputs["L1_attn_b_attn"],
        "l1a_w_proj": inputs["L1_attn_w_proj"], "l1a_b_proj": inputs["L1_attn_b_proj"],
        "l0a_W_qkvz": inputs["L0_attn_W_qkvz"], "l0a_W_ba": inputs["L0_attn_W_ba"],
        "l0a_conv_w": inputs["L0_attn_conv_w"], "l0a_dt_bias": inputs["L0_attn_dt_bias"],
        "l0a_A_log": inputs["L0_attn_A_log"], "l0a_norm_weight": inputs["L0_attn_norm_weight"],
        "l0a_W_out": inputs["L0_attn_W_out"],
    }
    cmap = const_inmap()
    in_maps = []
    for c in range(NCORES):
        m = dict(cmap)
        for k in wnames:
            m[k] = np.ascontiguousarray(np.asarray(wmap[k], np.float32))
        m["x_in"] = np.ascontiguousarray(x[c * BPC:(c + 1) * BPC].reshape(NTOK, D))
        in_maps.append(m)
    res = run_bass_kernel_spmd(nc, in_maps, core_ids=list(range(NCORES)))
    out = np.concatenate([r["x_out"].reshape(BPC, T, D) for r in res.results], axis=0)
    return out


def kernel(**inputs):
    return kernel_phases(["l0_attn", "l0_ffn", "l1_attn", "l1_ffn"], **inputs)
